# version 02
# baseline (speedup 1.0000x reference)
import math
import os
import numpy as np
import ml_dtypes
import concourse.bass as bass
import concourse.mybir as mybir
from concourse.bass_utils import run_bass_kernel_spmd

F32 = mybir.dt.float32
BF16 = mybir.dt.bfloat16
I32 = mybir.dt.int32
AF = mybir.ActivationFunctionType
ALU = mybir.AluOpType
AX = mybir.AxisListType

D = 1024
T = 2048
CH = 8
NT = 16
MAGIC = float(1.5 * 2 ** 23)
CPEN = 400.0
LAM_INIT = 0.8 - 0.6 * math.exp(-0.3 * 12)
KR = {0: 68, 1: 65}
VW = {0: 1040, 1: 1024}

FUSED = False


class Prog:

    def __init__(self):
        self.items = {k: [] for k in ("t", "s", "v", "sy", "g")}
        self.counts = {}

    def n(self, sem):
        return self.counts.get(sem, 0)

    def op(self, eng, fn, waits=(), inc=None, inck=1):
        w = [(s, v) for (s, v) in waits if v > 0]
        self.items[eng].append((fn, w, inc, inck))
        if inc is not None:
            self.counts[inc] = self.counts.get(inc, 0) + inck
        for s, _ in w:
            self.counts.setdefault(s, 0)

    def dma(self, eng, out, in_, waits=(), inc=None, **kw):
        assert inc is not None
        self.op(eng, lambda e: e.dma_start(out=out, in_=in_, **kw), waits, inc, 16)

    def emit(self, eng_name, e, sems):
        self_edge = eng_name in ("v", "s") and eng_name in sems
        waited = {}
        own = 0
        for fn, waits, inc, inck in self.items[eng_name]:
            if self_edge and own > 0 and waited.get(eng_name, 0) < own:
                e.wait_ge(sems[eng_name], own)
                waited[eng_name] = own
            for snm, val in waits:
                if waited.get(snm, 0) >= val:
                    continue
                e.wait_ge(sems[snm], val)
                waited[snm] = val
            instr = fn(e)
            if inc is not None:
                instr.then_inc(sems[inc], inck)
                if inc == eng_name:
                    own += inck


def build_program(layers, upto=None, dbg=False):
    nc = bass.Bass("TRN2", target_bir_lowering=False, debug=False)
    p = Prog()

    xoT = nc.dram_tensor("xoT", [D, T], F32, kind="ExternalInput").ap()
    outT = nc.dram_tensor("outT", [D, T], F32, kind="ExternalOutput").ap()
    ins = {}

    def ext(name, shape, dtype=F32):
        ins[name] = nc.dram_tensor(name, shape, dtype, kind="ExternalInput").ap()

    if 0 in layers:
        for w in ("l0_wq", "l0_wk", "l0_wv", "l0_wo", "l0_fw", "l0_fwt"):
            ext(w, [D, D])
        for b in ("l0_bq", "l0_bk", "l0_bv", "l0_bo", "l0_fb"):
            ext(b, [128, CH])
        ext("l0_rt", [64, 8])
    if 1 in layers:
        for w in ("l1_wq", "l1_wk", "l1_wv", "l1_wo"):
            ext(w, [D, D])
        for b in ("l1_lq1", "l1_lk1", "l1_lq2", "l1_lk2"):
            ext(b, [1, 64])
        ext("l1_lng", [128, 1])
        ext("l1_lnb", [128, 1])
        ext("l1_wgt", [D, 3072], I32)
        ext("l1_wut", [D, 3072], I32)
        ext("l1_wdt", [3072, D], I32)

    hbm = {"xnb": nc.dram_tensor("xnb", [D, T], BF16, kind="Internal")}
    for L in layers:
        kr, vw = KR[L], VW[L]
        hbm[f"qpp{L}"] = nc.dram_tensor(f"qpp{L}", [16 * kr, T], BF16, kind="Internal")
        hbm[f"kpp{L}"] = nc.dram_tensor(f"kpp{L}", [16 * kr, T], BF16, kind="Internal")
        hbm[f"kppf{L}"] = nc.dram_tensor(f"kppf{L}", [32 * kr, T], BF16, kind="Internal")
        hbm[f"vp{L}"] = nc.dram_tensor(f"vp{L}", [T, vw], BF16, kind="Internal")
        hbm[f"vpf{L}"] = nc.dram_tensor(f"vpf{L}", [2 * T, vw], BF16, kind="Internal")
    if 1 in layers:
        hbm["m8"] = nc.dram_tensor("m8", [8, T], F32, kind="Internal")

    A = nc.alloc_sbuf_tensor
    xT = [A(f"xT{c}", [128, T], BF16) for c in range(CH)]
    oTs = [A(f"oTs{c}", [128, T], BF16) for c in range(CH)]
    R1 = A("R1", [128, 16896], BF16)
    R2 = A("R2", [128, 16384], BF16)
    R3 = A("R3", [128, 4096], F32)
    R3b = R3.bitcast(BF16)
    asm = [A(f"asm{i}", [128, T], BF16) for i in range(2)]
    ws = [A(f"ws{i}", [128, 1024], BF16) for i in range(2)]
    AR = A("AR", [128, 10240], BF16)
    ROWS = A("ROWS", [128, 4096], F32)
    ROWSB = A("ROWSB", [128, 4096], BF16)
    RB32 = ROWSB.bitcast(F32)
    IDX = A("IDX", [128, 512], F32)
    IDXB = A("IDXB", [128, 16], BF16)
    ident = A("ident", [128, 128], BF16)
    idrow = A("idrow", [128, 128], F32)
    pidx = A("pidx", [128, 1], F32)
    identf = A("identf", [128, 128], F32)
    ones1 = A("ones1", [128, 1], BF16)
    onesr = A("onesr", [128, 128], BF16)
    onesrF = A("onesrF", [128, 128], F32)
    onesW = A("onesW", [128, 128], BF16)
    iota16 = A("iota16", [128, 16], F32)
    epsb = A("epsb", [128, 1], F32)
    wscale = A("wscale", [128, CH], F32)
    wrecip = A("wrecip", [128, CH], F32)
    biases = {}
    lrows = {}
    if 0 in layers:
        for b in ("l0_bq", "l0_bk", "l0_bv", "l0_bo", "l0_fb"):
            biases[b] = A("sb_" + b, [128, CH], F32)
        rtsb = A("rtsb", [128, 8], BF16)
    if 1 in layers:
        for b in ("l1_lq1", "l1_lk1", "l1_lq2", "l1_lk2"):
            lrows[b] = A("sb_" + b, [1, 64], F32)
        biases["l1_lng"] = A("sb_l1_lng", [128, 1], F32)
        biases["l1_lnb"] = A("sb_l1_lnb", [128, 1], F32)
        lamv = A("lam", [1, 1], F32)
        lngb = A("lngb", [128, 1], BF16)
        lnbb = A("lnbb", [128, 1], BF16)
        lngW = A("lngW", [128, 64], BF16)
        c1sb = A("c1sb", [128, CH], F32)

    QT = [R1[:, 2048 * c: 2048 * (c + 1)] for c in range(CH)]
    dacc = QT

    def vst(L, buf):
        w = VW[L] if L == 0 else 128
        return R1[:, 8448 * buf: 8448 * buf + 8 * w].rearrange("p (a b) -> p a b", a=8)

    xnT = [R2[:, 2048 * c: 2048 * (c + 1)] for c in range(CH)]
    sT = xnT
    kst = [R2[0:128, 0:4096], R2[0:128, 4096:8192]]
    qst = [R2[0:128, 8192:9216], R2[0:128, 9216:10240]]
    expb = [R2[:, 10240:11264], R2[:, 11264:12288]]
    o01 = [R2[:, 12288:14336], R2[:, 14336:16384]]

    mubc = R3b[:, 0:2048]
    rstdbc = R3b[:, 2048:4096]
    st_sc = [R3b[:, 0:2048], R3b[:, 2048:4096], R3b[:, 4096:6144]]
    rbcs = [R3b[:, 0:1024], R3b[:, 1024:2048]]
    l0tsl = R3b[:, 4096:6144]
    hsl = [R3b[:, 2048 * i: 2048 * (i + 1)] for i in range(4)] + \
          [asm[0][:, :], asm[1][:, :]]

    PA = nc.alloc_psum_tensor("PA", [128, 2048], F32)
    PB = nc.alloc_psum_tensor("PB", [128, 2048], F32)
    PBT16 = PB[0:16, 1152:1216].bitcast(BF16)
    PBT128 = PB[0:128, 1216:1217].bitcast(BF16)[:, 0:1]
    PBF1 = PB[0:1, 1280:1408]

    def mm4(PPcols, lhsT, rhs, start, stop, waits=(), width=2048):
        n = width // 512
        for i in range(n):
            sl = slice(512 * i, 512 * (i + 1))
            p.op("t", lambda e, sl=sl: e.matmul(PPcols[:, sl], lhsT, rhs[:, sl],
                                                start=start, stop=stop),
                 waits=(waits if i == 0 else ()), inc="t")

    p.op("g", lambda e: e.iota(idrow[:], pattern=[[1, 128]], base=0,
                               channel_multiplier=0, allow_small_or_imprecise_dtypes=True), inc="g")
    p.op("g", lambda e: e.iota(pidx[:], pattern=[[0, 1]], base=0,
                               channel_multiplier=1, allow_small_or_imprecise_dtypes=True), inc="g")
    p.op("g", lambda e: e.iota(iota16[:], pattern=[[0, 2], [1, 8]], base=0,
                               channel_multiplier=0, allow_small_or_imprecise_dtypes=True), inc="g")
    p.op("v", lambda e: e.tensor_scalar(out=identf[:], in0=idrow[:], scalar1=pidx[:],
                                        scalar2=None, op0=ALU.is_equal),
         waits=[("g", 2)], inc="v")
    p.op("v", lambda e: e.tensor_copy(ident[:], identf[:]), inc="v")
    p.op("v", lambda e: e.memset(ones1[:], 1.0), inc="v")
    p.op("v", lambda e: e.memset(onesr[:], 1.0), inc="v")
    p.op("v", lambda e: e.memset(onesrF[:], 1.0), inc="v")
    p.op("v", lambda e: e.memset(onesW[:], 0.0), inc="v")
    p.op("v", lambda e: e.memset(epsb[:], 1e-5), inc="v")
    for c in range(CH):
        p.op("v", lambda e, c=c: e.memset(onesW[0:64, 16 * c + 2 * c: 16 * c + 2 * c + 1], 1.0), inc="v")
        p.op("v", lambda e, c=c: e.memset(onesW[64:128, 16 * c + 2 * c + 1: 16 * c + 2 * c + 2], 1.0), inc="v")
    for b, sb in biases.items():
        p.dma("sy", sb[:], ins[b], inc="syq")
    for b, sb in lrows.items():
        p.dma("sy", sb[:], ins[b], inc="syq")
    if 0 in layers:
        p.dma("g", rtsb[0:64, :], ins["l0_rt"], inc="gq")
        p.dma("g", rtsb[64:128, :], ins["l0_rt"], inc="gq")
    for c in range(CH):
        p.dma("g", xT[c][:], xoT[128 * c: 128 * (c + 1), :], inc="gq")
    if 1 in layers:
        p.op("v", lambda e: e.tensor_copy(lngb[:], biases["l1_lng"][:]),
             waits=[("syq", p.n("syq"))], inc="v")
        p.op("v", lambda e: e.tensor_copy(lnbb[:], biases["l1_lnb"][:]), inc="v")
        p.op("v", lambda e: e.memset(lngW[:], 0.0), inc="v")
        for h in range(8):
            p.op("v", lambda e, h=h: e.tensor_copy(lngW[:, 8 * h + h: 8 * h + h + 1], lngb[:]), inc="v")
        p.op("v", lambda e: e.tensor_mul(ROWS[0:1, 0:64], lrows["l1_lq1"][:], lrows["l1_lk1"][:]), inc="v")
        p.op("v", lambda e: e.tensor_mul(ROWS[0:1, 64:128], lrows["l1_lq2"][:], lrows["l1_lk2"][:]), inc="v")
        p.op("v", lambda e: e.reduce_sum(ROWS[0:1, 128:129], ROWS[0:1, 0:64], axis=AX.X), inc="v")
        p.op("v", lambda e: e.reduce_sum(ROWS[0:1, 129:130], ROWS[0:1, 64:128], axis=AX.X), inc="v")
        p.op("s", lambda e: e.activation(out=ROWS[0:1, 130:131], in_=ROWS[0:1, 128:129], func=AF.Exp),
             waits=[("v", p.n("v"))], inc="s")
        p.op("s", lambda e: e.activation(out=ROWS[0:1, 131:132], in_=ROWS[0:1, 129:130], func=AF.Exp), inc="s")
        p.op("v", lambda e: e.tensor_sub(ROWS[0:1, 132:133], ROWS[0:1, 130:131], ROWS[0:1, 131:132]),
             waits=[("s", p.n("s"))], inc="v")
        p.op("v", lambda e: e.tensor_scalar(out=lamv[:], in0=ROWS[0:1, 132:133], scalar1=LAM_INIT,
                                            scalar2=None, op0=ALU.add), inc="v")
    setup_gq = p.n("gq")
    setup_syq = p.n("syq")

    def phase_A(L, first):
        xw = [("gq", setup_gq), ("syq", setup_syq)] if first else [("v", p.n("v")), ("sy2", p.n("sy2"))]
        amm = []
        for c in range(CH):
            buf = asm[c % 2]
            w = list(xw)
            if c >= 2:
                w.append(("t", amm[c - 2]))
            p.op("v", lambda e, b=buf, c=c: e.tensor_mul(b[:], xT[c][:], xT[c][:]),
                 waits=w, inc="v")
            vsq = p.n("v")
            mm4(PA[0:1, :], ones1[:], xT[c][:], c == 0, c == CH - 1, waits=xw)
            mm4(PB[0:1, :], ones1[:], buf[:], c == 0, c == CH - 1, waits=[("v", vsq)])
            amm.append(p.n("t"))
        ts = p.n("t")
        p.op("v", lambda e: e.tensor_scalar(out=ROWS[0:1, 0:2048], in0=PA[0:1, :],
                                            scalar1=1.0 / D, scalar2=None, op0=ALU.mult),
             waits=[("t", ts)], inc="v")
        p.op("v", lambda e: e.tensor_scalar(out=ROWS[0:1, 2048:4096], in0=PB[0:1, :],
                                            scalar1=1.0 / D, scalar2=None, op0=ALU.mult), inc="v")
        p.op("v", lambda e: e.tensor_mul(RB32[0:1, 0:2048], ROWS[0:1, 0:2048], ROWS[0:1, 0:2048]), inc="v")
        p.op("v", lambda e: e.tensor_sub(ROWS[0:1, 2048:4096], ROWS[0:1, 2048:4096],
                                         RB32[0:1, 0:2048]), inc="v")
        p.op("s", lambda e: e.activation(out=RB32[0:1, 0:2048], in_=ROWS[0:1, 2048:4096],
                                         func=AF.Sqrt, bias=epsb[0:1, :]), waits=[("v", p.n("v"))], inc="s")
        p.op("v", lambda e: e.reciprocal(ROWS[0:1, 2048:4096], RB32[0:1, 0:2048]),
             waits=[("s", p.n("s"))], inc="v")
        p.op("v", lambda e: e.tensor_copy(asm[0][0:1, 0:2048], ROWS[0:1, 0:2048]), inc="v")
        p.op("v", lambda e: e.tensor_copy(asm[1][0:1, 0:2048], ROWS[0:1, 2048:4096]), inc="v")
        vr = p.n("v")
        mm4(PA, onesr[0:1, :], asm[0][0:1, 0:2048], True, True, waits=[("v", vr)])
        mm4(PB, onesr[0:1, :], asm[1][0:1, 0:2048], True, True)
        tb = p.n("t")
        p.op("v", lambda e: e.tensor_copy(mubc[:], PA[:, :]), waits=[("t", tb)], inc="v")
        p.op("v", lambda e: e.tensor_copy(rstdbc[:], PB[:, :]), inc="v")
        vb = p.n("v")
        for c in range(CH):
            p.op("v", lambda e, c=c: e.tensor_sub(asm[c % 2][:], xT[c][:], mubc[:]),
                 waits=[("v", vb)], inc="v")
            p.op("v", lambda e, c=c: e.tensor_mul(xnT[c], asm[c % 2][:], rstdbc[:]), inc="v")
            p.dma("sy", hbm["xnb"][128 * c: 128 * (c + 1), :], xnT[c],
                  waits=[("v", p.n("v"))], inc="syq")
        return p.n("v")

    def proj(L, wname, xn_v, post):
        prev_mm = [p.n("t"), p.n("t")]
        prev_ev = [p.n("s"), p.n("s")]
        entry_v = p.n("v")
        for oc in range(CH):
            p.dma("g", ws[oc % 2][:].rearrange("p (a b) -> p a b", a=CH),
                  ins[wname].rearrange("(a p) b -> p a b", p=128)[:, :, 128 * oc: 128 * (oc + 1)],
                  waits=[("t", prev_mm[oc % 2])], inc="gq")
            ld = p.n("gq")
            PP = PA if oc % 2 == 0 else PB
            for c in range(CH):
                mm4(PP, ws[oc % 2][:, 128 * c: 128 * (c + 1)], xnT[c],
                    c == 0, c == CH - 1,
                    waits=[("gq", ld), ("v", max(xn_v, entry_v)), ("s", prev_ev[oc % 2])])
            prev_mm[oc % 2] = p.n("t")
            post(oc, PP, p.n("t"))
            prev_ev[oc % 2] = p.n("s")

    def bucket_idx(srcs, colbase, src_s):
        for oc in range(CH):
            for tt in range(NT):
                sl = slice(128 * tt, 128 * (tt + 1))
                w = [("s", src_s), ("v", p.n("v"))]
                p.op("t", lambda e, oc=oc, sl=sl: e.matmul(
                    PB[:, 1024:1032], srcs[oc][0:64, sl], rtsb[0:64, :], start=True, stop=True),
                    waits=w, inc="t")
                p.op("t", lambda e, oc=oc, sl=sl: e.matmul(
                    PB[:, 1032:1040], srcs[oc][64:128, sl], rtsb[64:128, :], start=True, stop=True), inc="t")
                tl = p.n("t")
                p.op("v", lambda e: e.reduce_max(
                    R3[:, 0:2].rearrange("p (a b) -> p a b", b=1),
                    PB[:, 1024:1040].rearrange("p (a b) -> p a b", b=8), axis=AX.X),
                    waits=[("t", tl)], inc="v")
                p.op("v", lambda e: e.tensor_tensor(
                    out=R3[:, 2:18].rearrange("p (a b) -> p a b", b=8),
                    in0=PB[:, 1024:1040].rearrange("p (a b) -> p a b", b=8),
                    in1=R3[:, 0:2].rearrange("p (a b) -> p a b", b=1).broadcast_to([128, 2, 8]),
                    op=ALU.is_ge), inc="v")
                p.op("v", lambda e: e.tensor_mul(R3[:, 2:18], R3[:, 2:18], iota16[:, 0:16]), inc="v")
                p.op("v", lambda e, oc=oc, tt=tt: e.reduce_sum(
                    IDX[:, colbase + 16 * tt + 2 * oc: colbase + 16 * tt + 2 * oc + 2].rearrange(
                        "p (a b) -> p a b", b=1),
                    R3[:, 2:18].rearrange("p (a b) -> p a b", b=8), axis=AX.X), inc="v")
        return p.n("v")

    def idx_to_rows(colbase, dst_rw, v_ready):
        for tt in range(NT):
            p.op("v", lambda e, tt=tt: e.tensor_copy(
                IDXB[:, 0:16], IDX[:, colbase + 16 * tt: colbase + 16 * tt + 16]),
                waits=[("v", v_ready), ("t", p.n("t"))], inc="v")
            p.op("t", lambda e: e.transpose(PBT16, IDXB[:, 0:16], ident[:]),
                 waits=[("v", p.n("v"))], inc="t")
            p.op("v", lambda e, tt=tt: e.tensor_copy(
                dst_rw[:, 128 * tt: 128 * (tt + 1)], PBT16),
                waits=[("t", p.n("t"))], inc="v")
        return p.n("v")

    def row_dma(dst_pp, kr, rowoff, src_rows_b, waits):
        p.dma("sy", dst_pp.ap().rearrange("(h r) t -> h r t", r=kr)[:, rowoff: rowoff + 1, :],
              src_rows_b.rearrange("h (o t) -> h o t", o=1), waits=waits, inc="syq")

    def phase_B(L, xn_v, sub=None):
        plain_ev = (sub == 'B5a')
        small_wv = (sub == 'B5b')
        kr, vw = KR[L], VW[L]
        pref = "l0_" if L == 0 else "l1_"
        qpp, kpp, vp = hbm[f"qpp{L}"], hbm[f"kpp{L}"], hbm[f"vp{L}"]

        def qpost(oc, PP, tmm):
            if L == 0:
                p.op("v", lambda e, oc=oc, PP=PP: e.tensor_scalar(
                    out=QT[oc], in0=PP[:, :], scalar1=biases["l0_bq"][:, oc: oc + 1],
                    scalar2=None, op0=ALU.add), waits=[("t", tmm)], inc="v")
                p.op("s", lambda e: e.nop(), waits=[("v", p.n("v"))], inc="s")
            else:
                p.op("s", lambda e, oc=oc, PP=PP: e.activation(
                    out=QT[oc], in_=PP[:, :], func=AF.Copy), waits=[("t", tmm)], inc="s")
        proj(L, pref + "wq", xn_v, qpost)
        s_qt = p.n("s")
        if sub == "B1":
            return

        def kpost(oc, PP, tmm):
            if L == 0:
                p.op("v", lambda e, oc=oc, PP=PP: e.tensor_scalar(
                    out=oTs[oc][:], in0=PP[:, :], scalar1=biases["l0_bk"][:, oc: oc + 1],
                    scalar2=None, op0=ALU.add), waits=[("t", tmm)], inc="v")
                p.op("s", lambda e: e.nop(), waits=[("v", p.n("v"))], inc="s")
            else:
                p.op("s", lambda e, oc=oc, PP=PP: e.activation(
                    out=oTs[oc][:], in_=PP[:, :], func=AF.Copy), waits=[("t", tmm)], inc="s")
        proj(L, pref + "wk", xn_v, kpost)
        s_kt = p.n("s")
        if sub == "B2":
            return

        for h in range(16):
            p.dma("sy", kpp[kr * h: kr * h + 64, :],
                  oTs[h // 2][64 * (h % 2): 64 * (h % 2) + 64, :],
                  waits=[("s", s_kt)], inc="syq")
        for h in range(16):
            p.dma("sy", qpp[kr * h: kr * h + 64, :],
                  QT[h // 2][64 * (h % 2): 64 * (h % 2) + 64, :],
                  waits=[("s", s_qt)], inc="syq")

        if L == 1:
            p.op("v", lambda e: e.memset(ROWSB[0:16, 0:2048], 1.0),
                 waits=[("v", p.n("v")), ("syq", p.n("syq"))], inc="v")
            row_dma(kpp, kr, 64, ROWSB[0:16, 0:2048], [("v", p.n("v"))])
        if L == 0:
            vk = bucket_idx(oTs, 0, s_kt)
            vk2 = idx_to_rows(0, ROWS[0:16, 2048:4096], vk)
            p.op("v", lambda e: e.memset(ROWSB[0:16, 0:2048], 1.0), waits=[("v", vk2)], inc="v")
            row_dma(kpp, kr, 64, ROWSB[0:16, 0:2048], [("v", p.n("v"))])
            p.op("v", lambda e: e.tensor_mul(ROWSB[0:16, 2048:4096], ROWS[0:16, 2048:4096],
                                             ROWS[0:16, 2048:4096]), inc="v")
            row_dma(kpp, kr, 65, ROWSB[0:16, 2048:4096], [("v", p.n("v"))])
            p.op("v", lambda e: e.memset(ROWSB[0:16, 0:2048], -CPEN),
                 waits=[("syq", p.n("syq"))], inc="v")
            row_dma(kpp, kr, 66, ROWSB[0:16, 0:2048], [("v", p.n("v"))])
            p.op("v", lambda e: e.tensor_scalar(out=ROWSB[0:16, 2048:4096],
                                                in0=ROWS[0:16, 2048:4096], scalar1=2 * CPEN,
                                                scalar2=None, op0=ALU.mult),
                 waits=[("syq", p.n("syq"))], inc="v")
            row_dma(kpp, kr, 67, ROWSB[0:16, 2048:4096], [("v", p.n("v"))])
            vq = bucket_idx(QT, 256, s_qt)
            vq2 = idx_to_rows(256, ROWS[0:16, 2048:4096], vq)
        else:
            vq2 = p.n("v")
        if sub == "B3":
            return

        q2mm = []
        for c in range(CH):
            w = [("s", s_qt), ("syq", p.n("syq"))]
            if c >= 2:
                w.append(("t", q2mm[c - 2]))
            p.op("v", lambda e, c=c: e.tensor_mul(asm[c % 2][:], QT[c][:], QT[c][:]),
                 waits=w, inc="v")
            mm4(PA[0:16, :], onesW[:, 16 * c: 16 * (c + 1)], asm[c % 2][:],
                c == 0, c == CH - 1, waits=[("v", p.n("v"))])
            q2mm.append(p.n("t"))
        p.op("v", lambda e: e.tensor_copy(ROWS[0:16, 0:2048], PA[0:16, :]),
             waits=[("t", p.n("t"))], inc="v")
        p.op("s", lambda e: e.activation(out=ROWS[0:16, 0:2048], in_=ROWS[0:16, 0:2048],
                                         func=AF.Sqrt), waits=[("v", p.n("v"))], inc="s")
        p.op("v", lambda e: e.tensor_scalar(out=ROWSB[0:16, 0:2048], in0=ROWS[0:16, 0:2048],
                                            scalar1=-16.0, scalar2=None, op0=ALU.mult),
             waits=[("s", p.n("s"))], inc="v")
        row_dma(qpp, kr, 64, ROWSB[0:16, 0:2048], [("v", p.n("v"))])
        if L == 0:
            p.op("v", lambda e: e.memset(ROWSB[0:16, 0:2048], -CPEN),
                 waits=[("syq", p.n("syq")), ("v", vq2)], inc="v")
            row_dma(qpp, kr, 65, ROWSB[0:16, 0:2048], [("v", p.n("v"))])
            p.op("v", lambda e: e.tensor_mul(ROWSB[0:16, 2048:4096], ROWS[0:16, 2048:4096],
                                             ROWS[0:16, 2048:4096]),
                 waits=[("syq", p.n("syq"))], inc="v")
            row_dma(qpp, kr, 66, ROWSB[0:16, 2048:4096], [("v", p.n("v"))])
            p.op("v", lambda e: e.tensor_copy(ROWSB[0:16, 0:2048], ROWS[0:16, 2048:4096]),
                 waits=[("syq", p.n("syq"))], inc="v")
            row_dma(qpp, kr, 67, ROWSB[0:16, 0:2048], [("v", p.n("v"))])
        if sub == "B4":
            return

        wvfull = AR[:, 0:8192]
        if small_wv:
            for c in range(CH):
                p.dma("g", wvfull[:, 1024 * c: 1024 * (c + 1)],
                      ins[pref + "wv"][128 * c: 128 * (c + 1), :], inc="gq")
        else:
            p.dma("g", wvfull.rearrange("p (a b) -> p a b", a=CH),
                  ins[pref + "wv"].rearrange("(a p) b -> p a b", p=128), inc="gq")
        wvld = p.n("gq")
        prev_t = [p.n("t"), p.n("t")]
        prev_s = [p.n("s"), p.n("s")]
        entry_v = p.n("v")
        prev_v = [entry_v, entry_v]
        prev_dma = [0, 0]
        for tt in range(NT):
            PP = PA if tt % 2 == 0 else PB
            for c in range(CH):
                mm4(PP[:, 0:1024], xnT[c][:, 128 * tt: 128 * (tt + 1)],
                    wvfull[:, 1024 * c: 1024 * (c + 1)], c == 0, c == CH - 1,
                    waits=[("gq", wvld), ("v", max(xn_v, prev_v[tt % 2])), ("t", prev_t[tt % 2]),
                           ("s", prev_s[tt % 2])], width=1024)
            tmm = p.n("t")
            buf = asm[tt % 2]
            if L == 0 and plain_ev:
                p.op("s", lambda e, PP=PP, buf=buf: e.activation(
                    out=buf[:, 0:1024], in_=PP[:, 0:1024], func=AF.Copy),
                    waits=[("t", tmm), ("syq", prev_dma[tt % 2])], inc="s")
                p.op("v", lambda e: e.nop(), inc="v")
            elif L == 0:
                p.op("s", lambda e, PP=PP, buf=buf: e.activation(
                    out=buf[:, 0:1040].rearrange("p (a b) -> p a b", b=65)[:, :, 0:64],
                    in_=PP[:, 0:1024].rearrange("p (a b) -> p a b", b=64), func=AF.Copy),
                    waits=[("t", tmm), ("syq", prev_dma[tt % 2])], inc="s")
                p.op("v", lambda e, buf=buf: e.memset(
                    buf[:, 0:1040].rearrange("p (a b) -> p a b", b=65)[:, :, 64:65], 1.0),
                    waits=[("s", p.n("s")), ("syq", prev_dma[tt % 2])], inc="v")
            else:
                p.op("s", lambda e, PP=PP, buf=buf: e.activation(
                    out=buf[:, 0:1024], in_=PP[:, 0:1024], func=AF.Copy),
                    waits=[("t", tmm), ("syq", prev_dma[tt % 2])], inc="s")
                p.op("v", lambda e: e.nop(), inc="v")
            p.dma("sy", vp[128 * tt: 128 * (tt + 1), :], buf[:, 0:vw],
                  waits=[("s", p.n("s")), ("v", p.n("v"))], inc="syq")
            prev_dma[tt % 2] = p.n("syq")
            prev_t[tt % 2] = p.n("t")
            prev_s[tt % 2] = p.n("s")
            prev_v[tt % 2] = p.n("v")
        if sub in ("B5", "B5a", "B5b"):
            return

        agw = [("syq", p.n("syq")), ("gq", p.n("gq")), ("v", p.n("v")), ("s", p.n("s"))]
        p.op("g", lambda e: e.collective_compute(
            "AllGather", ALU.bypass, replica_groups=[[0, 1], [2, 3], [4, 5], [6, 7]],
            ins=[kpp.ap().opt()], outs=[hbm[f"kppf{L}"].ap().opt()]),
            waits=agw, inc=f"cck{L}")
        p.op("g", lambda e: e.collective_compute(
            "AllGather", ALU.bypass, replica_groups=[[0, 1], [2, 3], [4, 5], [6, 7]],
            ins=[vp.ap().opt()], outs=[hbm[f"vpf{L}"].ap().opt()]),
            waits=agw, inc=f"ccv{L}")

    R1f = R1.bitcast(F32)

    def l1_head_post(h):
        SUM0 = ROWS[0:1, 0:2048]
        SUM1 = ROWS[0:1, 2048:4096]
        FMU = R1f[0:1, 4736:6784]
        F0b = R1[0:1, 5120:7168]
        F1b = R1[0:1, 13568:15616]
        H0 = RB32[0:1, 0:1024]
        H1 = RB32[0:1, 1024:2048]
        H2 = R1f[0:1, 512:1536]
        H3 = R1f[0:1, 1536:2560]
        H4 = R3[0:1, 3072:4096]
        H5 = R1f[0:1, 2560:3584]
        H6 = R1f[0:1, 6784:7808]
        stg32 = ROWS[32:33, 0:2048]
        stg64 = ROWS[64:65, 0:2048]
        osh = p.n("odma")
        p.op("v", lambda e: e.tensor_mul(st_sc[0][:, :], o01[0][:, :], o01[0][:, :]),
             waits=[("odma", osh)], inc="v")
        p.op("v", lambda e: e.tensor_mul(st_sc[1][:, :], o01[1][:, :], o01[1][:, :]), inc="v")
        p.op("v", lambda e: e.tensor_mul(st_sc[2][:, :], o01[0][:, :], o01[1][:, :]), inc="v")
        vp_ = p.n("v")
        mm4(PA[0:1, :], ones1[:], o01[0][:, :], True, True,
            waits=[("v", vp_), ("s", p.n("s"))])
        mm4(PA[32:33, :], ones1[:], o01[1][:, :], True, True)
        mm4(PA[64:65, :], ones1[:], st_sc[0][:, :], True, True)
        mm4(PB[0:1, :], ones1[:], st_sc[1][:, :], True, True)
        mm4(PB[32:33, :], ones1[:], st_sc[2][:, :], True, True)
        tst = p.n("t")
        p.op("v", lambda e: e.tensor_copy(stg32, PA[32:33, :]), waits=[("t", tst)], inc="v")
        p.op("v", lambda e: e.tensor_copy(stg64, PA[64:65, :]), inc="v")
        p.op("v", lambda e: e.tensor_copy(ROWS[32:33, 2048:4096], PB[32:33, :]), inc="v")
        for qh in range(2):
            qsl = slice(1024 * qh, 1024 * (qh + 1))
            s0h = SUM0[:, qsl]
            s1h = SUM1[:, qsl]
            p.op("v", lambda e, s0h=s0h: e.reciprocal(H5, s0h), inc="v")
            p.op("v", lambda e, s1h=s1h: e.reciprocal(H6, s1h), inc="v")
            p.op("v", lambda e, qsl=qsl: e.tensor_copy(H0, PA[0:1, qsl]),
                 waits=[("t", tst)], inc="v")
            p.dma("g", H1, stg32[:, qsl], waits=[("v", p.n("v"))], inc="odma")
            p.dma("g", H2, stg64[:, qsl], inc="odma")
            p.op("v", lambda e, qsl=qsl: e.tensor_copy(H3, PB[0:1, qsl]), inc="v")
            od = p.n("odma")
            p.op("v", lambda e: e.tensor_mul(H0, H5, H0), waits=[("odma", od)], inc="v")
            p.op("v", lambda e: e.tensor_mul(H1, H6, H1), inc="v")
            p.op("v", lambda e: e.tensor_scalar(out=H1, in0=H1, scalar1=lamv[0:1, 0:1],
                                                scalar2=None, op0=ALU.mult), inc="v")
            p.op("v", lambda e: e.tensor_sub(H0, H0, H1), inc="v")
            p.op("v", lambda e: e.tensor_scalar(out=H0, in0=H0, scalar1=1.0 / 128,
                                                scalar2=None, op0=ALU.mult), inc="v")
            p.op("v", lambda e: e.tensor_mul(H1, H5, H5), inc="v")
            p.op("v", lambda e: e.tensor_mul(H1, H1, H2), inc="v")
            p.op("v", lambda e: e.tensor_mul(H2, H6, H6), inc="v")
            p.op("v", lambda e: e.tensor_mul(H2, H2, H3), inc="v")
            p.op("v", lambda e: e.tensor_scalar(out=H2, in0=H2, scalar1=lamv[0:1, 0:1],
                                                scalar2=None, op0=ALU.mult), inc="v")
            p.op("v", lambda e: e.tensor_scalar(out=H2, in0=H2, scalar1=lamv[0:1, 0:1],
                                                scalar2=None, op0=ALU.mult), inc="v")
            p.dma("g", H3, ROWS[32:33, 2048 + 1024 * qh: 2048 + 1024 * (qh + 1)],
                  waits=[("v", p.n("v"))], inc="odma")
            p.op("v", lambda e: e.tensor_mul(H4, H5, H6),
                 waits=[("odma", p.n("odma"))], inc="v")
            p.op("v", lambda e: e.tensor_mul(H4, H4, H3), inc="v")
            p.op("v", lambda e: e.tensor_scalar(out=H4, in0=H4, scalar1=lamv[0:1, 0:1],
                                                scalar2=None, op0=ALU.mult), inc="v")
            p.op("v", lambda e: e.tensor_scalar(out=H4, in0=H4, scalar1=2.0,
                                                scalar2=None, op0=ALU.mult), inc="v")
            p.op("v", lambda e: e.tensor_add(H1, H1, H2), inc="v")
            p.op("v", lambda e: e.tensor_sub(H1, H1, H4), inc="v")
            p.op("v", lambda e: e.tensor_scalar(out=H1, in0=H1, scalar1=1.0 / 128,
                                                scalar2=None, op0=ALU.mult), inc="v")
            p.op("v", lambda e: e.tensor_mul(H2, H0, H0), inc="v")
            p.op("v", lambda e: e.tensor_sub(H1, H1, H2), inc="v")
            p.op("s", lambda e: e.activation(out=H2, in_=H1, func=AF.Sqrt, bias=epsb[0:1, :]),
                 waits=[("v", p.n("v"))], inc="s")
            p.op("v", lambda e: e.reciprocal(H1, H2), waits=[("s", p.n("s"))], inc="v")
            p.op("v", lambda e, s0h=s0h: e.tensor_mul(s0h, H5, H1), inc="v")
            p.op("v", lambda e, s1h=s1h: e.tensor_mul(s1h, H6, H1), inc="v")
            p.op("v", lambda e, s1h=s1h: e.tensor_scalar(out=s1h, in0=s1h,
                                                         scalar1=lamv[0:1, 0:1],
                                                         scalar2=None, op0=ALU.mult), inc="v")
            p.op("v", lambda e: e.tensor_mul(H0, H0, H1), inc="v")
            p.op("v", lambda e, qsl=qsl: e.tensor_copy(FMU[:, qsl], H0),
                 waits=[("syq", p.n("syq"))], inc="v")
        p.dma("sy", hbm["m8"][h: h + 1, :], FMU, waits=[("v", p.n("v"))], inc="syq")
        p.op("v", lambda e: e.tensor_copy(F0b, SUM0), inc="v")
        vF = p.n("v")
        mm4(PA, onesr[0:1, :], F0b, True, True, waits=[("v", vF)])
        p.op("v", lambda e: e.tensor_copy(asm[0][:, :], PA[:, :]), waits=[("t", p.n("t"))], inc="v")
        p.op("v", lambda e: e.tensor_copy(F1b, SUM1), inc="v")
        vF1 = p.n("v")
        mm4(PA, onesr[0:1, :], F1b, True, True, waits=[("v", vF1)])
        p.op("v", lambda e: e.tensor_copy(asm[1][:, :], PA[:, :]), waits=[("t", p.n("t"))], inc="v")
        p.op("v", lambda e: e.tensor_mul(st_sc[0][:, :], o01[0][:, :], asm[0][:, :]), inc="v")
        p.op("v", lambda e: e.tensor_mul(st_sc[1][:, :], o01[1][:, :], asm[1][:, :]), inc="v")
        p.op("v", lambda e, h=h: e.tensor_sub(oTs[h][:, :], st_sc[0][:, :], st_sc[1][:, :]), inc="v")

    def phase_C(L, c0=False, upto_c1=False):
        kr, vw = KR[L], VW[L]
        nh = 16 if L == 0 else 8
        njl = 1 if L == 0 else 2
        kppf, vpf, qpp = hbm[f"kppf{L}"], hbm[f"vpf{L}"], hbm[f"qpp{L}"]
        scores = [PA[:, 0:1024], PA[:, 1024:2048]]
        misc = PB[:, 1024:2048]
        opsum = PB[0:65, 0:1024] if L == 0 else PB[0:128, 0:1024]

        kst_used = [0, 0]
        qst_used = [0, 0]
        vst_used = [0, 0]
        kn = [0, 0]
        qn = [0, 0]
        vn = [0, 0]
        sc_free = [0, 0]
        exp_free = [0, 0]
        exp_done = [0, 0]
        op_free = p.n("v")
        misc_free = p.n("v")
        pa_free = p.n("s")
        state = {"qtog": 0}

        def load_kst(h, b):
            for chunk in range(2):
                p.dma("sy", kst[b][0:kr, 2048 * chunk: 2048 * (chunk + 1)],
                      kppf[chunk * 16 * kr + kr * h: chunk * 16 * kr + kr * h + kr, :],
                      waits=[("t", kst_used[b]), (f"cck{L}", 1)], inc=f"kl{b}")
            kn[b] += 32

        def load_qst(hh, qh, b):
            p.dma("sy", qst[b][0:kr, :], qpp[kr * hh: kr * hh + kr, 1024 * qh: 1024 * (qh + 1)],
                  waits=[("t", qst_used[b])], inc=f"ql{b}")
            qn[b] += 16

        def load_vst(h, kq, b):
            csl = slice(0, 1040) if L == 0 else slice(128 * h, 128 * (h + 1))
            p.dma("sy", vst(L, b)[:, :, :],
                  vpf.ap().rearrange("(a p) b -> p a b", p=128)[:, 8 * kq: 8 * kq + 8, csl],
                  waits=[("t", vst_used[b]), (f"ccv{L}", 1)], inc=f"vl{b}")
            vn[b] += 16

        if L == 0:
            load_kst(0, 0)
        for h in range(nh):
            if L == 0:
                kb = h % 2
                if h + 1 < nh:
                    load_kst(h + 1, (h + 1) % 2)
            else:
                load_kst(2 * h, 0)
                load_kst(2 * h + 1, 1)
            for qh in range(2):
                for j in range(njl):
                    hh = h if L == 0 else 2 * h + j
                    if L == 1:
                        kb = j
                    qb = state["qtog"]
                    state["qtog"] ^= 1
                    load_qst(hh, qh, qb)
                    myq = qn[qb]
                    for kq in range(4):
                        vb = kq % 2
                        load_vst(h, kq, vb)
                        myv = vn[vb]
                        for ki in range(8):
                            k = 8 * kq + ki
                            sp = k % 2
                            w = [("s", max(sc_free[sp], pa_free)),
                                 (f"kl{kb}", 16 * (kn[kb] // 16)), (f"ql{qb}", myq)]
                            for i in range(2):
                                sl = slice(512 * i, 512 * (i + 1))
                                p.op("t", lambda e, sp=sp, sl=sl, k=k, kb=kb, qb=qb: e.matmul(
                                    scores[sp][:, sl],
                                    kst[kb][0:kr, 128 * k: 128 * (k + 1)],
                                    qst[qb][0:kr, sl], start=True, stop=True),
                                    waits=(w if i == 0 else ()), inc="t")
                            kst_used[kb] = qst_used[qb] = tqk = p.n("t")
                            p.op("s", lambda e, sp=sp: e.activation(
                                out=expb[sp], in_=scores[sp][:, :], func=AF.Exp, scale=0.125),
                                waits=[("t", max(tqk, exp_free[sp]))], inc="s")
                            exp_done[sp] = sc_free[sp] = p.n("s")
                            w2 = [("s", exp_done[sp]), (f"vl{vb}", myv)]
                            if k == 0:
                                w2.append(("v", op_free))
                                w2.append(("s", pa_free))
                            dv = 65 if L == 0 else 128
                            voff = 65 * h if L == 0 else 0
                            for i in range(2):
                                sl = slice(512 * i, 512 * (i + 1))
                                p.op("t", lambda e, vb=vb, ki=ki, sl=sl, k=k, sp=sp,
                                     dv=dv, voff=voff: e.matmul(
                                         opsum[:, sl],
                                         vst(L, vb)[:, ki: ki + 1, voff: voff + dv].rearrange(
                                             "p a b -> p (a b)"),
                                         expb[sp][:, sl], start=(k == 0), stop=(k == 31)),
                                     waits=(w2 if i == 0 else ()), inc="t")
                            vst_used[vb] = p.n("t")
                            if L == 1:
                                w3 = [("v", misc_free)] if k == 0 else ()
                                for i in range(2):
                                    sl = slice(512 * i, 512 * (i + 1))
                                    p.op("t", lambda e, j=j, sl=sl, k=k, sp=sp: e.matmul(
                                        misc[64 * j: 64 * j + 1, sl], ones1[:],
                                        expb[sp][:, sl], start=(k == 0), stop=(k == 31)),
                                        waits=(w3 if i == 0 else ()), inc="t")
                            exp_free[sp] = p.n("t")
                    tav = p.n("t")
                    if L == 0:
                        rsl = ROWS[64:65, 1024 * qh: 1024 * (qh + 1)]
                        rbl = ROWSB[64:65, 1024 * qh: 1024 * (qh + 1)]
                        p.op("v", lambda e, rsl=rsl: e.reciprocal(rsl, opsum[64:65, :]),
                             waits=[("t", tav)], inc="v")
                        p.op("v", lambda e, rsl=rsl, rbl=rbl: e.tensor_copy(rbl, rsl), inc="v")
                        vr = p.n("v")
                        for i in range(2):
                            sl = slice(512 * i, 512 * (i + 1))
                            p.op("t", lambda e, rbl=rbl, sl=sl: e.matmul(
                                misc[:, sl], onesr[64:65, :], rbl[:, sl], start=True, stop=True),
                                waits=([("v", max(vr, misc_free))] if i == 0 else ()), inc="t")
                        p.op("v", lambda e, qh=qh: e.tensor_copy(rbcs[qh][:, :], misc[:, :]),
                             waits=[("t", p.n("t"))], inc="v")
                        misc_free = p.n("v")
                        p.op("v", lambda e, qh=qh: e.tensor_mul(
                            l0tsl[0:64, 0:1024], opsum[0:64, :], rbcs[qh][0:64, :]), inc="v")
                        op_free = p.n("v")
                        if h % 2 == 0:
                            p.op("v", lambda e, h=h, qh=qh: e.tensor_scalar(
                                out=oTs[h // 2][0:64, 1024 * qh: 1024 * (qh + 1)],
                                in0=l0tsl[0:64, 0:1024],
                                scalar1=biases["l0_bv"][0:64, h // 2: h // 2 + 1],
                                scalar2=None, op0=ALU.add), inc="v")
                        else:
                            p.dma("g", oTs[h // 2][64:128, 1024 * qh: 1024 * (qh + 1)],
                                  l0tsl[0:64, 0:1024], waits=[("v", p.n("v"))], inc="odma")
                            p.op("v", lambda e, h=h, qh=qh: e.tensor_scalar(
                                out=oTs[h // 2][64:128, 1024 * qh: 1024 * (qh + 1)],
                                in0=oTs[h // 2][64:128, 1024 * qh: 1024 * (qh + 1)],
                                scalar1=biases["l0_bv"][64:128, h // 2: h // 2 + 1],
                                scalar2=None, op0=ALU.add),
                                waits=[("odma", p.n("odma"))], inc="v")
                    else:
                        p.op("v", lambda e, j=j, qh=qh: e.tensor_copy(
                            o01[j][:, 1024 * qh: 1024 * (qh + 1)], opsum[:, :]),
                            waits=[("t", tav)], inc="v")
                        op_free = p.n("v")
                        if j == 0:
                            p.op("v", lambda e, qh=qh: e.tensor_copy(
                                ROWS[0:1, 1024 * qh: 1024 * (qh + 1)], misc[0:1, :]), inc="v")
                        else:
                            p.op("v", lambda e, qh=qh: e.tensor_copy(
                                ROWS[64:65, 1024 * qh: 1024 * (qh + 1)], misc[64:65, :]), inc="v")
                            p.dma("g", ROWS[0:1, 2048 + 1024 * qh: 2048 + 1024 * (qh + 1)],
                                  ROWS[64:65, 1024 * qh: 1024 * (qh + 1)],
                                  waits=[("v", p.n("v"))], inc="odma")
                        misc_free = p.n("v")
            if upto_c1 and L == 1:
                l1_head_post(h)
                d_rows1 = nc.dram_tensor("d_rows1", [128, 4096], F32, kind="ExternalOutput").ap()
                d_f0bc = nc.dram_tensor("d_f0bc", [128, T], BF16, kind="ExternalOutput").ap()
                d_f1bc = nc.dram_tensor("d_f1bc", [128, T], BF16, kind="ExternalOutput").ap()
                d_oc = nc.dram_tensor("d_oc", [128, T], BF16, kind="ExternalOutput").ap()
                d_r1 = nc.dram_tensor("d_r1", [128, 16896], BF16, kind="ExternalOutput").ap()
                allw = [("t", p.n("t")), ("s", p.n("s")), ("v", p.n("v")),
                        ("syq", p.n("syq")), ("gq", p.n("gq")), ("odma", p.n("odma"))]
                p.dma("sy", d_rows1, ROWS[:], waits=allw, inc="ost")
                p.dma("sy", d_f0bc, asm[0][:], waits=allw, inc="ost")
                p.dma("sy", d_f1bc, asm[1][:], waits=allw, inc="ost")
                p.dma("sy", d_oc, oTs[0][:], waits=allw, inc="ost")
                p.dma("sy", d_r1, R1[:], waits=allw, inc="ost")
                return
            if c0:
                d_o0 = nc.dram_tensor("d_o0", [128, T], BF16, kind="ExternalOutput").ap()
                d_o1 = nc.dram_tensor("d_o1", [128, T], BF16, kind="ExternalOutput").ap()
                d_rows = nc.dram_tensor("d_rows", [128, 4096], F32, kind="ExternalOutput").ap()
                allw = [("t", p.n("t")), ("s", p.n("s")), ("v", p.n("v")),
                        ("syq", p.n("syq")), ("gq", p.n("gq")), ("odma", p.n("odma"))]
                p.dma("sy", d_o0, o01[0], waits=allw, inc="ost")
                p.dma("sy", d_o1, o01[1], waits=allw, inc="ost")
                p.dma("sy", d_rows, ROWS[:], waits=allw, inc="ost")
                return
            if L == 1:
                l1_head_post(h)
                op_free = p.n("v")
                misc_free = p.n("v")
                pa_free = 0
                sc_free = [0, 0]
                vpost = p.n("v")
                sc_free = [0, 0]
                self_gate = vpost
                p.op("s", lambda e: e.nop(), waits=[("v", vpost)], inc="s")
                sc_free = [p.n("s"), p.n("s")]

    def phase_D(L):
        pref = "l0_" if L == 0 else "l1_"
        oc_v = p.n("v")
        vprep = p.n("v")
        if L == 1:
            prevg = p.n("t")
            dprep_s = p.n("s")
            for hh in range(8):
                p.dma("g", ws[hh % 2][:].rearrange("p (a b) -> p a b", a=CH),
                      ins["l1_wo"][128 * hh: 128 * (hh + 1), :].rearrange("p (a b) -> p a b", a=CH),
                      waits=[("t", prevg)], inc="gq")
                ld = p.n("gq")
                for i in range(2):
                    sl = slice(512 * i, 512 * (i + 1))
                    p.op("t", lambda e, hh=hh, sl=sl: e.matmul(
                        PA[0:8, sl], lngW[:, 8 * hh: 8 * hh + 8],
                        ws[hh % 2][:, sl], start=(hh == 0), stop=(hh == 7)),
                        waits=[("gq", ld), ("v", oc_v), ("s", dprep_s)], inc="t")
                for i in range(2):
                    sl = slice(512 * i, 512 * (i + 1))
                    p.op("t", lambda e, hh=hh, sl=sl: e.matmul(
                        PB[0:1, sl], lnbb[:], ws[hh % 2][:, sl],
                        start=(hh == 0), stop=(hh == 7)), inc="t")
                prevg = p.n("t")
            p.op("v", lambda e: e.tensor_copy(ROWSB[0:8, 0:1024], PA[0:8, 0:1024]),
                 waits=[("t", p.n("t"))], inc="v")
            p.op("v", lambda e: e.tensor_copy(ROWS[0:1, 0:1024], PB[0:1, 0:1024]), inc="v")
            p.op("v", lambda e: e.tensor_copy(ROWSB[0:1, 1024:2048], ROWS[0:1, 0:1024]), inc="v")
            vc1 = p.n("v")
            for dc in range(CH):
                p.op("t", lambda e, dc=dc: e.transpose(
                    PBT128, ROWSB[0:1, 1024 + 128 * dc: 1024 + 128 * (dc + 1)], ident[0:1, 0:1]),
                    waits=[("v", vc1), ("v", p.n("v"))], inc="t")
                p.op("v", lambda e, dc=dc: e.tensor_copy(c1sb[:, dc: dc + 1], PBT128),
                     waits=[("t", p.n("t"))], inc="v")
            p.dma("sy", ROWS[0:8, 0:2048], hbm["m8"][:, :],
                  waits=[("syq", p.n("syq")), ("v", p.n("v")), ("s", p.n("s")),
                         ("t", p.n("t"))], inc="syq")
            p.op("v", lambda e: e.tensor_scalar(out=ROWSB[0:8, 2048:4096], in0=ROWS[0:8, 0:2048],
                                                scalar1=-1.0, scalar2=None, op0=ALU.mult),
                 waits=[("syq", p.n("syq"))], inc="v")
            vprep = p.n("v")
            if os.environ.get("DUMPD") == "1":
                d_rb = nc.dram_tensor("d_rb", [128, 4096], BF16, kind="ExternalOutput").ap()
                d_rw = nc.dram_tensor("d_rw", [128, 4096], F32, kind="ExternalOutput").ap()
                p.dma("sy", d_rb, ROWSB[:], waits=[("v", vprep), ("t", p.n("t")), ("s", p.n("s"))], inc="dbgst")
                p.dma("sy", d_rw, ROWS[:], waits=[("v", vprep), ("t", p.n("t")), ("s", p.n("s"))], inc="dbgst")
                p.op("sy", lambda e: e.nop(), waits=[("dbgst", p.n("dbgst"))], inc="sy2")

        prev_mm = [p.n("t"), p.n("t")]
        prev_ev = [p.n("s"), p.n("s")]
        for dc in range(CH):
            p.dma("g", ws[dc % 2][:].rearrange("p (a b) -> p a b", a=CH),
                  ins[pref + "wo"].rearrange("(a p) b -> p a b", p=128)[:, :, 128 * dc: 128 * (dc + 1)],
                  waits=[("t", prev_mm[dc % 2])], inc="gq")
            ld = p.n("gq")
            if L == 1:
                p.op("v", lambda e, dc=dc: e.tensor_scalar(
                    out=ws[dc % 2][:], in0=ws[dc % 2][:], scalar1=biases["l1_lng"][:],
                    scalar2=None, op0=ALU.mult), waits=[("gq", ld), ("v", vprep)], inc="v")
                wsr = [("v", p.n("v")), ("s", prev_ev[dc % 2])]
            else:
                wsr = [("gq", ld), ("s", prev_ev[dc % 2])]
            PP = PA if dc % 2 == 0 else PB
            for oc in range(CH):
                last = (oc == CH - 1) and (L == 0 or os.environ.get("DRANK8") == "0")
                mm4(PP, ws[dc % 2][:, 128 * oc: 128 * (oc + 1)], oTs[oc][:, :],
                    oc == 0, last, waits=wsr + [("v", oc_v)])
            if L == 1 and os.environ.get("DRANK8") != "0":
                mm4(PP, ROWSB[0:8, 128 * dc: 128 * (dc + 1)],
                    ROWSB[0:8, 2048:4096], False, True, waits=[("v", vprep)])
            prev_mm[dc % 2] = p.n("t")
            bb = biases["l0_bo"][:, dc: dc + 1] if L == 0 else c1sb[:, dc: dc + 1]
            p.op("v", lambda e, dc=dc, PP=PP, bb=bb: e.tensor_scalar(
                out=sT[dc], in0=PP[:, :], scalar1=bb, scalar2=None, op0=ALU.add),
                waits=[("t", prev_mm[dc % 2])], inc="v")
            p.op("s", lambda e: e.nop(), waits=[("v", p.n("v"))], inc="s")
            prev_ev[dc % 2] = p.n("s")
        return p.n("s"), p.n("t")

    def phase_E0(sD, tD, first_out):
        for c in range(CH):
            p.dma("g", R3[:, 0:1024], ins["l0_fw"][128 * c: 128 * (c + 1), :],
                  waits=[("v", p.n("v")), ("s", sD)], inc="gq")
            p.op("v", lambda e, c=c: e.reduce_max(
                wscale[:, c: c + 1].rearrange("p (a b) -> p a b", b=1),
                R3[:, 0:1024].rearrange("p (a b) -> p a b", b=1024),
                axis=AX.X, apply_absolute_value=True),
                waits=[("gq", p.n("gq"))], inc="v")
            p.op("v", lambda e, c=c: e.tensor_scalar(
                out=wscale[:, c: c + 1], in0=wscale[:, c: c + 1], scalar1=1e-8,
                scalar2=None, op0=ALU.max), inc="v")
            p.op("v", lambda e, c=c: e.reciprocal(wrecip[:, c: c + 1], wscale[:, c: c + 1]), inc="v")
            p.op("v", lambda e, c=c: e.tensor_scalar(
                out=wrecip[:, c: c + 1], in0=wrecip[:, c: c + 1], scalar1=127.0,
                scalar2=None, op0=ALU.mult), inc="v")
            p.op("v", lambda e, c=c: e.tensor_scalar(
                out=wscale[:, c: c + 1], in0=wscale[:, c: c + 1], scalar1=1.0 / 127.0,
                scalar2=None, op0=ALU.mult), inc="v")
        vq = p.n("v")
        for c in range(CH):
            p.op("t", lambda e, c=c: e.transpose(PBF1, wrecip[:, c: c + 1], identf[:]),
                 waits=[("v", vq), ("v", p.n("v")), ("t", tD)], inc="t")
            p.op("v", lambda e, c=c: e.tensor_copy(
                ROWS[0:1, 128 * c: 128 * (c + 1)], PBF1),
                waits=[("t", p.n("t"))], inc="v")
        vrow = p.n("v")
        for i in range(2):
            sl = slice(512 * i, 512 * (i + 1))
            p.op("t", lambda e, sl=sl: e.matmul(
                PB[:, sl], onesrF[0:1, :], ROWS[0:1, 0:1024][:, sl], start=True, stop=True),
                waits=([("v", vrow)] if i == 0 else ()), inc="t")
        p.op("v", lambda e: e.tensor_copy(R3[:, 1024:2048], PB[:, 0:1024]),
             waits=[("t", p.n("t"))], inc="v")
        vrw = p.n("v")
        for c in range(CH):
            p.dma("g", R3[:, 0:1024], ins["l0_fwt"][128 * c: 128 * (c + 1), :],
                  waits=[("v", p.n("v")), ("v", vrw)], inc="gq")
            p.op("v", lambda e: e.tensor_mul(R3[:, 2048:3072], R3[:, 0:1024], R3[:, 1024:2048]),
                 waits=[("gq", p.n("gq"))], inc="v")
            p.op("v", lambda e, c=c: e.tensor_scalar(
                out=AR[:, 1024 * c: 1024 * (c + 1)], in0=R3[:, 2048:3072],
                scalar1=MAGIC, scalar2=MAGIC, op0=ALU.add, op1=ALU.subtract), inc="v")
        vwq = p.n("v")

        tmaj = [AR[:, 8192:9216], AR[:, 9216:10240]]
        prev_red = [0, 0]
        for tt in range(NT):
            b = tt % 2
            p.op("sy", lambda e, b=b, tt=tt: e.dma_start_transpose(
                tmaj[b], hbm["xnb"][:, 128 * tt: 128 * (tt + 1)]),
                waits=[("v", prev_red[b]), ("odma", p.n("odma")), ("syq", setup_syq)],
                inc=f"tm{b}", inck=16)
            p.op("v", lambda e, b=b, tt=tt: e.reduce_max(
                IDX[:, tt: tt + 1].rearrange("p (a b) -> p a b", b=1),
                tmaj[b].rearrange("p (a b) -> p a b", b=1024),
                axis=AX.X, apply_absolute_value=True),
                waits=[(f"tm{b}", 16 * (tt // 2 + 1))], inc="v")
            prev_red[b] = p.n("v")
        p.op("v", lambda e: e.tensor_scalar(out=IDX[:, 0:16], in0=IDX[:, 0:16],
                                            scalar1=1e-8, scalar2=None, op0=ALU.max), inc="v")
        vam = p.n("v")
        for tt in range(NT):
            p.op("t", lambda e, tt=tt: e.transpose(PBF1, IDX[:, tt: tt + 1], identf[:]),
                 waits=[("v", vam), ("v", p.n("v"))], inc="t")
            p.op("v", lambda e, tt=tt: e.tensor_copy(ROWS[0:1, 128 * tt: 128 * (tt + 1)], PBF1),
                 waits=[("t", p.n("t"))], inc="v")
        p.op("v", lambda e: e.tensor_scalar(out=ROWSB[0:1, 0:2048], in0=ROWS[0:1, 0:2048],
                                            scalar1=1.0 / 127.0, scalar2=None, op0=ALU.mult), inc="v")
        p.op("v", lambda e: e.reciprocal(ROWS[0:1, 2048:4096], ROWS[0:1, 0:2048]), inc="v")
        p.op("v", lambda e: e.tensor_scalar(out=ROWS[0:1, 2048:4096], in0=ROWS[0:1, 2048:4096],
                                            scalar1=127.0, scalar2=None, op0=ALU.mult), inc="v")
        vxr = p.n("v")
        mm4(PA, onesrF[0:1, :], ROWS[0:1, 2048:4096], True, True, waits=[("v", vxr)])
        p.op("v", lambda e: e.tensor_copy(R3[:, 0:2048], PA[:, :]), waits=[("t", p.n("t"))], inc="v")
        mm4(PB, onesr[0:1, :], ROWSB[0:1, 0:2048], True, True, waits=[("v", vxr)])
        p.op("v", lambda e: e.tensor_copy(asm[0][:, :], PB[:, :]), waits=[("t", p.n("t"))], inc="v")
        vbc = p.n("v")
        for c in range(CH):
            p.dma("sy", oTs[c][:], hbm["xnb"][128 * c: 128 * (c + 1), :],
                  waits=[("t", tD)], inc="syq")
        ldx = p.n("syq")
        for c in range(CH):
            p.op("v", lambda e, c=c: e.tensor_mul(R3[:, 2048:4096], oTs[c][:], R3[:, 0:2048]),
                 waits=[("syq", ldx), ("v", vbc)], inc="v")
            p.op("v", lambda e, c=c: e.tensor_scalar(
                out=oTs[c][:], in0=R3[:, 2048:4096], scalar1=MAGIC, scalar2=MAGIC,
                op0=ALU.add, op1=ALU.subtract), inc="v")
        vxq = p.n("v")

        prev_mm = [0, 0]
        prev_ev = [0, 0]
        for oc in range(CH):
            PP = PA if oc % 2 == 0 else PB
            for c in range(CH):
                mm4(PP, AR[:, 1024 * c + 128 * oc: 1024 * c + 128 * (oc + 1)],
                    oTs[c][:, :], c == 0, c == CH - 1,
                    waits=[("v", max(vxq, vwq)), ("t", prev_mm[oc % 2]),
                           ("v", prev_ev[oc % 2])])
            prev_mm[oc % 2] = p.n("t")
            p.op("v", lambda e, oc=oc, PP=PP: e.scalar_tensor_tensor(
                out=asm[1][:, :], in0=PP[:, :], scalar=wscale[:, oc: oc + 1],
                in1=asm[0][:, :], op0=ALU.mult, op1=ALU.mult),
                waits=[("t", prev_mm[oc % 2]), ("s", p.n("s"))], inc="v")
            prev_ev[oc % 2] = p.n("v")
            p.op("s", lambda e, oc=oc: e.activation(
                out=AR[:, 8192:10240], in_=asm[1][:, :], func=AF.Silu,
                bias=biases["l0_fb"][:, oc: oc + 1]), waits=[("v", p.n("v"))], inc="s")
            sg = p.n("s")
            if first_out:
                fout = R3[:, 0:2048] if oc % 2 == 0 else ROWS[:, 0:2048]
                p.op("v", lambda e, oc=oc, fout=fout: e.tensor_mul(fout, sT[oc], AR[:, 8192:10240]),
                     waits=[("s", sg), ("ost", max(0, p.n("ost") - 16))], inc="v")
                p.op("v", lambda e, oc=oc, fout=fout: e.tensor_add(fout, fout, xT[oc][:]),
                     inc="v")
                p.dma("sy", outT[128 * oc: 128 * (oc + 1), :], fout,
                      waits=[("v", p.n("v"))], inc="ost")
                p.op("sy", lambda e: e.nop(), waits=[("ost", p.n("ost"))], inc="sy2")
            else:
                p.op("v", lambda e, oc=oc: e.tensor_mul(R3b[:, 4096:6144], sT[oc], AR[:, 8192:10240]),
                     waits=[("s", sg)], inc="v")
                p.op("v", lambda e, oc=oc: e.tensor_add(xT[oc][:], xT[oc][:], R3b[:, 4096:6144]),
                     inc="v")

    def phase_E1(sD, tD, dump_dacc=False):
        if dump_dacc:
            d_st3 = nc.dram_tensor("d_st3", [D, T], BF16, kind="ExternalOutput").ap()
            for c in range(CH):
                p.dma("sy", d_st3[128 * c: 128 * (c + 1), :], sT[c],
                      waits=[("s", sD), ("t", tD)], inc="dbgst")
            p.op("sy", lambda e: e.nop(), waits=[("dbgst", p.n("dbgst"))], inc="sy2")
        wd2 = [AR[:, 0:3072], AR[:, 3072:6144]]
        ws2 = [AR[:, 6144:7168], AR[:, 7168:8192]]
        gsil = AR[:, 8192:8704]
        for c in range(CH):
            p.dma("sy", oTs[c][:], hbm["xnb"][128 * c: 128 * (c + 1), :],
                  waits=[("t", tD)], inc="syq")
        ldx = p.n("syq")
        e1_v = p.n("v")
        ev = {}
        for grp in range(4):
            for ig in range(6):
                hc = 6 * grp + ig
                p.dma("g", ws[hc % 2][:].rearrange("p (a b) -> p a b", a=CH),
                      ins["l1_wgt"].rearrange("(a p) b -> p a b", p=128)[:, :, 128 * hc: 128 * (hc + 1)],
                      waits=[("t", ev.get(("mm", hc - 2), tD))], inc="gq")
                p.dma("g", ws2[hc % 2].rearrange("p (a b) -> p a b", a=CH),
                      ins["l1_wut"].rearrange("(a p) b -> p a b", p=128)[:, :, 128 * hc: 128 * (hc + 1)],
                      inc="gq")
                ld = p.n("gq")
                w0 = [("gq", ld), ("syq", ldx), ("s", sD),
                      ("v", max(e1_v, ev.get(("hv", hc - 1), 0))),
                      ("s", ev.get(("hs", hc - 1), 0))]
                for c in range(CH):
                    for qq in range(4):
                        sl = slice(512 * qq, 512 * (qq + 1))
                        p.op("t", lambda e, hc=hc, c=c, sl=sl: e.matmul(
                            PA[:, sl], ws[hc % 2][:, 128 * c: 128 * (c + 1)],
                            oTs[c][:, sl], start=(c == 0), stop=(c == CH - 1)),
                            waits=(w0 if (c == 0 and qq == 0) else ()), inc="t")
                    for qq in range(4):
                        sl = slice(512 * qq, 512 * (qq + 1))
                        p.op("t", lambda e, hc=hc, c=c, sl=sl: e.matmul(
                            PB[:, sl], ws2[hc % 2][:, 128 * c: 128 * (c + 1)],
                            oTs[c][:, sl], start=(c == 0), stop=(c == CH - 1)), inc="t")
                ev[("mm", hc)] = p.n("t")
                for qq in range(4):
                    sl = slice(512 * qq, 512 * (qq + 1))
                    p.op("s", lambda e, sl=sl: e.activation(
                        out=gsil, in_=PA[:, sl], func=AF.Silu, scale=1.0 / 127.0),
                        waits=[("t", ev[("mm", hc)]), ("v", p.n("v"))], inc="s")
                    p.op("v", lambda e, ig=ig, sl=sl: e.scalar_tensor_tensor(
                        out=hsl[ig][:, sl], in0=PB[:, sl], scalar=1.0 / 127.0,
                        in1=gsil, op0=ALU.mult, op1=ALU.mult),
                        waits=[("s", p.n("s"))], inc="v")
                ev[("hv", hc)] = p.n("v")
                ev[("hs", hc)] = p.n("s")
            hv_grp = p.n("v")
            prev_d = [ev[("mm", 6 * grp + 5)], ev[("mm", 6 * grp + 5)]]
            for dc in range(CH):
                p.dma("g", wd2[dc % 2].rearrange("p (a b) -> p a b", a=24),
                      ins["l1_wdt"].rearrange("(a p) b -> p a b", p=128)[:, :, 128 * dc: 128 * (dc + 1)],
                      waits=[("t", prev_d[dc % 2])], inc="gq")
                ldd = p.n("gq")
                PP = PA if dc % 2 == 0 else PB
                for ig in range(6):
                    hc = 6 * grp + ig
                    mm4(PP, wd2[dc % 2][:, 128 * hc: 128 * (hc + 1)],
                        hsl[ig][:, :], ig == 0, ig == 5,
                        waits=[("gq", ldd), ("v", hv_grp), ("v", ev.get(("dacc", dc), 0))])
                prev_d[dc % 2] = p.n("t")
                if grp == 0:
                    p.op("v", lambda e, dc=dc, PP=PP: e.tensor_copy(dacc[dc], PP[:, :]),
                         waits=[("t", prev_d[dc % 2])], inc="v")
                else:
                    p.op("v", lambda e, dc=dc, PP=PP: e.tensor_add(dacc[dc], dacc[dc], PP[:, :]),
                         waits=[("t", prev_d[dc % 2])], inc="v")
                ev[("dacc", dc)] = p.n("v")
            ev[("hv", 6 * grp + 5)] = p.n("v")
        vdacc = p.n("v")
        if dump_dacc:
            d_da = nc.dram_tensor("d_da", [D, T], BF16, kind="ExternalOutput").ap()
            d_st2 = nc.dram_tensor("d_st2", [D, T], BF16, kind="ExternalOutput").ap()
            allw = [("t", p.n("t")), ("s", p.n("s")), ("v", p.n("v")),
                    ("syq", p.n("syq")), ("gq", p.n("gq"))]
            d_ots2 = nc.dram_tensor("d_ots2", [D, T], BF16, kind="ExternalOutput").ap()
            for c in range(CH):
                p.dma("sy", d_da[128 * c: 128 * (c + 1), :], dacc[c], waits=allw, inc="dbgst")
                p.dma("sy", d_st2[128 * c: 128 * (c + 1), :], sT[c], waits=allw, inc="dbgst")
                p.dma("sy", d_ots2[128 * c: 128 * (c + 1), :], oTs[c][:], waits=allw, inc="dbgst")
            p.op("sy", lambda e: e.nop(), waits=[("dbgst", p.n("dbgst"))], inc="sy2")
        ost0 = p.n("ost")
        if dump_dacc:
            d_g1 = nc.dram_tensor("d_g1", [D, T], BF16, kind="ExternalOutput").ap()
        for dc in range(CH):
            fout = R3[:, 2048:4096] if dc % 2 == 0 else ROWS[:, 0:2048]
            p.op("s", lambda e, dc=dc: e.activation(
                out=st_sc[0][:, :], in_=dacc[dc], func=AF.Silu, scale=1.0 / 127.0),
                waits=[("v", vdacc), ("v", p.n("v"))], inc="s")
            sg = p.n("s")
            p.op("v", lambda e, dc=dc, fout=fout: e.tensor_mul(fout, sT[dc], st_sc[0][:, :]),
                 waits=[("s", sg), ("ost", max(ost0, p.n("ost") - 16))], inc="v")
            p.op("v", lambda e, dc=dc, fout=fout: e.tensor_add(fout, fout, xT[dc][:]), inc="v")
            if dump_dacc:
                p.dma("sy", d_g1[128 * dc: 128 * (dc + 1), :], st_sc[0][:, :],
                      waits=[("v", p.n("v"))], inc="ost")
                p.op("sy", lambda e: e.nop(), waits=[("ost", p.n("ost"))], inc="sy2")
            p.dma("sy", outT[128 * dc: 128 * (dc + 1), :], fout,
                  waits=[("v", p.n("v"))], inc="ost")

    done = False
    for li, L in enumerate(layers):
        first = (li == 0)
        p.counts.setdefault("sy2", 0)
        p.counts.setdefault("odma", 0)
        xn_v = phase_A(L, first)
        if upto == "A":
            done = True
            break
        phase_B(L, xn_v, sub=(upto if upto and upto.startswith("B") else None))
        if dbg:
            kr = KR[L]
            d_xn = nc.dram_tensor("d_xn", [D, T], BF16, kind="ExternalOutput").ap()
            d_idx = nc.dram_tensor("d_idx", [128, 512], F32, kind="ExternalOutput").ap()
            d_qpp = nc.dram_tensor("d_qpp", [16 * kr, T], BF16, kind="ExternalOutput").ap()
            d_kpp = nc.dram_tensor("d_kpp", [16 * kr, T], BF16, kind="ExternalOutput").ap()
            d_vp = nc.dram_tensor("d_vp", [T, VW[L]], BF16, kind="ExternalOutput").ap()
            allw = [("t", p.n("t")), ("s", p.n("s")), ("v", p.n("v")),
                    ("syq", p.n("syq")), ("gq", p.n("gq"))]
            p.dma("sy", d_xn, hbm["xnb"][:, :], waits=allw, inc="ost")
            p.dma("sy", d_idx, IDX[:], waits=allw, inc="ost")
            p.dma("sy", d_qpp, hbm[f"qpp{L}"][:, :], waits=allw, inc="ost")
            p.dma("sy", d_kpp, hbm[f"kpp{L}"][:, :], waits=allw, inc="ost")
            p.dma("sy", d_vp, hbm[f"vp{L}"][:, :], waits=allw, inc="ost")
        if upto and upto.startswith("B"):
            done = True
            break
        phase_C(L, c0=(upto == "C0"), upto_c1=(upto == "C1"))
        if upto == "C1":
            done = True
            break
        if upto == "C0":
            done = True
            break
        if dbg and upto == "C":
            d_ots = nc.dram_tensor("d_ots", [D, T], BF16, kind="ExternalOutput").ap()
            allw = [("t", p.n("t")), ("s", p.n("s")), ("v", p.n("v")),
                    ("syq", p.n("syq")), ("gq", p.n("gq")), ("odma", p.n("odma"))]
            for c in range(CH):
                p.dma("sy", d_ots[128 * c: 128 * (c + 1), :], oTs[c][:], waits=allw, inc="ost")
            if L == 1:
                d_m8 = nc.dram_tensor("d_m8", [8, T], F32, kind="ExternalOutput").ap()
                p.dma("sy", d_m8, hbm["m8"][:, :], waits=allw, inc="ost")
        if upto == "C":
            done = True
            break
        sD, tD = phase_D(L)
        if dbg and upto == "D":
            d_st = nc.dram_tensor("d_st", [D, T], BF16, kind="ExternalOutput").ap()
            allw = [("t", p.n("t")), ("s", p.n("s")), ("v", p.n("v")),
                    ("syq", p.n("syq")), ("gq", p.n("gq")), ("odma", p.n("odma"))]
            for c in range(CH):
                p.dma("sy", d_st[128 * c: 128 * (c + 1), :], sT[c], waits=allw, inc="ost")
        if upto == "D":
            done = True
            break
        if L == 0:
            phase_E0(sD, tD, first_out=(L == layers[-1]))
        else:
            import os as _os2
            phase_E1(sD, tD, dump_dacc=((dbg and upto == "E") or _os2.environ.get("DUMPF") == "1"))
        if upto == "E":
            done = True
            break
    if done:
        for c in range(CH):
            p.dma("g", outT[128 * c: 128 * (c + 1), :], xT[c][:],
                  waits=[("t", p.n("t")), ("s", p.n("s")), ("v", p.n("v")),
                         ("syq", p.n("syq")), ("gq", p.n("gq")),
                         ("odma", p.n("odma"))] + (
                      [(f"cck{layers[0]}", 1), (f"ccv{layers[0]}", 1)]
                      if p.n(f"cck{layers[0]}") else []),
                  inc="ost")
    p.op("sy", lambda e: e.nop(), waits=[("ost", p.n("ost"))], inc="sy2")
    p.op("g", lambda e: e.nop(), waits=[("ost", p.n("ost"))], inc="g")

    sems = {name: nc.alloc_semaphore(name) for name in p.counts}
    with nc.Block() as block:
        block.tensor(lambda e: p.emit("t", e, sems))
        block.scalar(lambda e: p.emit("s", e, sems))
        block.vector(lambda e: p.emit("v", e, sems))
        block.sync(lambda e: p.emit("sy", e, sems))
        block.gpsimd(lambda e: p.emit("g", e, sems))
    return nc


def _prep_inputs(inputs):
    ac = np.ascontiguousarray
    f32 = np.float32

    def colmaj(v):
        return ac(np.asarray(v, f32).reshape(CH, 128).T)

    base0 = base1 = None
    base0 = {
        "l0_wq": ac(np.asarray(inputs["L0_attn_Wq"], f32)),
        "l0_wk": ac(np.asarray(inputs["L0_attn_Wk"], f32)),
        "l0_wv": ac(np.asarray(inputs["L0_attn_Wv"], f32)),
        "l0_wo": ac(np.asarray(inputs["L0_attn_Wo"], f32)),
        "l0_fw": ac(np.asarray(inputs["L0_ffn_weight"], f32)),
        "l0_fwt": ac(np.asarray(inputs["L0_ffn_weight"], f32).T),
        "l0_bq": colmaj(inputs["L0_attn_bq"]),
        "l0_bk": colmaj(inputs["L0_attn_bk"]),
        "l0_bv": colmaj(inputs["L0_attn_bv"]),
        "l0_bo": colmaj(inputs["L0_attn_bo"]),
        "l0_fb": colmaj(inputs["L0_ffn_bias"]),
        "l0_rt": ac(np.asarray(inputs["L0_attn_R"], f32).T),
    }
    base1 = {
        "l1_wq": ac(np.asarray(inputs["L1_attn_Wq"], f32)),
        "l1_wk": ac(np.asarray(inputs["L1_attn_Wk"], f32)),
        "l1_wv": ac(np.asarray(inputs["L1_attn_Wv"], f32)),
        "l1_wo": ac(np.asarray(inputs["L1_attn_Wo"], f32)),
        "l1_lq1": ac(np.asarray(inputs["L1_attn_lq1"], f32).reshape(1, 64)),
        "l1_lk1": ac(np.asarray(inputs["L1_attn_lk1"], f32).reshape(1, 64)),
        "l1_lq2": ac(np.asarray(inputs["L1_attn_lq2"], f32).reshape(1, 64)),
        "l1_lk2": ac(np.asarray(inputs["L1_attn_lk2"], f32).reshape(1, 64)),
        "l1_lng": ac(np.asarray(inputs["L1_attn_ln_g"], f32).reshape(128, 1)),
        "l1_lnb": ac(np.asarray(inputs["L1_attn_ln_b"], f32).reshape(128, 1)),
        "l1_wgt": ac(np.asarray(inputs["L1_ffn_w_gate"], np.int32).T),
        "l1_wut": ac(np.asarray(inputs["L1_ffn_w_up"], np.int32).T),
        "l1_wdt": ac(np.asarray(inputs["L1_ffn_w_down"], np.int32).T),
    }
    x = np.asarray(inputs["L0_attn_x"], f32)
    xsh = []
    for c in range(8):
        b, half = c // 2, c % 2
        xsh.append(ac(x[b, 2048 * half: 2048 * (half + 1), :].T))
    return base0, base1, xsh


def kernel(**inputs):
    base0, base1, xsh = _prep_inputs(inputs)
    if FUSED:
        nc = build_program([0, 1])
        in_maps = [dict(base0, **base1, xoT=xsh[c]) for c in range(8)]
        res = run_bass_kernel_spmd(nc, in_maps, core_ids=list(range(8)))
        outs = [res.results[c]["outT"] for c in range(8)]
    else:
        nc0 = build_program([0])
        in_maps = [dict(base0, xoT=xsh[c]) for c in range(8)]
        r0 = run_bass_kernel_spmd(nc0, in_maps, core_ids=list(range(8)))
        x1 = [np.ascontiguousarray(np.asarray(r0.results[c]["outT"], np.float32))
              for c in range(8)]
        nc1 = build_program([1])
        in_maps = [dict(base1, xoT=x1[c]) for c in range(8)]
        r1 = run_bass_kernel_spmd(nc1, in_maps, core_ids=list(range(8)))
        outs = [r1.results[c]["outT"] for c in range(8)]
    out = np.empty((4, 4096, 1024), np.float32)
    for c in range(8):
        b, half = c // 2, c % 2
        out[b, 2048 * half: 2048 * (half + 1), :] = np.asarray(outs[c], np.float32).T
    return out


# version 05
# speedup vs baseline: 1.0019x; 1.0019x over previous
import math
import os
import numpy as np
import ml_dtypes
import concourse.bass as bass
import concourse.mybir as mybir
from concourse.bass_utils import run_bass_kernel_spmd

F32 = mybir.dt.float32
BF16 = mybir.dt.bfloat16
I32 = mybir.dt.int32
AF = mybir.ActivationFunctionType
ALU = mybir.AluOpType
AX = mybir.AxisListType

D = 1024
T = 2048
CH = 8
NT = 16
MAGIC = float(1.5 * 2 ** 23)
CPEN = 400.0
LAM_INIT = 0.8 - 0.6 * math.exp(-0.3 * 12)
KR = {0: 68, 1: 65}
VW = {0: 1040, 1: 1024}

FUSED = True


class Prog:

    def __init__(self):
        self.items = {k: [] for k in ("t", "s", "v", "sy", "g")}
        self.counts = {}

    def n(self, sem):
        return self.counts.get(sem, 0)

    def op(self, eng, fn, waits=(), inc=None, inck=1):
        w = [(s, v) for (s, v) in waits if v > 0]
        self.items[eng].append((fn, w, inc, inck))
        if inc is not None:
            self.counts[inc] = self.counts.get(inc, 0) + inck
        for s, _ in w:
            self.counts.setdefault(s, 0)

    def dma(self, eng, out, in_, waits=(), inc=None, **kw):
        assert inc is not None
        self.op(eng, lambda e: e.dma_start(out=out, in_=in_, **kw), waits, inc, 16)

    def emit(self, eng_name, e, sems):
        self_edge = eng_name in ("v", "s") and eng_name in sems
        waited = {}
        own = 0
        for fn, waits, inc, inck in self.items[eng_name]:
            if self_edge and own > 0 and waited.get(eng_name, 0) < own:
                e.wait_ge(sems[eng_name], own)
                waited[eng_name] = own
            for snm, val in waits:
                if waited.get(snm, 0) >= val:
                    continue
                e.wait_ge(sems[snm], val)
                waited[snm] = val
            instr = fn(e)
            if inc is not None:
                instr.then_inc(sems[inc], inck)
                if inc == eng_name:
                    own += inck


def build_program(layers, upto=None, dbg=False):
    nc = bass.Bass("TRN2", target_bir_lowering=False, debug=False)
    p = Prog()

    xoT = nc.dram_tensor("xoT", [D, T], F32, kind="ExternalInput").ap()
    outT = nc.dram_tensor("outT", [D, T], F32, kind="ExternalOutput").ap()
    ins = {}

    def ext(name, shape, dtype=F32):
        ins[name] = nc.dram_tensor(name, shape, dtype, kind="ExternalInput").ap()

    if 0 in layers:
        for w in ("l0_wq", "l0_wk", "l0_wv", "l0_wo", "l0_fw", "l0_fwt"):
            ext(w, [D, D])
        for b in ("l0_bq", "l0_bk", "l0_bv", "l0_bo", "l0_fb"):
            ext(b, [128, CH])
        ext("l0_rt", [64, 8])
    if 1 in layers:
        for w in ("l1_wq", "l1_wk", "l1_wv", "l1_wo"):
            ext(w, [D, D])
        for b in ("l1_lq1", "l1_lk1", "l1_lq2", "l1_lk2"):
            ext(b, [1, 64])
        ext("l1_lng", [128, 1])
        ext("l1_lnb", [128, 1])
        ext("l1_wgt", [D, 3072], I32)
        ext("l1_wut", [D, 3072], I32)
        ext("l1_wdt", [3072, D], I32)

    hbm = {"xnb": nc.dram_tensor("xnb", [D, T], BF16, kind="Internal")}
    for L in layers:
        kr, vw = KR[L], VW[L]
        hbm[f"qpp{L}"] = nc.dram_tensor(f"qpp{L}", [16 * kr, T], BF16, kind="Internal")
        hbm[f"kpp{L}"] = nc.dram_tensor(f"kpp{L}", [16 * kr, T], BF16, kind="Internal")
        hbm[f"kppf{L}"] = nc.dram_tensor(f"kppf{L}", [32 * kr, T], BF16, kind="Internal")
        hbm[f"vp{L}"] = nc.dram_tensor(f"vp{L}", [T, vw], BF16, kind="Internal")
        hbm[f"vpf{L}"] = nc.dram_tensor(f"vpf{L}", [2 * T, vw], BF16, kind="Internal")
    if 1 in layers:
        hbm["m8"] = nc.dram_tensor("m8", [8, T], F32, kind="Internal")

    A = nc.alloc_sbuf_tensor
    xT = [A(f"xT{c}", [128, T], BF16) for c in range(CH)]
    oTs = [A(f"oTs{c}", [128, T], BF16) for c in range(CH)]
    R1 = A("R1", [128, 16896], BF16)
    R2 = A("R2", [128, 16384], BF16)
    R3 = A("R3", [128, 4096], F32)
    R3b = R3.bitcast(BF16)
    asm = [A(f"asm{i}", [128, T], BF16) for i in range(2)]
    ws = [A(f"ws{i}", [128, 1024], BF16) for i in range(2)]
    AR = A("AR", [128, 10240], BF16)
    ROWS = A("ROWS", [128, 4096], F32)
    ROWSB = A("ROWSB", [128, 4096], BF16)
    RB32 = ROWSB.bitcast(F32)
    IDX = A("IDX", [128, 512], F32)
    IDXB = A("IDXB", [128, 16], BF16)
    ident = A("ident", [128, 128], BF16)
    idrow = A("idrow", [128, 128], F32)
    pidx = A("pidx", [128, 1], F32)
    identf = A("identf", [128, 128], F32)
    ones1 = A("ones1", [128, 1], BF16)
    onesr = A("onesr", [128, 128], BF16)
    onesrF = A("onesrF", [128, 128], F32)
    onesW = A("onesW", [128, 128], BF16)
    iota16 = A("iota16", [128, 16], F32)
    epsb = A("epsb", [128, 1], F32)
    wscale = A("wscale", [128, CH], F32)
    wrecip = A("wrecip", [128, CH], F32)
    biases = {}
    lrows = {}
    if 0 in layers:
        for b in ("l0_bq", "l0_bk", "l0_bv", "l0_bo", "l0_fb"):
            biases[b] = A("sb_" + b, [128, CH], F32)
        rtsb = A("rtsb", [128, 8], BF16)
    if 1 in layers:
        for b in ("l1_lq1", "l1_lk1", "l1_lq2", "l1_lk2"):
            lrows[b] = A("sb_" + b, [1, 64], F32)
        biases["l1_lng"] = A("sb_l1_lng", [128, 1], F32)
        biases["l1_lnb"] = A("sb_l1_lnb", [128, 1], F32)
        lamv = A("lam", [1, 1], F32)
        lngb = A("lngb", [128, 1], BF16)
        lnbb = A("lnbb", [128, 1], BF16)
        lngW = A("lngW", [128, 64], BF16)
        c1sb = A("c1sb", [128, CH], F32)

    QT = [R1[:, 2048 * c: 2048 * (c + 1)] for c in range(CH)]
    dacc = QT

    def vst(L, buf):
        w = VW[L] if L == 0 else 128
        return R1[:, 8448 * buf: 8448 * buf + 8 * w].rearrange("p (a b) -> p a b", a=8)

    xnT = [R2[:, 2048 * c: 2048 * (c + 1)] for c in range(CH)]
    sT = xnT
    kst = [R2[0:128, 0:4096], R2[0:128, 4096:8192]]
    qst = [R2[0:128, 8192:9216], R2[0:128, 9216:10240]]
    expb = [R2[:, 10240:11264], R2[:, 11264:12288]]
    o01 = [R2[:, 12288:14336], R2[:, 14336:16384]]

    mubc = R3b[:, 0:2048]
    rstdbc = R3b[:, 2048:4096]
    st_sc = [R3b[:, 0:2048], R3b[:, 2048:4096], R3b[:, 4096:6144]]
    rbcs = [R3b[:, 0:1024], R3b[:, 1024:2048]]
    l0tsl = R3b[:, 4096:6144]
    hsl = [R3b[:, 2048 * i: 2048 * (i + 1)] for i in range(4)] + \
          [asm[0][:, :], asm[1][:, :]]

    PA = nc.alloc_psum_tensor("PA", [128, 2048], F32)
    PB = nc.alloc_psum_tensor("PB", [128, 2048], F32)
    PBT16 = PB[0:16, 1152:1216].bitcast(BF16)
    PBT128 = PB[0:128, 1216:1217].bitcast(BF16)[:, 0:1]
    PBF1 = PB[0:1, 1280:1408]

    def mm4(PPcols, lhsT, rhs, start, stop, waits=(), width=2048):
        n = width // 512
        for i in range(n):
            sl = slice(512 * i, 512 * (i + 1))
            p.op("t", lambda e, sl=sl: e.matmul(PPcols[:, sl], lhsT, rhs[:, sl],
                                                start=start, stop=stop),
                 waits=(waits if i == 0 else ()), inc="t")

    p.op("g", lambda e: e.iota(idrow[:], pattern=[[1, 128]], base=0,
                               channel_multiplier=0, allow_small_or_imprecise_dtypes=True), inc="g")
    p.op("g", lambda e: e.iota(pidx[:], pattern=[[0, 1]], base=0,
                               channel_multiplier=1, allow_small_or_imprecise_dtypes=True), inc="g")
    p.op("g", lambda e: e.iota(iota16[:], pattern=[[0, 2], [1, 8]], base=0,
                               channel_multiplier=0, allow_small_or_imprecise_dtypes=True), inc="g")
    p.op("v", lambda e: e.tensor_scalar(out=identf[:], in0=idrow[:], scalar1=pidx[:],
                                        scalar2=None, op0=ALU.is_equal),
         waits=[("g", 2)], inc="v")
    p.op("v", lambda e: e.tensor_copy(ident[:], identf[:]), inc="v")
    p.op("v", lambda e: e.memset(ones1[:], 1.0), inc="v")
    p.op("v", lambda e: e.memset(onesr[:], 1.0), inc="v")
    p.op("v", lambda e: e.memset(onesrF[:], 1.0), inc="v")
    p.op("v", lambda e: e.memset(onesW[:], 0.0), inc="v")
    p.op("v", lambda e: e.memset(epsb[:], 1e-5), inc="v")
    for c in range(CH):
        p.op("v", lambda e, c=c: e.memset(onesW[0:64, 16 * c + 2 * c: 16 * c + 2 * c + 1], 1.0), inc="v")
        p.op("v", lambda e, c=c: e.memset(onesW[64:128, 16 * c + 2 * c + 1: 16 * c + 2 * c + 2], 1.0), inc="v")
    for b, sb in biases.items():
        p.dma("sy", sb[:], ins[b], inc="syq")
    for b, sb in lrows.items():
        p.dma("sy", sb[:], ins[b], inc="syq")
    if 0 in layers:
        p.dma("g", rtsb[0:64, :], ins["l0_rt"], inc="gq")
        p.dma("g", rtsb[64:128, :], ins["l0_rt"], inc="gq")
    for c in range(CH):
        p.dma("g", xT[c][:], xoT[128 * c: 128 * (c + 1), :], inc="gq")
    if 1 in layers:
        p.op("v", lambda e: e.tensor_copy(lngb[:], biases["l1_lng"][:]),
             waits=[("syq", p.n("syq"))], inc="v")
        p.op("v", lambda e: e.tensor_copy(lnbb[:], biases["l1_lnb"][:]), inc="v")
        p.op("v", lambda e: e.memset(lngW[:], 0.0), inc="v")
        for h in range(8):
            p.op("v", lambda e, h=h: e.tensor_copy(lngW[:, 8 * h + h: 8 * h + h + 1], lngb[:]), inc="v")
        p.op("v", lambda e: e.tensor_mul(ROWS[0:1, 0:64], lrows["l1_lq1"][:], lrows["l1_lk1"][:]), inc="v")
        p.op("v", lambda e: e.tensor_mul(ROWS[0:1, 64:128], lrows["l1_lq2"][:], lrows["l1_lk2"][:]), inc="v")
        p.op("v", lambda e: e.reduce_sum(ROWS[0:1, 128:129], ROWS[0:1, 0:64], axis=AX.X), inc="v")
        p.op("v", lambda e: e.reduce_sum(ROWS[0:1, 129:130], ROWS[0:1, 64:128], axis=AX.X), inc="v")
        p.op("s", lambda e: e.activation(out=ROWS[0:1, 130:131], in_=ROWS[0:1, 128:129], func=AF.Exp),
             waits=[("v", p.n("v"))], inc="s")
        p.op("s", lambda e: e.activation(out=ROWS[0:1, 131:132], in_=ROWS[0:1, 129:130], func=AF.Exp), inc="s")
        p.op("v", lambda e: e.tensor_sub(ROWS[0:1, 132:133], ROWS[0:1, 130:131], ROWS[0:1, 131:132]),
             waits=[("s", p.n("s"))], inc="v")
        p.op("v", lambda e: e.tensor_scalar(out=lamv[:], in0=ROWS[0:1, 132:133], scalar1=LAM_INIT,
                                            scalar2=None, op0=ALU.add), inc="v")
    setup_gq = p.n("gq")
    setup_syq = p.n("syq")

    def phase_A(L, first):
        if first:
            xw = [("gq", setup_gq), ("syq", setup_syq)]
        else:
            xw = [("v", p.n("v")), ("s", p.n("s")), ("t", p.n("t")),
                  ("syq", p.n("syq")), ("gq", p.n("gq")), ("odma", p.n("odma"))]
        amm = []
        for c in range(CH):
            buf = asm[c % 2]
            w = list(xw)
            if c >= 2:
                w.append(("t", amm[c - 2]))
            p.op("v", lambda e, b=buf, c=c: e.tensor_mul(b[:], xT[c][:], xT[c][:]),
                 waits=w, inc="v")
            vsq = p.n("v")
            mm4(PA[0:1, :], ones1[:], xT[c][:], c == 0, c == CH - 1, waits=xw)
            mm4(PB[0:1, :], ones1[:], buf[:], c == 0, c == CH - 1, waits=[("v", vsq)])
            amm.append(p.n("t"))
        ts = p.n("t")
        p.op("v", lambda e: e.tensor_scalar(out=ROWS[0:1, 0:2048], in0=PA[0:1, :],
                                            scalar1=1.0 / D, scalar2=None, op0=ALU.mult),
             waits=[("t", ts)], inc="v")
        p.op("v", lambda e: e.tensor_scalar(out=ROWS[0:1, 2048:4096], in0=PB[0:1, :],
                                            scalar1=1.0 / D, scalar2=None, op0=ALU.mult), inc="v")
        p.op("v", lambda e: e.tensor_mul(RB32[0:1, 0:2048], ROWS[0:1, 0:2048], ROWS[0:1, 0:2048]), inc="v")
        p.op("v", lambda e: e.tensor_sub(ROWS[0:1, 2048:4096], ROWS[0:1, 2048:4096],
                                         RB32[0:1, 0:2048]), inc="v")
        p.op("s", lambda e: e.activation(out=RB32[0:1, 0:2048], in_=ROWS[0:1, 2048:4096],
                                         func=AF.Sqrt, bias=epsb[0:1, :]), waits=[("v", p.n("v"))], inc="s")
        p.op("v", lambda e: e.reciprocal(ROWS[0:1, 2048:4096], RB32[0:1, 0:2048]),
             waits=[("s", p.n("s"))], inc="v")
        p.op("v", lambda e: e.tensor_copy(asm[0][0:1, 0:2048], ROWS[0:1, 0:2048]), inc="v")
        p.op("v", lambda e: e.tensor_copy(asm[1][0:1, 0:2048], ROWS[0:1, 2048:4096]), inc="v")
        vr = p.n("v")
        mm4(PA, onesr[0:1, :], asm[0][0:1, 0:2048], True, True, waits=[("v", vr)])
        mm4(PB, onesr[0:1, :], asm[1][0:1, 0:2048], True, True)
        tb = p.n("t")
        p.op("v", lambda e: e.tensor_copy(mubc[:], PA[:, :]), waits=[("t", tb)], inc="v")
        p.op("v", lambda e: e.tensor_copy(rstdbc[:], PB[:, :]), inc="v")
        vb = p.n("v")
        for c in range(CH):
            p.op("v", lambda e, c=c: e.tensor_sub(asm[c % 2][:], xT[c][:], mubc[:]),
                 waits=[("v", vb)], inc="v")
            p.op("v", lambda e, c=c: e.tensor_mul(xnT[c], asm[c % 2][:], rstdbc[:]), inc="v")
            p.dma("sy", hbm["xnb"][128 * c: 128 * (c + 1), :], xnT[c],
                  waits=[("v", p.n("v"))], inc="syq")
        return p.n("v")

    def proj(L, wname, xn_v, post):
        prev_mm = [p.n("t"), p.n("t")]
        prev_ev = [p.n("s"), p.n("s")]
        entry_v = p.n("v")
        for oc in range(CH):
            p.dma("g", ws[oc % 2][:].rearrange("p (a b) -> p a b", a=CH),
                  ins[wname].rearrange("(a p) b -> p a b", p=128)[:, :, 128 * oc: 128 * (oc + 1)],
                  waits=[("t", prev_mm[oc % 2])], inc="gq")
            ld = p.n("gq")
            PP = PA if oc % 2 == 0 else PB
            for c in range(CH):
                mm4(PP, ws[oc % 2][:, 128 * c: 128 * (c + 1)], xnT[c],
                    c == 0, c == CH - 1,
                    waits=[("gq", ld), ("v", max(xn_v, entry_v)), ("s", prev_ev[oc % 2])])
            prev_mm[oc % 2] = p.n("t")
            post(oc, PP, p.n("t"))
            prev_ev[oc % 2] = p.n("s")

    def bucket_idx(srcs, colbase, src_s):
        for oc in range(CH):
            for tt in range(NT):
                sl = slice(128 * tt, 128 * (tt + 1))
                w = [("s", src_s), ("v", p.n("v"))]
                p.op("t", lambda e, oc=oc, sl=sl: e.matmul(
                    PB[:, 1024:1032], srcs[oc][0:64, sl], rtsb[0:64, :], start=True, stop=True),
                    waits=w, inc="t")
                p.op("t", lambda e, oc=oc, sl=sl: e.matmul(
                    PB[:, 1032:1040], srcs[oc][64:128, sl], rtsb[64:128, :], start=True, stop=True), inc="t")
                tl = p.n("t")
                p.op("v", lambda e: e.reduce_max(
                    R3[:, 0:2].rearrange("p (a b) -> p a b", b=1),
                    PB[:, 1024:1040].rearrange("p (a b) -> p a b", b=8), axis=AX.X),
                    waits=[("t", tl)], inc="v")
                p.op("v", lambda e: e.tensor_tensor(
                    out=R3[:, 2:18].rearrange("p (a b) -> p a b", b=8),
                    in0=PB[:, 1024:1040].rearrange("p (a b) -> p a b", b=8),
                    in1=R3[:, 0:2].rearrange("p (a b) -> p a b", b=1).broadcast_to([128, 2, 8]),
                    op=ALU.is_ge), inc="v")
                p.op("v", lambda e: e.tensor_mul(R3[:, 2:18], R3[:, 2:18], iota16[:, 0:16]), inc="v")
                p.op("v", lambda e, oc=oc, tt=tt: e.reduce_sum(
                    IDX[:, colbase + 16 * tt + 2 * oc: colbase + 16 * tt + 2 * oc + 2].rearrange(
                        "p (a b) -> p a b", b=1),
                    R3[:, 2:18].rearrange("p (a b) -> p a b", b=8), axis=AX.X), inc="v")
        return p.n("v")

    def idx_to_rows(colbase, dst_rw, v_ready):
        for tt in range(NT):
            p.op("v", lambda e, tt=tt: e.tensor_copy(
                IDXB[:, 0:16], IDX[:, colbase + 16 * tt: colbase + 16 * tt + 16]),
                waits=[("v", v_ready), ("t", p.n("t"))], inc="v")
            p.op("t", lambda e: e.transpose(PBT16, IDXB[:, 0:16], ident[:]),
                 waits=[("v", p.n("v"))], inc="t")
            p.op("v", lambda e, tt=tt: e.tensor_copy(
                dst_rw[:, 128 * tt: 128 * (tt + 1)], PBT16),
                waits=[("t", p.n("t"))], inc="v")
        return p.n("v")

    def row_dma(dst_pp, kr, rowoff, src_rows_b, waits):
        p.dma("sy", dst_pp.ap().rearrange("(h r) t -> h r t", r=kr)[:, rowoff: rowoff + 1, :],
              src_rows_b.rearrange("h (o t) -> h o t", o=1), waits=waits, inc="syq")

    def phase_B(L, xn_v, sub=None):
        plain_ev = (sub == 'B5a')
        small_wv = (sub == 'B5b')
        kr, vw = KR[L], VW[L]
        pref = "l0_" if L == 0 else "l1_"
        qpp, kpp, vp = hbm[f"qpp{L}"], hbm[f"kpp{L}"], hbm[f"vp{L}"]

        def qpost(oc, PP, tmm):
            if L == 0:
                p.op("v", lambda e, oc=oc, PP=PP: e.tensor_scalar(
                    out=QT[oc], in0=PP[:, :], scalar1=biases["l0_bq"][:, oc: oc + 1],
                    scalar2=None, op0=ALU.add), waits=[("t", tmm)], inc="v")
                p.op("s", lambda e: e.nop(), waits=[("v", p.n("v"))], inc="s")
            else:
                p.op("s", lambda e, oc=oc, PP=PP: e.activation(
                    out=QT[oc], in_=PP[:, :], func=AF.Copy), waits=[("t", tmm)], inc="s")
        proj(L, pref + "wq", xn_v, qpost)
        s_qt = p.n("s")
        if sub == "B1":
            return

        def kpost(oc, PP, tmm):
            if L == 0:
                p.op("v", lambda e, oc=oc, PP=PP: e.tensor_scalar(
                    out=oTs[oc][:], in0=PP[:, :], scalar1=biases["l0_bk"][:, oc: oc + 1],
                    scalar2=None, op0=ALU.add), waits=[("t", tmm)], inc="v")
                p.op("s", lambda e: e.nop(), waits=[("v", p.n("v"))], inc="s")
            else:
                p.op("s", lambda e, oc=oc, PP=PP: e.activation(
                    out=oTs[oc][:], in_=PP[:, :], func=AF.Copy), waits=[("t", tmm)], inc="s")
        proj(L, pref + "wk", xn_v, kpost)
        s_kt = p.n("s")
        if sub == "B2":
            return

        for h in range(16):
            p.dma("sy", kpp[kr * h: kr * h + 64, :],
                  oTs[h // 2][64 * (h % 2): 64 * (h % 2) + 64, :],
                  waits=[("s", s_kt)], inc="syq")
        for h in range(16):
            p.dma("sy", qpp[kr * h: kr * h + 64, :],
                  QT[h // 2][64 * (h % 2): 64 * (h % 2) + 64, :],
                  waits=[("s", s_qt)], inc="syq")

        if L == 1:
            p.op("v", lambda e: e.memset(ROWSB[0:16, 0:2048], 1.0),
                 waits=[("v", p.n("v")), ("syq", p.n("syq"))], inc="v")
            row_dma(kpp, kr, 64, ROWSB[0:16, 0:2048], [("v", p.n("v"))])
        if L == 0:
            vk = bucket_idx(oTs, 0, s_kt)
            vk2 = idx_to_rows(0, ROWS[0:16, 2048:4096], vk)
            p.op("v", lambda e: e.memset(ROWSB[0:16, 0:2048], 1.0), waits=[("v", vk2)], inc="v")
            row_dma(kpp, kr, 64, ROWSB[0:16, 0:2048], [("v", p.n("v"))])
            p.op("v", lambda e: e.tensor_mul(ROWSB[0:16, 2048:4096], ROWS[0:16, 2048:4096],
                                             ROWS[0:16, 2048:4096]), inc="v")
            row_dma(kpp, kr, 65, ROWSB[0:16, 2048:4096], [("v", p.n("v"))])
            p.op("v", lambda e: e.memset(ROWSB[0:16, 0:2048], -CPEN),
                 waits=[("syq", p.n("syq"))], inc="v")
            row_dma(kpp, kr, 66, ROWSB[0:16, 0:2048], [("v", p.n("v"))])
            p.op("v", lambda e: e.tensor_scalar(out=ROWSB[0:16, 2048:4096],
                                                in0=ROWS[0:16, 2048:4096], scalar1=2 * CPEN,
                                                scalar2=None, op0=ALU.mult),
                 waits=[("syq", p.n("syq"))], inc="v")
            row_dma(kpp, kr, 67, ROWSB[0:16, 2048:4096], [("v", p.n("v"))])
            vq = bucket_idx(QT, 256, s_qt)
            vq2 = idx_to_rows(256, ROWS[0:16, 2048:4096], vq)
        else:
            vq2 = p.n("v")
        if sub == "B3":
            return

        q2mm = []
        for c in range(CH):
            w = [("s", s_qt), ("syq", p.n("syq"))]
            if c >= 2:
                w.append(("t", q2mm[c - 2]))
            p.op("v", lambda e, c=c: e.tensor_mul(asm[c % 2][:], QT[c][:], QT[c][:]),
                 waits=w, inc="v")
            mm4(PA[0:16, :], onesW[:, 16 * c: 16 * (c + 1)], asm[c % 2][:],
                c == 0, c == CH - 1, waits=[("v", p.n("v"))])
            q2mm.append(p.n("t"))
        p.op("v", lambda e: e.tensor_copy(ROWS[0:16, 0:2048], PA[0:16, :]),
             waits=[("t", p.n("t"))], inc="v")
        p.op("s", lambda e: e.activation(out=ROWS[0:16, 0:2048], in_=ROWS[0:16, 0:2048],
                                         func=AF.Sqrt), waits=[("v", p.n("v"))], inc="s")
        p.op("v", lambda e: e.tensor_scalar(out=ROWSB[0:16, 0:2048], in0=ROWS[0:16, 0:2048],
                                            scalar1=-16.0, scalar2=None, op0=ALU.mult),
             waits=[("s", p.n("s"))], inc="v")
        row_dma(qpp, kr, 64, ROWSB[0:16, 0:2048], [("v", p.n("v"))])
        if L == 0:
            p.op("v", lambda e: e.memset(ROWSB[0:16, 0:2048], -CPEN),
                 waits=[("syq", p.n("syq")), ("v", vq2)], inc="v")
            row_dma(qpp, kr, 65, ROWSB[0:16, 0:2048], [("v", p.n("v"))])
            p.op("v", lambda e: e.tensor_mul(ROWSB[0:16, 2048:4096], ROWS[0:16, 2048:4096],
                                             ROWS[0:16, 2048:4096]),
                 waits=[("syq", p.n("syq"))], inc="v")
            row_dma(qpp, kr, 66, ROWSB[0:16, 2048:4096], [("v", p.n("v"))])
            p.op("v", lambda e: e.tensor_copy(ROWSB[0:16, 0:2048], ROWS[0:16, 2048:4096]),
                 waits=[("syq", p.n("syq"))], inc="v")
            row_dma(qpp, kr, 67, ROWSB[0:16, 0:2048], [("v", p.n("v"))])
        if sub == "B4":
            return

        wvfull = AR[:, 0:8192]
        ar_guard = [("t", p.n("t")), ("v", p.n("v")), ("s", p.n("s"))]
        if small_wv:
            for c in range(CH):
                p.dma("g", wvfull[:, 1024 * c: 1024 * (c + 1)],
                      ins[pref + "wv"][128 * c: 128 * (c + 1), :], inc="gq")
        else:
            p.dma("g", wvfull.rearrange("p (a b) -> p a b", a=CH),
                  ins[pref + "wv"].rearrange("(a p) b -> p a b", p=128),
                  waits=ar_guard, inc="gq")
        wvld = p.n("gq")
        prev_t = [p.n("t"), p.n("t")]
        prev_s = [p.n("s"), p.n("s")]
        entry_v = p.n("v")
        prev_v = [entry_v, entry_v]
        prev_dma = [0, 0]
        for tt in range(NT):
            PP = PA if tt % 2 == 0 else PB
            for c in range(CH):
                mm4(PP[:, 0:1024], xnT[c][:, 128 * tt: 128 * (tt + 1)],
                    wvfull[:, 1024 * c: 1024 * (c + 1)], c == 0, c == CH - 1,
                    waits=[("gq", wvld), ("v", max(xn_v, prev_v[tt % 2])), ("t", prev_t[tt % 2]),
                           ("s", prev_s[tt % 2])], width=1024)
            tmm = p.n("t")
            buf = asm[tt % 2]
            if L == 0 and plain_ev:
                p.op("s", lambda e, PP=PP, buf=buf: e.activation(
                    out=buf[:, 0:1024], in_=PP[:, 0:1024], func=AF.Copy),
                    waits=[("t", tmm), ("syq", prev_dma[tt % 2])], inc="s")
                p.op("v", lambda e: e.nop(), inc="v")
            elif L == 0:
                p.op("s", lambda e, PP=PP, buf=buf: e.activation(
                    out=buf[:, 0:1040].rearrange("p (a b) -> p a b", b=65)[:, :, 0:64],
                    in_=PP[:, 0:1024].rearrange("p (a b) -> p a b", b=64), func=AF.Copy),
                    waits=[("t", tmm), ("syq", prev_dma[tt % 2])], inc="s")
                p.op("v", lambda e, buf=buf: e.memset(
                    buf[:, 0:1040].rearrange("p (a b) -> p a b", b=65)[:, :, 64:65], 1.0),
                    waits=[("s", p.n("s")), ("syq", prev_dma[tt % 2])], inc="v")
            else:
                p.op("s", lambda e, PP=PP, buf=buf: e.activation(
                    out=buf[:, 0:1024], in_=PP[:, 0:1024], func=AF.Copy),
                    waits=[("t", tmm), ("syq", prev_dma[tt % 2])], inc="s")
                p.op("v", lambda e: e.nop(), inc="v")
            p.dma("sy", vp[128 * tt: 128 * (tt + 1), :], buf[:, 0:vw],
                  waits=[("s", p.n("s")), ("v", p.n("v"))], inc="syq")
            prev_dma[tt % 2] = p.n("syq")
            prev_t[tt % 2] = p.n("t")
            prev_s[tt % 2] = p.n("s")
            prev_v[tt % 2] = p.n("v")
        if sub in ("B5", "B5a", "B5b"):
            return

        agw = [("syq", p.n("syq")), ("gq", p.n("gq")), ("v", p.n("v")), ("s", p.n("s"))]
        p.op("g", lambda e: e.collective_compute(
            "AllGather", ALU.bypass, replica_groups=[[0, 1], [2, 3], [4, 5], [6, 7]],
            ins=[kpp.ap().opt()], outs=[hbm[f"kppf{L}"].ap().opt()]),
            waits=agw, inc=f"cck{L}")
        p.op("g", lambda e: e.collective_compute(
            "AllGather", ALU.bypass, replica_groups=[[0, 1], [2, 3], [4, 5], [6, 7]],
            ins=[vp.ap().opt()], outs=[hbm[f"vpf{L}"].ap().opt()]),
            waits=agw, inc=f"ccv{L}")

    R1f = R1.bitcast(F32)

    def l1_head_post(h):
        SUM0 = ROWS[0:1, 0:2048]
        SUM1 = ROWS[0:1, 2048:4096]
        FMU = R1f[0:1, 4736:6784]
        F0b = R1[0:1, 5120:7168]
        F1b = R1[0:1, 13568:15616]
        H0 = RB32[0:1, 0:1024]
        H1 = RB32[0:1, 1024:2048]
        H2 = R1f[0:1, 512:1536]
        H3 = R1f[0:1, 1536:2560]
        H4 = R3[0:1, 3072:4096]
        H5 = R1f[0:1, 2560:3584]
        H6 = R1f[0:1, 6784:7808]
        stg32 = ROWS[32:33, 0:2048]
        stg64 = ROWS[64:65, 0:2048]
        osh = p.n("odma")
        p.op("v", lambda e: e.tensor_mul(st_sc[0][:, :], o01[0][:, :], o01[0][:, :]),
             waits=[("odma", osh)], inc="v")
        p.op("v", lambda e: e.tensor_mul(st_sc[1][:, :], o01[1][:, :], o01[1][:, :]), inc="v")
        p.op("v", lambda e: e.tensor_mul(st_sc[2][:, :], o01[0][:, :], o01[1][:, :]), inc="v")
        vp_ = p.n("v")
        mm4(PA[0:1, :], ones1[:], o01[0][:, :], True, True,
            waits=[("v", vp_), ("s", p.n("s"))])
        mm4(PA[32:33, :], ones1[:], o01[1][:, :], True, True)
        mm4(PA[64:65, :], ones1[:], st_sc[0][:, :], True, True)
        mm4(PB[0:1, :], ones1[:], st_sc[1][:, :], True, True)
        mm4(PB[32:33, :], ones1[:], st_sc[2][:, :], True, True)
        tst = p.n("t")
        p.op("v", lambda e: e.tensor_copy(stg32, PA[32:33, :]), waits=[("t", tst)], inc="v")
        p.op("v", lambda e: e.tensor_copy(stg64, PA[64:65, :]), inc="v")
        p.op("v", lambda e: e.tensor_copy(ROWS[32:33, 2048:4096], PB[32:33, :]), inc="v")
        for qh in range(2):
            qsl = slice(1024 * qh, 1024 * (qh + 1))
            s0h = SUM0[:, qsl]
            s1h = SUM1[:, qsl]
            p.op("v", lambda e, s0h=s0h: e.reciprocal(H5, s0h), inc="v")
            p.op("v", lambda e, s1h=s1h: e.reciprocal(H6, s1h), inc="v")
            p.op("v", lambda e, qsl=qsl: e.tensor_copy(H0, PA[0:1, qsl]),
                 waits=[("t", tst)], inc="v")
            p.dma("g", H1, stg32[:, qsl], waits=[("v", p.n("v"))], inc="odma")
            p.dma("g", H2, stg64[:, qsl], inc="odma")
            p.op("v", lambda e, qsl=qsl: e.tensor_copy(H3, PB[0:1, qsl]), inc="v")
            od = p.n("odma")
            p.op("v", lambda e: e.tensor_mul(H0, H5, H0), waits=[("odma", od)], inc="v")
            p.op("v", lambda e: e.tensor_mul(H1, H6, H1), inc="v")
            p.op("v", lambda e: e.tensor_scalar(out=H1, in0=H1, scalar1=lamv[0:1, 0:1],
                                                scalar2=None, op0=ALU.mult), inc="v")
            p.op("v", lambda e: e.tensor_sub(H0, H0, H1), inc="v")
            p.op("v", lambda e: e.tensor_scalar(out=H0, in0=H0, scalar1=1.0 / 128,
                                                scalar2=None, op0=ALU.mult), inc="v")
            p.op("v", lambda e: e.tensor_mul(H1, H5, H5), inc="v")
            p.op("v", lambda e: e.tensor_mul(H1, H1, H2), inc="v")
            p.op("v", lambda e: e.tensor_mul(H2, H6, H6), inc="v")
            p.op("v", lambda e: e.tensor_mul(H2, H2, H3), inc="v")
            p.op("v", lambda e: e.tensor_scalar(out=H2, in0=H2, scalar1=lamv[0:1, 0:1],
                                                scalar2=None, op0=ALU.mult), inc="v")
            p.op("v", lambda e: e.tensor_scalar(out=H2, in0=H2, scalar1=lamv[0:1, 0:1],
                                                scalar2=None, op0=ALU.mult), inc="v")
            p.dma("g", H3, ROWS[32:33, 2048 + 1024 * qh: 2048 + 1024 * (qh + 1)],
                  waits=[("v", p.n("v"))], inc="odma")
            p.op("v", lambda e: e.tensor_mul(H4, H5, H6),
                 waits=[("odma", p.n("odma"))], inc="v")
            p.op("v", lambda e: e.tensor_mul(H4, H4, H3), inc="v")
            p.op("v", lambda e: e.tensor_scalar(out=H4, in0=H4, scalar1=lamv[0:1, 0:1],
                                                scalar2=None, op0=ALU.mult), inc="v")
            p.op("v", lambda e: e.tensor_scalar(out=H4, in0=H4, scalar1=2.0,
                                                scalar2=None, op0=ALU.mult), inc="v")
            p.op("v", lambda e: e.tensor_add(H1, H1, H2), inc="v")
            p.op("v", lambda e: e.tensor_sub(H1, H1, H4), inc="v")
            p.op("v", lambda e: e.tensor_scalar(out=H1, in0=H1, scalar1=1.0 / 128,
                                                scalar2=None, op0=ALU.mult), inc="v")
            p.op("v", lambda e: e.tensor_mul(H2, H0, H0), inc="v")
            p.op("v", lambda e: e.tensor_sub(H1, H1, H2), inc="v")
            p.op("s", lambda e: e.activation(out=H2, in_=H1, func=AF.Sqrt, bias=epsb[0:1, :]),
                 waits=[("v", p.n("v"))], inc="s")
            p.op("v", lambda e: e.reciprocal(H1, H2), waits=[("s", p.n("s"))], inc="v")
            p.op("v", lambda e, s0h=s0h: e.tensor_mul(s0h, H5, H1), inc="v")
            p.op("v", lambda e, s1h=s1h: e.tensor_mul(s1h, H6, H1), inc="v")
            p.op("v", lambda e, s1h=s1h: e.tensor_scalar(out=s1h, in0=s1h,
                                                         scalar1=lamv[0:1, 0:1],
                                                         scalar2=None, op0=ALU.mult), inc="v")
            p.op("v", lambda e: e.tensor_mul(H0, H0, H1), inc="v")
            p.op("v", lambda e, qsl=qsl: e.tensor_copy(FMU[:, qsl], H0),
                 waits=[("syq", p.n("syq"))], inc="v")
        p.dma("sy", hbm["m8"][h: h + 1, :], FMU, waits=[("v", p.n("v"))], inc="syq")
        p.op("v", lambda e: e.tensor_copy(F0b, SUM0), inc="v")
        vF = p.n("v")
        mm4(PA, onesr[0:1, :], F0b, True, True, waits=[("v", vF)])
        p.op("v", lambda e: e.tensor_copy(asm[0][:, :], PA[:, :]), waits=[("t", p.n("t"))], inc="v")
        p.op("v", lambda e: e.tensor_copy(F1b, SUM1), inc="v")
        vF1 = p.n("v")
        mm4(PA, onesr[0:1, :], F1b, True, True, waits=[("v", vF1)])
        p.op("v", lambda e: e.tensor_copy(asm[1][:, :], PA[:, :]), waits=[("t", p.n("t"))], inc="v")
        p.op("v", lambda e: e.tensor_mul(st_sc[0][:, :], o01[0][:, :], asm[0][:, :]), inc="v")
        p.op("v", lambda e: e.tensor_mul(st_sc[1][:, :], o01[1][:, :], asm[1][:, :]), inc="v")
        p.op("v", lambda e, h=h: e.tensor_sub(oTs[h][:, :], st_sc[0][:, :], st_sc[1][:, :]), inc="v")

    def phase_C(L, c0=False, upto_c1=False):
        kr, vw = KR[L], VW[L]
        nh = 16 if L == 0 else 8
        njl = 1 if L == 0 else 2
        kppf, vpf, qpp = hbm[f"kppf{L}"], hbm[f"vpf{L}"], hbm[f"qpp{L}"]
        scores = [PA[:, 0:1024], PA[:, 1024:2048]]
        misc = PB[:, 1024:2048]
        opsum = PB[0:65, 0:1024] if L == 0 else PB[0:128, 0:1024]

        kst_used = [0, 0]
        qst_used = [0, 0]
        vst_used = [0, 0]
        kn = [p.n("kl0"), p.n("kl1")]
        qn = [p.n("ql0"), p.n("ql1")]
        vn = [p.n("vl0"), p.n("vl1")]
        p.counts.setdefault("kl0", 0); p.counts.setdefault("kl1", 0)
        p.counts.setdefault("ql0", 0); p.counts.setdefault("ql1", 0)
        p.counts.setdefault("vl0", 0); p.counts.setdefault("vl1", 0)
        sc_free = [0, 0]
        exp_free = [0, 0]
        exp_done = [0, 0]
        op_free = p.n("v")
        misc_free = p.n("v")
        pa_free = p.n("s")
        state = {"qtog": 0}

        def load_kst(h, b):
            for chunk in range(2):
                p.dma("sy", kst[b][0:kr, 2048 * chunk: 2048 * (chunk + 1)],
                      kppf[chunk * 16 * kr + kr * h: chunk * 16 * kr + kr * h + kr, :],
                      waits=[("t", kst_used[b]), (f"cck{L}", 1)], inc=f"kl{b}")
            kn[b] += 32

        def load_qst(hh, qh, b):
            p.dma("sy", qst[b][0:kr, :], qpp[kr * hh: kr * hh + kr, 1024 * qh: 1024 * (qh + 1)],
                  waits=[("t", qst_used[b])], inc=f"ql{b}")
            qn[b] += 16

        def load_vst(h, kq, b):
            csl = slice(0, 1040) if L == 0 else slice(128 * h, 128 * (h + 1))
            p.dma("sy", vst(L, b)[:, :, :],
                  vpf.ap().rearrange("(a p) b -> p a b", p=128)[:, 8 * kq: 8 * kq + 8, csl],
                  waits=[("t", vst_used[b]), (f"ccv{L}", 1)], inc=f"vl{b}")
            vn[b] += 16

        if L == 0:
            load_kst(0, 0)
        for h in range(nh):
            if L == 0:
                kb = h % 2
                if h + 1 < nh:
                    load_kst(h + 1, (h + 1) % 2)
            else:
                load_kst(2 * h, 0)
                load_kst(2 * h + 1, 1)
            for qh in range(2):
                for j in range(njl):
                    hh = h if L == 0 else 2 * h + j
                    if L == 1:
                        kb = j
                    qb = state["qtog"]
                    state["qtog"] ^= 1
                    load_qst(hh, qh, qb)
                    myq = qn[qb]
                    for kq in range(4):
                        vb = kq % 2
                        load_vst(h, kq, vb)
                        myv = vn[vb]
                        for ki in range(8):
                            k = 8 * kq + ki
                            sp = k % 2
                            w = [("s", max(sc_free[sp], pa_free)),
                                 (f"kl{kb}", kn[kb]), (f"ql{qb}", myq)]
                            for i in range(2):
                                sl = slice(512 * i, 512 * (i + 1))
                                p.op("t", lambda e, sp=sp, sl=sl, k=k, kb=kb, qb=qb: e.matmul(
                                    scores[sp][:, sl],
                                    kst[kb][0:kr, 128 * k: 128 * (k + 1)],
                                    qst[qb][0:kr, sl], start=True, stop=True),
                                    waits=(w if i == 0 else ()), inc="t")
                            kst_used[kb] = qst_used[qb] = tqk = p.n("t")
                            p.op("s", lambda e, sp=sp: e.activation(
                                out=expb[sp], in_=scores[sp][:, :], func=AF.Exp, scale=0.125),
                                waits=[("t", max(tqk, exp_free[sp]))], inc="s")
                            exp_done[sp] = sc_free[sp] = p.n("s")
                            w2 = [("s", exp_done[sp]), (f"vl{vb}", myv)]
                            if k == 0:
                                w2.append(("v", op_free))
                                w2.append(("s", pa_free))
                            dv = 65 if L == 0 else 128
                            voff = 65 * h if L == 0 else 0
                            for i in range(2):
                                sl = slice(512 * i, 512 * (i + 1))
                                p.op("t", lambda e, vb=vb, ki=ki, sl=sl, k=k, sp=sp,
                                     dv=dv, voff=voff: e.matmul(
                                         opsum[:, sl],
                                         vst(L, vb)[:, ki: ki + 1, voff: voff + dv].rearrange(
                                             "p a b -> p (a b)"),
                                         expb[sp][:, sl], start=(k == 0), stop=(k == 31)),
                                     waits=(w2 if i == 0 else ()), inc="t")
                            vst_used[vb] = p.n("t")
                            if L == 1:
                                w3 = [("v", misc_free)] if k == 0 else ()
                                for i in range(2):
                                    sl = slice(512 * i, 512 * (i + 1))
                                    p.op("t", lambda e, j=j, sl=sl, k=k, sp=sp: e.matmul(
                                        misc[64 * j: 64 * j + 1, sl], ones1[:],
                                        expb[sp][:, sl], start=(k == 0), stop=(k == 31)),
                                        waits=(w3 if i == 0 else ()), inc="t")
                            exp_free[sp] = p.n("t")
                    tav = p.n("t")
                    if L == 0:
                        rsl = ROWS[64:65, 1024 * qh: 1024 * (qh + 1)]
                        rbl = ROWSB[64:65, 1024 * qh: 1024 * (qh + 1)]
                        p.op("v", lambda e, rsl=rsl: e.reciprocal(rsl, opsum[64:65, :]),
                             waits=[("t", tav)], inc="v")
                        p.op("v", lambda e, rsl=rsl, rbl=rbl: e.tensor_copy(rbl, rsl), inc="v")
                        vr = p.n("v")
                        for i in range(2):
                            sl = slice(512 * i, 512 * (i + 1))
                            p.op("t", lambda e, rbl=rbl, sl=sl: e.matmul(
                                misc[:, sl], onesr[64:65, :], rbl[:, sl], start=True, stop=True),
                                waits=([("v", max(vr, misc_free))] if i == 0 else ()), inc="t")
                        p.op("v", lambda e, qh=qh: e.tensor_copy(rbcs[qh][:, :], misc[:, :]),
                             waits=[("t", p.n("t"))], inc="v")
                        misc_free = p.n("v")
                        p.op("v", lambda e, qh=qh: e.tensor_mul(
                            l0tsl[0:64, 0:1024], opsum[0:64, :], rbcs[qh][0:64, :]), inc="v")
                        op_free = p.n("v")
                        if h % 2 == 0:
                            p.op("v", lambda e, h=h, qh=qh: e.tensor_scalar(
                                out=oTs[h // 2][0:64, 1024 * qh: 1024 * (qh + 1)],
                                in0=l0tsl[0:64, 0:1024],
                                scalar1=biases["l0_bv"][0:64, h // 2: h // 2 + 1],
                                scalar2=None, op0=ALU.add), inc="v")
                        else:
                            p.dma("g", oTs[h // 2][64:128, 1024 * qh: 1024 * (qh + 1)],
                                  l0tsl[0:64, 0:1024], waits=[("v", p.n("v"))], inc="odma")
                            p.op("v", lambda e, h=h, qh=qh: e.tensor_scalar(
                                out=oTs[h // 2][64:128, 1024 * qh: 1024 * (qh + 1)],
                                in0=oTs[h // 2][64:128, 1024 * qh: 1024 * (qh + 1)],
                                scalar1=biases["l0_bv"][64:128, h // 2: h // 2 + 1],
                                scalar2=None, op0=ALU.add),
                                waits=[("odma", p.n("odma"))], inc="v")
                    else:
                        p.op("v", lambda e, j=j, qh=qh: e.tensor_copy(
                            o01[j][:, 1024 * qh: 1024 * (qh + 1)], opsum[:, :]),
                            waits=[("t", tav)], inc="v")
                        op_free = p.n("v")
                        if j == 0:
                            p.op("v", lambda e, qh=qh: e.tensor_copy(
                                ROWS[0:1, 1024 * qh: 1024 * (qh + 1)], misc[0:1, :]), inc="v")
                        else:
                            p.op("v", lambda e, qh=qh: e.tensor_copy(
                                ROWS[64:65, 1024 * qh: 1024 * (qh + 1)], misc[64:65, :]), inc="v")
                            p.dma("g", ROWS[0:1, 2048 + 1024 * qh: 2048 + 1024 * (qh + 1)],
                                  ROWS[64:65, 1024 * qh: 1024 * (qh + 1)],
                                  waits=[("v", p.n("v"))], inc="odma")
                        misc_free = p.n("v")
            if upto_c1 and L == 1:
                l1_head_post(h)
                d_rows1 = nc.dram_tensor("d_rows1", [128, 4096], F32, kind="ExternalOutput").ap()
                d_f0bc = nc.dram_tensor("d_f0bc", [128, T], BF16, kind="ExternalOutput").ap()
                d_f1bc = nc.dram_tensor("d_f1bc", [128, T], BF16, kind="ExternalOutput").ap()
                d_oc = nc.dram_tensor("d_oc", [128, T], BF16, kind="ExternalOutput").ap()
                d_r1 = nc.dram_tensor("d_r1", [128, 16896], BF16, kind="ExternalOutput").ap()
                allw = [("t", p.n("t")), ("s", p.n("s")), ("v", p.n("v")),
                        ("syq", p.n("syq")), ("gq", p.n("gq")), ("odma", p.n("odma"))]
                p.dma("sy", d_rows1, ROWS[:], waits=allw, inc="ost")
                p.dma("sy", d_f0bc, asm[0][:], waits=allw, inc="ost")
                p.dma("sy", d_f1bc, asm[1][:], waits=allw, inc="ost")
                p.dma("sy", d_oc, oTs[0][:], waits=allw, inc="ost")
                p.dma("sy", d_r1, R1[:], waits=allw, inc="ost")
                return
            if c0:
                d_o0 = nc.dram_tensor("d_o0", [128, T], BF16, kind="ExternalOutput").ap()
                d_o1 = nc.dram_tensor("d_o1", [128, T], BF16, kind="ExternalOutput").ap()
                d_rows = nc.dram_tensor("d_rows", [128, 4096], F32, kind="ExternalOutput").ap()
                allw = [("t", p.n("t")), ("s", p.n("s")), ("v", p.n("v")),
                        ("syq", p.n("syq")), ("gq", p.n("gq")), ("odma", p.n("odma"))]
                p.dma("sy", d_o0, o01[0], waits=allw, inc="ost")
                p.dma("sy", d_o1, o01[1], waits=allw, inc="ost")
                p.dma("sy", d_rows, ROWS[:], waits=allw, inc="ost")
                return
            if L == 1:
                l1_head_post(h)
                op_free = p.n("v")
                misc_free = p.n("v")
                pa_free = 0
                sc_free = [0, 0]
                vpost = p.n("v")
                sc_free = [0, 0]
                self_gate = vpost
                p.op("s", lambda e: e.nop(), waits=[("v", vpost)], inc="s")
                sc_free = [p.n("s"), p.n("s")]

    def phase_D(L):
        pref = "l0_" if L == 0 else "l1_"
        oc_v = p.n("v")
        vprep = p.n("v")
        if L == 1:
            prevg = p.n("t")
            dprep_s = p.n("s")
            for hh in range(8):
                p.dma("g", ws[hh % 2][:].rearrange("p (a b) -> p a b", a=CH),
                      ins["l1_wo"][128 * hh: 128 * (hh + 1), :].rearrange("p (a b) -> p a b", a=CH),
                      waits=[("t", prevg)], inc="gq")
                ld = p.n("gq")
                for i in range(2):
                    sl = slice(512 * i, 512 * (i + 1))
                    p.op("t", lambda e, hh=hh, sl=sl: e.matmul(
                        PA[0:8, sl], lngW[:, 8 * hh: 8 * hh + 8],
                        ws[hh % 2][:, sl], start=(hh == 0), stop=(hh == 7)),
                        waits=[("gq", ld), ("v", oc_v), ("s", dprep_s)], inc="t")
                for i in range(2):
                    sl = slice(512 * i, 512 * (i + 1))
                    p.op("t", lambda e, hh=hh, sl=sl: e.matmul(
                        PB[0:1, sl], lnbb[:], ws[hh % 2][:, sl],
                        start=(hh == 0), stop=(hh == 7)), inc="t")
                prevg = p.n("t")
            p.op("v", lambda e: e.tensor_copy(ROWSB[0:8, 0:1024], PA[0:8, 0:1024]),
                 waits=[("t", p.n("t"))], inc="v")
            p.op("v", lambda e: e.tensor_copy(ROWS[0:1, 0:1024], PB[0:1, 0:1024]), inc="v")
            p.op("v", lambda e: e.tensor_copy(ROWSB[0:1, 1024:2048], ROWS[0:1, 0:1024]), inc="v")
            vc1 = p.n("v")
            for dc in range(CH):
                p.op("t", lambda e, dc=dc: e.transpose(
                    PBT128, ROWSB[0:1, 1024 + 128 * dc: 1024 + 128 * (dc + 1)], ident[0:1, 0:1]),
                    waits=[("v", vc1), ("v", p.n("v"))], inc="t")
                p.op("v", lambda e, dc=dc: e.tensor_copy(c1sb[:, dc: dc + 1], PBT128),
                     waits=[("t", p.n("t"))], inc="v")
            p.dma("sy", ROWS[0:8, 0:2048], hbm["m8"][:, :],
                  waits=[("syq", p.n("syq")), ("v", p.n("v")), ("s", p.n("s")),
                         ("t", p.n("t"))], inc="syq")
            p.op("v", lambda e: e.tensor_scalar(out=ROWSB[0:8, 2048:4096], in0=ROWS[0:8, 0:2048],
                                                scalar1=-1.0, scalar2=None, op0=ALU.mult),
                 waits=[("syq", p.n("syq"))], inc="v")
            vprep = p.n("v")
            if os.environ.get("DUMPD") == "1":
                d_rb = nc.dram_tensor("d_rb", [128, 4096], BF16, kind="ExternalOutput").ap()
                d_rw = nc.dram_tensor("d_rw", [128, 4096], F32, kind="ExternalOutput").ap()
                p.dma("sy", d_rb, ROWSB[:], waits=[("v", vprep), ("t", p.n("t")), ("s", p.n("s"))], inc="dbgst")
                p.dma("sy", d_rw, ROWS[:], waits=[("v", vprep), ("t", p.n("t")), ("s", p.n("s"))], inc="dbgst")
                p.op("sy", lambda e: e.nop(), waits=[("dbgst", p.n("dbgst"))], inc="sy2")

        prev_mm = [p.n("t"), p.n("t")]
        prev_ev = [p.n("s"), p.n("s")]
        for dc in range(CH):
            p.dma("g", ws[dc % 2][:].rearrange("p (a b) -> p a b", a=CH),
                  ins[pref + "wo"].rearrange("(a p) b -> p a b", p=128)[:, :, 128 * dc: 128 * (dc + 1)],
                  waits=[("t", prev_mm[dc % 2])], inc="gq")
            ld = p.n("gq")
            if L == 1:
                p.op("v", lambda e, dc=dc: e.tensor_scalar(
                    out=ws[dc % 2][:], in0=ws[dc % 2][:], scalar1=biases["l1_lng"][:],
                    scalar2=None, op0=ALU.mult), waits=[("gq", ld), ("v", vprep)], inc="v")
                wsr = [("v", p.n("v")), ("s", prev_ev[dc % 2])]
            else:
                wsr = [("gq", ld), ("s", prev_ev[dc % 2])]
            PP = PA if dc % 2 == 0 else PB
            for oc in range(CH):
                last = (oc == CH - 1) and (L == 0 or os.environ.get("DRANK8") == "0")
                mm4(PP, ws[dc % 2][:, 128 * oc: 128 * (oc + 1)], oTs[oc][:, :],
                    oc == 0, last, waits=wsr + [("v", oc_v)])
            if L == 1 and os.environ.get("DRANK8") != "0":
                mm4(PP, ROWSB[0:8, 128 * dc: 128 * (dc + 1)],
                    ROWSB[0:8, 2048:4096], False, True, waits=[("v", vprep)])
            prev_mm[dc % 2] = p.n("t")
            bb = biases["l0_bo"][:, dc: dc + 1] if L == 0 else c1sb[:, dc: dc + 1]
            p.op("v", lambda e, dc=dc, PP=PP, bb=bb: e.tensor_scalar(
                out=sT[dc], in0=PP[:, :], scalar1=bb, scalar2=None, op0=ALU.add),
                waits=[("t", prev_mm[dc % 2])], inc="v")
            p.op("s", lambda e: e.nop(), waits=[("v", p.n("v"))], inc="s")
            prev_ev[dc % 2] = p.n("s")
        return p.n("s"), p.n("t")

    def phase_E0(sD, tD, first_out):
        for c in range(CH):
            p.dma("g", R3[:, 0:1024], ins["l0_fw"][128 * c: 128 * (c + 1), :],
                  waits=[("v", p.n("v")), ("s", sD)], inc="gq")
            p.op("v", lambda e, c=c: e.reduce_max(
                wscale[:, c: c + 1].rearrange("p (a b) -> p a b", b=1),
                R3[:, 0:1024].rearrange("p (a b) -> p a b", b=1024),
                axis=AX.X, apply_absolute_value=True),
                waits=[("gq", p.n("gq"))], inc="v")
            p.op("v", lambda e, c=c: e.tensor_scalar(
                out=wscale[:, c: c + 1], in0=wscale[:, c: c + 1], scalar1=1e-8,
                scalar2=None, op0=ALU.max), inc="v")
            p.op("v", lambda e, c=c: e.reciprocal(wrecip[:, c: c + 1], wscale[:, c: c + 1]), inc="v")
            p.op("v", lambda e, c=c: e.tensor_scalar(
                out=wrecip[:, c: c + 1], in0=wrecip[:, c: c + 1], scalar1=127.0,
                scalar2=None, op0=ALU.mult), inc="v")
            p.op("v", lambda e, c=c: e.tensor_scalar(
                out=wscale[:, c: c + 1], in0=wscale[:, c: c + 1], scalar1=1.0 / 127.0,
                scalar2=None, op0=ALU.mult), inc="v")
        vq = p.n("v")
        for c in range(CH):
            p.op("t", lambda e, c=c: e.transpose(PBF1, wrecip[:, c: c + 1], identf[:]),
                 waits=[("v", vq), ("v", p.n("v")), ("t", tD)], inc="t")
            p.op("v", lambda e, c=c: e.tensor_copy(
                ROWS[0:1, 128 * c: 128 * (c + 1)], PBF1),
                waits=[("t", p.n("t"))], inc="v")
        vrow = p.n("v")
        for i in range(2):
            sl = slice(512 * i, 512 * (i + 1))
            p.op("t", lambda e, sl=sl: e.matmul(
                PB[:, sl], onesrF[0:1, :], ROWS[0:1, 0:1024][:, sl], start=True, stop=True),
                waits=([("v", vrow)] if i == 0 else ()), inc="t")
        p.op("v", lambda e: e.tensor_copy(R3[:, 1024:2048], PB[:, 0:1024]),
             waits=[("t", p.n("t"))], inc="v")
        vrw = p.n("v")
        for c in range(CH):
            p.dma("g", R3[:, 0:1024], ins["l0_fwt"][128 * c: 128 * (c + 1), :],
                  waits=[("v", p.n("v")), ("v", vrw)], inc="gq")
            p.op("v", lambda e: e.tensor_mul(R3[:, 2048:3072], R3[:, 0:1024], R3[:, 1024:2048]),
                 waits=[("gq", p.n("gq"))], inc="v")
            p.op("v", lambda e, c=c: e.tensor_scalar(
                out=AR[:, 1024 * c: 1024 * (c + 1)], in0=R3[:, 2048:3072],
                scalar1=MAGIC, scalar2=MAGIC, op0=ALU.add, op1=ALU.subtract), inc="v")
        vwq = p.n("v")

        tmaj = [AR[:, 8192:9216], AR[:, 9216:10240]]
        prev_red = [0, 0]
        for tt in range(NT):
            b = tt % 2
            p.op("sy", lambda e, b=b, tt=tt: e.dma_start_transpose(
                tmaj[b], hbm["xnb"][:, 128 * tt: 128 * (tt + 1)]),
                waits=[("v", prev_red[b]), ("odma", p.n("odma")), ("syq", setup_syq)],
                inc=f"tm{b}", inck=16)
            p.op("v", lambda e, b=b, tt=tt: e.reduce_max(
                IDX[:, tt: tt + 1].rearrange("p (a b) -> p a b", b=1),
                tmaj[b].rearrange("p (a b) -> p a b", b=1024),
                axis=AX.X, apply_absolute_value=True),
                waits=[(f"tm{b}", 16 * (tt // 2 + 1))], inc="v")
            prev_red[b] = p.n("v")
        p.op("v", lambda e: e.tensor_scalar(out=IDX[:, 0:16], in0=IDX[:, 0:16],
                                            scalar1=1e-8, scalar2=None, op0=ALU.max), inc="v")
        vam = p.n("v")
        for tt in range(NT):
            p.op("t", lambda e, tt=tt: e.transpose(PBF1, IDX[:, tt: tt + 1], identf[:]),
                 waits=[("v", vam), ("v", p.n("v"))], inc="t")
            p.op("v", lambda e, tt=tt: e.tensor_copy(ROWS[0:1, 128 * tt: 128 * (tt + 1)], PBF1),
                 waits=[("t", p.n("t"))], inc="v")
        p.op("v", lambda e: e.tensor_scalar(out=ROWSB[0:1, 0:2048], in0=ROWS[0:1, 0:2048],
                                            scalar1=1.0 / 127.0, scalar2=None, op0=ALU.mult), inc="v")
        p.op("v", lambda e: e.reciprocal(ROWS[0:1, 2048:4096], ROWS[0:1, 0:2048]), inc="v")
        p.op("v", lambda e: e.tensor_scalar(out=ROWS[0:1, 2048:4096], in0=ROWS[0:1, 2048:4096],
                                            scalar1=127.0, scalar2=None, op0=ALU.mult), inc="v")
        vxr = p.n("v")
        mm4(PA, onesrF[0:1, :], ROWS[0:1, 2048:4096], True, True, waits=[("v", vxr)])
        p.op("v", lambda e: e.tensor_copy(R3[:, 0:2048], PA[:, :]), waits=[("t", p.n("t"))], inc="v")
        mm4(PB, onesr[0:1, :], ROWSB[0:1, 0:2048], True, True, waits=[("v", vxr)])
        p.op("v", lambda e: e.tensor_copy(asm[0][:, :], PB[:, :]), waits=[("t", p.n("t"))], inc="v")
        vbc = p.n("v")
        for c in range(CH):
            p.dma("sy", oTs[c][:], hbm["xnb"][128 * c: 128 * (c + 1), :],
                  waits=[("t", tD)], inc="syq")
        ldx = p.n("syq")
        for c in range(CH):
            p.op("v", lambda e, c=c: e.tensor_mul(R3[:, 2048:4096], oTs[c][:], R3[:, 0:2048]),
                 waits=[("syq", ldx), ("v", vbc)], inc="v")
            p.op("v", lambda e, c=c: e.tensor_scalar(
                out=oTs[c][:], in0=R3[:, 2048:4096], scalar1=MAGIC, scalar2=MAGIC,
                op0=ALU.add, op1=ALU.subtract), inc="v")
        vxq = p.n("v")

        prev_mm = [0, 0]
        prev_ev = [0, 0]
        for oc in range(CH):
            PP = PA if oc % 2 == 0 else PB
            for c in range(CH):
                mm4(PP, AR[:, 1024 * c + 128 * oc: 1024 * c + 128 * (oc + 1)],
                    oTs[c][:, :], c == 0, c == CH - 1,
                    waits=[("v", max(vxq, vwq)), ("t", prev_mm[oc % 2]),
                           ("v", prev_ev[oc % 2])])
            prev_mm[oc % 2] = p.n("t")
            p.op("v", lambda e, oc=oc, PP=PP: e.scalar_tensor_tensor(
                out=asm[1][:, :], in0=PP[:, :], scalar=wscale[:, oc: oc + 1],
                in1=asm[0][:, :], op0=ALU.mult, op1=ALU.mult),
                waits=[("t", prev_mm[oc % 2]), ("s", p.n("s"))], inc="v")
            prev_ev[oc % 2] = p.n("v")
            p.op("s", lambda e, oc=oc: e.activation(
                out=AR[:, 8192:10240], in_=asm[1][:, :], func=AF.Silu,
                bias=biases["l0_fb"][:, oc: oc + 1]), waits=[("v", p.n("v"))], inc="s")
            sg = p.n("s")
            if first_out:
                fout = R3[:, 0:2048] if oc % 2 == 0 else ROWS[:, 0:2048]
                p.op("v", lambda e, oc=oc, fout=fout: e.tensor_mul(fout, sT[oc], AR[:, 8192:10240]),
                     waits=[("s", sg), ("ost", max(0, p.n("ost") - 16))], inc="v")
                p.op("v", lambda e, oc=oc, fout=fout: e.tensor_add(fout, fout, xT[oc][:]),
                     inc="v")
                p.dma("sy", outT[128 * oc: 128 * (oc + 1), :], fout,
                      waits=[("v", p.n("v"))], inc="ost")
                p.op("sy", lambda e: e.nop(), waits=[("ost", p.n("ost"))], inc="sy2")
            else:
                p.op("v", lambda e, oc=oc: e.tensor_mul(R3b[:, 4096:6144], sT[oc], AR[:, 8192:10240]),
                     waits=[("s", sg)], inc="v")
                p.op("v", lambda e, oc=oc: e.tensor_add(xT[oc][:], xT[oc][:], R3b[:, 4096:6144]),
                     inc="v")

    def phase_E1(sD, tD, dump_dacc=False):
        if dump_dacc:
            d_st3 = nc.dram_tensor("d_st3", [D, T], BF16, kind="ExternalOutput").ap()
            for c in range(CH):
                p.dma("sy", d_st3[128 * c: 128 * (c + 1), :], sT[c],
                      waits=[("s", sD), ("t", tD)], inc="dbgst")
            p.op("sy", lambda e: e.nop(), waits=[("dbgst", p.n("dbgst"))], inc="sy2")
        wd2 = [AR[:, 0:3072], AR[:, 3072:6144]]
        ws2 = [AR[:, 6144:7168], AR[:, 7168:8192]]
        gsil = AR[:, 8192:8704]
        for c in range(CH):
            p.dma("sy", oTs[c][:], hbm["xnb"][128 * c: 128 * (c + 1), :],
                  waits=[("t", tD)], inc="syq")
        ldx = p.n("syq")
        e1_v = p.n("v")
        ev = {}
        for grp in range(4):
            for ig in range(6):
                hc = 6 * grp + ig
                p.dma("g", ws[hc % 2][:].rearrange("p (a b) -> p a b", a=CH),
                      ins["l1_wgt"].rearrange("(a p) b -> p a b", p=128)[:, :, 128 * hc: 128 * (hc + 1)],
                      waits=[("t", ev.get(("mm", hc - 2), tD))], inc="gq")
                p.dma("g", ws2[hc % 2].rearrange("p (a b) -> p a b", a=CH),
                      ins["l1_wut"].rearrange("(a p) b -> p a b", p=128)[:, :, 128 * hc: 128 * (hc + 1)],
                      inc="gq")
                ld = p.n("gq")
                w0 = [("gq", ld), ("syq", ldx), ("s", sD),
                      ("v", max(e1_v, ev.get(("hv", hc - 1), 0))),
                      ("s", ev.get(("hs", hc - 1), 0))]
                for c in range(CH):
                    for qq in range(4):
                        sl = slice(512 * qq, 512 * (qq + 1))
                        p.op("t", lambda e, hc=hc, c=c, sl=sl: e.matmul(
                            PA[:, sl], ws[hc % 2][:, 128 * c: 128 * (c + 1)],
                            oTs[c][:, sl], start=(c == 0), stop=(c == CH - 1)),
                            waits=(w0 if (c == 0 and qq == 0) else ()), inc="t")
                    for qq in range(4):
                        sl = slice(512 * qq, 512 * (qq + 1))
                        p.op("t", lambda e, hc=hc, c=c, sl=sl: e.matmul(
                            PB[:, sl], ws2[hc % 2][:, 128 * c: 128 * (c + 1)],
                            oTs[c][:, sl], start=(c == 0), stop=(c == CH - 1)), inc="t")
                ev[("mm", hc)] = p.n("t")
                for qq in range(4):
                    sl = slice(512 * qq, 512 * (qq + 1))
                    p.op("s", lambda e, sl=sl: e.activation(
                        out=gsil, in_=PA[:, sl], func=AF.Silu, scale=1.0 / 127.0),
                        waits=[("t", ev[("mm", hc)]), ("v", p.n("v"))], inc="s")
                    p.op("v", lambda e, ig=ig, sl=sl: e.scalar_tensor_tensor(
                        out=hsl[ig][:, sl], in0=PB[:, sl], scalar=1.0 / 127.0,
                        in1=gsil, op0=ALU.mult, op1=ALU.mult),
                        waits=[("s", p.n("s"))], inc="v")
                ev[("hv", hc)] = p.n("v")
                ev[("hs", hc)] = p.n("s")
            hv_grp = p.n("v")
            prev_d = [ev[("mm", 6 * grp + 5)], ev[("mm", 6 * grp + 5)]]
            for dc in range(CH):
                p.dma("g", wd2[dc % 2].rearrange("p (a b) -> p a b", a=24),
                      ins["l1_wdt"].rearrange("(a p) b -> p a b", p=128)[:, :, 128 * dc: 128 * (dc + 1)],
                      waits=[("t", prev_d[dc % 2])], inc="gq")
                ldd = p.n("gq")
                PP = PA if dc % 2 == 0 else PB
                for ig in range(6):
                    hc = 6 * grp + ig
                    mm4(PP, wd2[dc % 2][:, 128 * hc: 128 * (hc + 1)],
                        hsl[ig][:, :], ig == 0, ig == 5,
                        waits=[("gq", ldd), ("v", hv_grp), ("v", ev.get(("dacc", dc), 0))])
                prev_d[dc % 2] = p.n("t")
                if grp == 0:
                    p.op("v", lambda e, dc=dc, PP=PP: e.tensor_copy(dacc[dc], PP[:, :]),
                         waits=[("t", prev_d[dc % 2])], inc="v")
                else:
                    p.op("v", lambda e, dc=dc, PP=PP: e.tensor_add(dacc[dc], dacc[dc], PP[:, :]),
                         waits=[("t", prev_d[dc % 2])], inc="v")
                ev[("dacc", dc)] = p.n("v")
            ev[("hv", 6 * grp + 5)] = p.n("v")
        vdacc = p.n("v")
        if dump_dacc:
            d_da = nc.dram_tensor("d_da", [D, T], BF16, kind="ExternalOutput").ap()
            d_st2 = nc.dram_tensor("d_st2", [D, T], BF16, kind="ExternalOutput").ap()
            allw = [("t", p.n("t")), ("s", p.n("s")), ("v", p.n("v")),
                    ("syq", p.n("syq")), ("gq", p.n("gq"))]
            d_ots2 = nc.dram_tensor("d_ots2", [D, T], BF16, kind="ExternalOutput").ap()
            for c in range(CH):
                p.dma("sy", d_da[128 * c: 128 * (c + 1), :], dacc[c], waits=allw, inc="dbgst")
                p.dma("sy", d_st2[128 * c: 128 * (c + 1), :], sT[c], waits=allw, inc="dbgst")
                p.dma("sy", d_ots2[128 * c: 128 * (c + 1), :], oTs[c][:], waits=allw, inc="dbgst")
            p.op("sy", lambda e: e.nop(), waits=[("dbgst", p.n("dbgst"))], inc="sy2")
        ost0 = p.n("ost")
        if dump_dacc:
            d_g1 = nc.dram_tensor("d_g1", [D, T], BF16, kind="ExternalOutput").ap()
        for dc in range(CH):
            fout = R3[:, 2048:4096] if dc % 2 == 0 else ROWS[:, 0:2048]
            p.op("s", lambda e, dc=dc: e.activation(
                out=st_sc[0][:, :], in_=dacc[dc], func=AF.Silu, scale=1.0 / 127.0),
                waits=[("v", vdacc), ("v", p.n("v"))], inc="s")
            sg = p.n("s")
            p.op("v", lambda e, dc=dc, fout=fout: e.tensor_mul(fout, sT[dc], st_sc[0][:, :]),
                 waits=[("s", sg), ("ost", max(ost0, p.n("ost") - 16))], inc="v")
            p.op("v", lambda e, dc=dc, fout=fout: e.tensor_add(fout, fout, xT[dc][:]), inc="v")
            if dump_dacc:
                p.dma("sy", d_g1[128 * dc: 128 * (dc + 1), :], st_sc[0][:, :],
                      waits=[("v", p.n("v"))], inc="ost")
                p.op("sy", lambda e: e.nop(), waits=[("ost", p.n("ost"))], inc="sy2")
            p.dma("sy", outT[128 * dc: 128 * (dc + 1), :], fout,
                  waits=[("v", p.n("v"))], inc="ost")

    done = False
    for li, L in enumerate(layers):
        first = (li == 0)
        p.counts.setdefault("sy2", 0)
        p.counts.setdefault("odma", 0)
        if os.environ.get("DUMPX1") == "1" and li == 1:
            d_x1 = nc.dram_tensor("d_x1", [D, T], BF16, kind="ExternalOutput").ap()
            bw = [("v", p.n("v")), ("s", p.n("s")), ("t", p.n("t"))]
            for c in range(CH):
                p.dma("sy", d_x1[128 * c: 128 * (c + 1), :], xT[c][:], waits=bw, inc="dbgst")
            p.op("sy", lambda e: e.nop(), waits=[("dbgst", p.n("dbgst"))], inc="sy2")
        xn_v = phase_A(L, first)
        if upto == "A":
            done = True
            break
        phase_B(L, xn_v, sub=(upto if upto and upto.startswith("B") else None))
        if dbg:
            kr = KR[L]
            d_xn = nc.dram_tensor("d_xn", [D, T], BF16, kind="ExternalOutput").ap()
            d_idx = nc.dram_tensor("d_idx", [128, 512], F32, kind="ExternalOutput").ap()
            d_qpp = nc.dram_tensor("d_qpp", [16 * kr, T], BF16, kind="ExternalOutput").ap()
            d_kpp = nc.dram_tensor("d_kpp", [16 * kr, T], BF16, kind="ExternalOutput").ap()
            d_vp = nc.dram_tensor("d_vp", [T, VW[L]], BF16, kind="ExternalOutput").ap()
            allw = [("t", p.n("t")), ("s", p.n("s")), ("v", p.n("v")),
                    ("syq", p.n("syq")), ("gq", p.n("gq"))]
            p.dma("sy", d_xn, hbm["xnb"][:, :], waits=allw, inc="ost")
            p.dma("sy", d_idx, IDX[:], waits=allw, inc="ost")
            p.dma("sy", d_qpp, hbm[f"qpp{L}"][:, :], waits=allw, inc="ost")
            p.dma("sy", d_kpp, hbm[f"kpp{L}"][:, :], waits=allw, inc="ost")
            p.dma("sy", d_vp, hbm[f"vp{L}"][:, :], waits=allw, inc="ost")
        if upto and upto.startswith("B"):
            done = True
            break
        phase_C(L, c0=(upto == "C0"), upto_c1=(upto == "C1"))
        if upto == "C1":
            done = True
            break
        if upto == "C0":
            done = True
            break
        if dbg and upto == "C":
            d_ots = nc.dram_tensor("d_ots", [D, T], BF16, kind="ExternalOutput").ap()
            allw = [("t", p.n("t")), ("s", p.n("s")), ("v", p.n("v")),
                    ("syq", p.n("syq")), ("gq", p.n("gq")), ("odma", p.n("odma"))]
            for c in range(CH):
                p.dma("sy", d_ots[128 * c: 128 * (c + 1), :], oTs[c][:], waits=allw, inc="ost")
            if L == 1:
                d_m8 = nc.dram_tensor("d_m8", [8, T], F32, kind="ExternalOutput").ap()
                p.dma("sy", d_m8, hbm["m8"][:, :], waits=allw, inc="ost")
        if upto == "C":
            done = True
            break
        sD, tD = phase_D(L)
        if dbg and upto == "D":
            d_st = nc.dram_tensor("d_st", [D, T], BF16, kind="ExternalOutput").ap()
            allw = [("t", p.n("t")), ("s", p.n("s")), ("v", p.n("v")),
                    ("syq", p.n("syq")), ("gq", p.n("gq")), ("odma", p.n("odma"))]
            for c in range(CH):
                p.dma("sy", d_st[128 * c: 128 * (c + 1), :], sT[c], waits=allw, inc="ost")
        if upto == "D":
            done = True
            break
        if L == 0:
            phase_E0(sD, tD, first_out=(L == layers[-1]))
        else:
            import os as _os2
            phase_E1(sD, tD, dump_dacc=((dbg and upto == "E") or _os2.environ.get("DUMPF") == "1"))
        if upto == "E":
            done = True
            break
    if done:
        for c in range(CH):
            p.dma("g", outT[128 * c: 128 * (c + 1), :], xT[c][:],
                  waits=[("t", p.n("t")), ("s", p.n("s")), ("v", p.n("v")),
                         ("syq", p.n("syq")), ("gq", p.n("gq")),
                         ("odma", p.n("odma"))] + (
                      [(f"cck{layers[0]}", 1), (f"ccv{layers[0]}", 1)]
                      if p.n(f"cck{layers[0]}") else []),
                  inc="ost")
    p.op("sy", lambda e: e.nop(), waits=[("ost", p.n("ost"))], inc="sy2")
    p.op("g", lambda e: e.nop(), waits=[("ost", p.n("ost"))], inc="g")

    sems = {name: nc.alloc_semaphore(name) for name in p.counts}
    with nc.Block() as block:
        block.tensor(lambda e: p.emit("t", e, sems))
        block.scalar(lambda e: p.emit("s", e, sems))
        block.vector(lambda e: p.emit("v", e, sems))
        block.sync(lambda e: p.emit("sy", e, sems))
        block.gpsimd(lambda e: p.emit("g", e, sems))
    return nc


def _prep_inputs(inputs):
    ac = np.ascontiguousarray
    f32 = np.float32

    def colmaj(v):
        return ac(np.asarray(v, f32).reshape(CH, 128).T)

    base0 = base1 = None
    base0 = {
        "l0_wq": ac(np.asarray(inputs["L0_attn_Wq"], f32)),
        "l0_wk": ac(np.asarray(inputs["L0_attn_Wk"], f32)),
        "l0_wv": ac(np.asarray(inputs["L0_attn_Wv"], f32)),
        "l0_wo": ac(np.asarray(inputs["L0_attn_Wo"], f32)),
        "l0_fw": ac(np.asarray(inputs["L0_ffn_weight"], f32)),
        "l0_fwt": ac(np.asarray(inputs["L0_ffn_weight"], f32).T),
        "l0_bq": colmaj(inputs["L0_attn_bq"]),
        "l0_bk": colmaj(inputs["L0_attn_bk"]),
        "l0_bv": colmaj(inputs["L0_attn_bv"]),
        "l0_bo": colmaj(inputs["L0_attn_bo"]),
        "l0_fb": colmaj(inputs["L0_ffn_bias"]),
        "l0_rt": ac(np.asarray(inputs["L0_attn_R"], f32).T),
    }
    base1 = {
        "l1_wq": ac(np.asarray(inputs["L1_attn_Wq"], f32)),
        "l1_wk": ac(np.asarray(inputs["L1_attn_Wk"], f32)),
        "l1_wv": ac(np.asarray(inputs["L1_attn_Wv"], f32)),
        "l1_wo": ac(np.asarray(inputs["L1_attn_Wo"], f32)),
        "l1_lq1": ac(np.asarray(inputs["L1_attn_lq1"], f32).reshape(1, 64)),
        "l1_lk1": ac(np.asarray(inputs["L1_attn_lk1"], f32).reshape(1, 64)),
        "l1_lq2": ac(np.asarray(inputs["L1_attn_lq2"], f32).reshape(1, 64)),
        "l1_lk2": ac(np.asarray(inputs["L1_attn_lk2"], f32).reshape(1, 64)),
        "l1_lng": ac(np.asarray(inputs["L1_attn_ln_g"], f32).reshape(128, 1)),
        "l1_lnb": ac(np.asarray(inputs["L1_attn_ln_b"], f32).reshape(128, 1)),
        "l1_wgt": ac(np.asarray(inputs["L1_ffn_w_gate"], np.int32).T),
        "l1_wut": ac(np.asarray(inputs["L1_ffn_w_up"], np.int32).T),
        "l1_wdt": ac(np.asarray(inputs["L1_ffn_w_down"], np.int32).T),
    }
    x = np.asarray(inputs["L0_attn_x"], f32)
    xsh = []
    for c in range(8):
        b, half = c // 2, c % 2
        xsh.append(ac(x[b, 2048 * half: 2048 * (half + 1), :].T))
    return base0, base1, xsh


def kernel(**inputs):
    base0, base1, xsh = _prep_inputs(inputs)
    if FUSED:
        nc = build_program([0, 1])
        in_maps = [dict(base0, **base1, xoT=xsh[c]) for c in range(8)]
        res = run_bass_kernel_spmd(nc, in_maps, core_ids=list(range(8)))
        outs = [res.results[c]["outT"] for c in range(8)]
    else:
        nc0 = build_program([0])
        in_maps = [dict(base0, xoT=xsh[c]) for c in range(8)]
        r0 = run_bass_kernel_spmd(nc0, in_maps, core_ids=list(range(8)))
        x1 = [np.ascontiguousarray(np.asarray(r0.results[c]["outT"], np.float32))
              for c in range(8)]
        nc1 = build_program([1])
        in_maps = [dict(base1, xoT=x1[c]) for c in range(8)]
        r1 = run_bass_kernel_spmd(nc1, in_maps, core_ids=list(range(8)))
        outs = [r1.results[c]["outT"] for c in range(8)]
    out = np.empty((4, 4096, 1024), np.float32)
    for c in range(8):
        b, half = c // 2, c % 2
        out[b, 2048 * half: 2048 * (half + 1), :] = np.asarray(outs[c], np.float32).T
    return out


# version 06
# speedup vs baseline: 1.0046x; 1.0026x over previous
import math
import os
import numpy as np
import ml_dtypes
import concourse.bass as bass
import concourse.mybir as mybir
from concourse.bass_utils import run_bass_kernel_spmd

F32 = mybir.dt.float32
BF16 = mybir.dt.bfloat16
I32 = mybir.dt.int32
AF = mybir.ActivationFunctionType
ALU = mybir.AluOpType
AX = mybir.AxisListType

D = 1024
T = 2048
CH = 8
NT = 16
MAGIC = float(1.5 * 2 ** 23)
CPEN = 400.0
LAM_INIT = 0.8 - 0.6 * math.exp(-0.3 * 12)
KR = {0: 68, 1: 65}
VW = {0: 1040, 1: 1024}

FUSED = True


class Prog:

    def __init__(self):
        self.items = {k: [] for k in ("t", "s", "v", "sy", "g")}
        self.counts = {}

    def n(self, sem):
        return self.counts.get(sem, 0)

    def op(self, eng, fn, waits=(), inc=None, inck=1):
        w = [(s, v) for (s, v) in waits if v > 0]
        self.items[eng].append((fn, w, inc, inck))
        if inc is not None:
            self.counts[inc] = self.counts.get(inc, 0) + inck
        for s, _ in w:
            self.counts.setdefault(s, 0)

    def dma(self, eng, out, in_, waits=(), inc=None, **kw):
        assert inc is not None
        self.op(eng, lambda e: e.dma_start(out=out, in_=in_, **kw), waits, inc, 16)

    def emit(self, eng_name, e, sems):
        self_edge = eng_name in ("v", "s") and eng_name in sems
        waited = {}
        own = 0
        for fn, waits, inc, inck in self.items[eng_name]:
            if self_edge and own > 0 and waited.get(eng_name, 0) < own:
                e.wait_ge(sems[eng_name], own)
                waited[eng_name] = own
            for snm, val in waits:
                if waited.get(snm, 0) >= val:
                    continue
                e.wait_ge(sems[snm], val)
                waited[snm] = val
            instr = fn(e)
            if inc is not None:
                instr.then_inc(sems[inc], inck)
                if inc == eng_name:
                    own += inck


def build_program(layers, upto=None, dbg=False):
    nc = bass.Bass("TRN2", target_bir_lowering=False, debug=False)
    p = Prog()

    xoT = nc.dram_tensor("xoT", [D, T], F32, kind="ExternalInput").ap()
    outT = nc.dram_tensor("outT", [D, T], F32, kind="ExternalOutput").ap()
    ins = {}

    def ext(name, shape, dtype=F32):
        ins[name] = nc.dram_tensor(name, shape, dtype, kind="ExternalInput").ap()

    if 0 in layers:
        for w in ("l0_wq", "l0_wk", "l0_wv", "l0_wo", "l0_fw", "l0_fwt"):
            ext(w, [D, D])
        for b in ("l0_bq", "l0_bk", "l0_bv", "l0_bo", "l0_fb"):
            ext(b, [128, CH])
        ext("l0_rt", [64, 8])
    if 1 in layers:
        for w in ("l1_wq", "l1_wk", "l1_wv", "l1_wo"):
            ext(w, [D, D])
        for b in ("l1_lq1", "l1_lk1", "l1_lq2", "l1_lk2"):
            ext(b, [1, 64])
        ext("l1_lng", [128, 1])
        ext("l1_lnb", [128, 1])
        ext("l1_wgt", [D, 3072], I32)
        ext("l1_wut", [D, 3072], I32)
        ext("l1_wdt", [3072, D], I32)

    hbm = {"xnb": nc.dram_tensor("xnb", [D, T], BF16, kind="Internal")}
    for L in layers:
        kr, vw = KR[L], VW[L]
        hbm[f"qpp{L}"] = nc.dram_tensor(f"qpp{L}", [16 * kr, T], BF16, kind="Internal")
        hbm[f"kpp{L}"] = nc.dram_tensor(f"kpp{L}", [16 * kr, T], BF16, kind="Internal")
        hbm[f"kppf{L}"] = nc.dram_tensor(f"kppf{L}", [32 * kr, T], BF16, kind="Internal")
        hbm[f"vp{L}"] = nc.dram_tensor(f"vp{L}", [T, vw], BF16, kind="Internal")
        hbm[f"vpf{L}"] = nc.dram_tensor(f"vpf{L}", [2 * T, vw], BF16, kind="Internal")
    if 1 in layers:
        hbm["m8"] = nc.dram_tensor("m8", [8, T], F32, kind="Internal")

    A = nc.alloc_sbuf_tensor
    xT = [A(f"xT{c}", [128, T], BF16) for c in range(CH)]
    oTs = [A(f"oTs{c}", [128, T], BF16) for c in range(CH)]
    R1 = A("R1", [128, 16896], BF16)
    R2 = A("R2", [128, 16384], BF16)
    R3 = A("R3", [128, 4096], F32)
    R3b = R3.bitcast(BF16)
    asm = [A(f"asm{i}", [128, T], BF16) for i in range(2)]
    ws = [A(f"ws{i}", [128, 1024], BF16) for i in range(2)]
    AR = A("AR", [128, 10240], BF16)
    ROWS = A("ROWS", [128, 4096], F32)
    ROWSB = A("ROWSB", [128, 4096], BF16)
    RB32 = ROWSB.bitcast(F32)
    IDX = A("IDX", [128, 512], F32)
    IDXB = A("IDXB", [128, 16], BF16)
    ident = A("ident", [128, 128], BF16)
    idrow = A("idrow", [128, 128], F32)
    pidx = A("pidx", [128, 1], F32)
    identf = A("identf", [128, 128], F32)
    ones1 = A("ones1", [128, 1], BF16)
    onesr = A("onesr", [128, 128], BF16)
    onesrF = A("onesrF", [128, 128], F32)
    onesW = A("onesW", [128, 128], BF16)
    iota16 = A("iota16", [128, 16], F32)
    epsb = A("epsb", [128, 1], F32)
    wscale = A("wscale", [128, CH], F32)
    wrecip = A("wrecip", [128, CH], F32)
    biases = {}
    lrows = {}
    if 0 in layers:
        for b in ("l0_bq", "l0_bk", "l0_bv", "l0_bo", "l0_fb"):
            biases[b] = A("sb_" + b, [128, CH], F32)
        rtsb = A("rtsb", [128, 8], BF16)
    if 1 in layers:
        for b in ("l1_lq1", "l1_lk1", "l1_lq2", "l1_lk2"):
            lrows[b] = A("sb_" + b, [1, 64], F32)
        biases["l1_lng"] = A("sb_l1_lng", [128, 1], F32)
        biases["l1_lnb"] = A("sb_l1_lnb", [128, 1], F32)
        lamv = A("lam", [1, 1], F32)
        lngb = A("lngb", [128, 1], BF16)
        lnbb = A("lnbb", [128, 1], BF16)
        lngW = A("lngW", [128, 64], BF16)
        c1sb = A("c1sb", [128, CH], F32)

    QT = [R1[:, 2048 * c: 2048 * (c + 1)] for c in range(CH)]
    dacc = QT

    def vst(L, buf):
        w = VW[L] if L == 0 else 128
        return R1[:, 8448 * buf: 8448 * buf + 8 * w].rearrange("p (a b) -> p a b", a=8)

    xnT = [R2[:, 2048 * c: 2048 * (c + 1)] for c in range(CH)]
    sT = xnT
    kst = [R2[0:128, 0:4096], R2[0:128, 4096:8192]]
    qst = [R2[0:128, 8192:9216], R2[0:128, 9216:10240]]
    expb = [R2[:, 10240:11264], R2[:, 11264:12288]]
    o01 = [R2[:, 12288:14336], R2[:, 14336:16384]]

    mubc = R3b[:, 0:2048]
    rstdbc = R3b[:, 2048:4096]
    st_sc = [R3b[:, 0:2048], R3b[:, 2048:4096], R3b[:, 4096:6144]]
    rbcs = [R3b[:, 0:1024], R3b[:, 1024:2048]]
    l0tsl = R3b[:, 4096:6144]
    hsl = [R3b[:, 2048 * i: 2048 * (i + 1)] for i in range(4)] + \
          [asm[0][:, :], asm[1][:, :]]

    PA = nc.alloc_psum_tensor("PA", [128, 2048], F32)
    PB = nc.alloc_psum_tensor("PB", [128, 2048], F32)
    PBT16 = PB[0:16, 1152:1216].bitcast(BF16)
    PBT128 = PB[0:128, 1216:1217].bitcast(BF16)[:, 0:1]
    PBF1 = PB[0:1, 1280:1408]

    def mm4(PPcols, lhsT, rhs, start, stop, waits=(), width=2048):
        n = width // 512
        for i in range(n):
            sl = slice(512 * i, 512 * (i + 1))
            p.op("t", lambda e, sl=sl: e.matmul(PPcols[:, sl], lhsT, rhs[:, sl],
                                                start=start, stop=stop),
                 waits=(waits if i == 0 else ()), inc="t")

    p.op("g", lambda e: e.iota(idrow[:], pattern=[[1, 128]], base=0,
                               channel_multiplier=0, allow_small_or_imprecise_dtypes=True), inc="g")
    p.op("g", lambda e: e.iota(pidx[:], pattern=[[0, 1]], base=0,
                               channel_multiplier=1, allow_small_or_imprecise_dtypes=True), inc="g")
    p.op("g", lambda e: e.iota(iota16[:], pattern=[[0, 2], [1, 8]], base=0,
                               channel_multiplier=0, allow_small_or_imprecise_dtypes=True), inc="g")
    p.op("v", lambda e: e.tensor_scalar(out=identf[:], in0=idrow[:], scalar1=pidx[:],
                                        scalar2=None, op0=ALU.is_equal),
         waits=[("g", 2)], inc="v")
    p.op("v", lambda e: e.tensor_copy(ident[:], identf[:]), inc="v")
    p.op("v", lambda e: e.memset(ones1[:], 1.0), inc="v")
    p.op("v", lambda e: e.memset(onesr[:], 1.0), inc="v")
    p.op("v", lambda e: e.memset(onesrF[:], 1.0), inc="v")
    p.op("v", lambda e: e.memset(onesW[:], 0.0), inc="v")
    p.op("v", lambda e: e.memset(epsb[:], 1e-5), inc="v")
    for c in range(CH):
        p.op("v", lambda e, c=c: e.memset(onesW[0:64, 16 * c + 2 * c: 16 * c + 2 * c + 1], 1.0), inc="v")
        p.op("v", lambda e, c=c: e.memset(onesW[64:128, 16 * c + 2 * c + 1: 16 * c + 2 * c + 2], 1.0), inc="v")
    for b, sb in biases.items():
        p.dma("sy", sb[:], ins[b], inc="syq")
    for b, sb in lrows.items():
        p.dma("sy", sb[:], ins[b], inc="syq")
    if 0 in layers:
        p.dma("g", rtsb[0:64, :], ins["l0_rt"], inc="gq")
        p.dma("g", rtsb[64:128, :], ins["l0_rt"], inc="gq")
    for c in range(CH):
        p.dma("g", xT[c][:], xoT[128 * c: 128 * (c + 1), :], inc="gq")
    if 1 in layers:
        p.op("v", lambda e: e.tensor_copy(lngb[:], biases["l1_lng"][:]),
             waits=[("syq", p.n("syq"))], inc="v")
        p.op("v", lambda e: e.tensor_copy(lnbb[:], biases["l1_lnb"][:]), inc="v")
        p.op("v", lambda e: e.memset(lngW[:], 0.0), inc="v")
        for h in range(8):
            p.op("v", lambda e, h=h: e.tensor_copy(lngW[:, 8 * h + h: 8 * h + h + 1], lngb[:]), inc="v")
        p.op("v", lambda e: e.tensor_mul(ROWS[0:1, 0:64], lrows["l1_lq1"][:], lrows["l1_lk1"][:]), inc="v")
        p.op("v", lambda e: e.tensor_mul(ROWS[0:1, 64:128], lrows["l1_lq2"][:], lrows["l1_lk2"][:]), inc="v")
        p.op("v", lambda e: e.reduce_sum(ROWS[0:1, 128:129], ROWS[0:1, 0:64], axis=AX.X), inc="v")
        p.op("v", lambda e: e.reduce_sum(ROWS[0:1, 129:130], ROWS[0:1, 64:128], axis=AX.X), inc="v")
        p.op("s", lambda e: e.activation(out=ROWS[0:1, 130:131], in_=ROWS[0:1, 128:129], func=AF.Exp),
             waits=[("v", p.n("v"))], inc="s")
        p.op("s", lambda e: e.activation(out=ROWS[0:1, 131:132], in_=ROWS[0:1, 129:130], func=AF.Exp), inc="s")
        p.op("v", lambda e: e.tensor_sub(ROWS[0:1, 132:133], ROWS[0:1, 130:131], ROWS[0:1, 131:132]),
             waits=[("s", p.n("s"))], inc="v")
        p.op("v", lambda e: e.tensor_scalar(out=lamv[:], in0=ROWS[0:1, 132:133], scalar1=LAM_INIT,
                                            scalar2=None, op0=ALU.add), inc="v")
    setup_gq = p.n("gq")
    setup_syq = p.n("syq")
    setup_v = p.n("v")
    setup_g = p.n("g")

    def phase_A(L, first):
        if first:
            xw = [("gq", setup_gq), ("syq", setup_syq), ("v", setup_v), ("g", setup_g)]
        else:
            xw = [("v", p.n("v")), ("s", p.n("s")), ("t", p.n("t")),
                  ("syq", p.n("syq")), ("gq", p.n("gq")), ("odma", p.n("odma"))]
        amm = []
        for c in range(CH):
            buf = asm[c % 2]
            w = list(xw)
            if c >= 2:
                w.append(("t", amm[c - 2]))
            p.op("v", lambda e, b=buf, c=c: e.tensor_mul(b[:], xT[c][:], xT[c][:]),
                 waits=w, inc="v")
            vsq = p.n("v")
            mm4(PA[0:1, :], ones1[:], xT[c][:], c == 0, c == CH - 1, waits=xw)
            mm4(PB[0:1, :], ones1[:], buf[:], c == 0, c == CH - 1, waits=[("v", vsq)])
            amm.append(p.n("t"))
        ts = p.n("t")
        p.op("v", lambda e: e.tensor_scalar(out=ROWS[0:1, 0:2048], in0=PA[0:1, :],
                                            scalar1=1.0 / D, scalar2=None, op0=ALU.mult),
             waits=[("t", ts)], inc="v")
        p.op("v", lambda e: e.tensor_scalar(out=ROWS[0:1, 2048:4096], in0=PB[0:1, :],
                                            scalar1=1.0 / D, scalar2=None, op0=ALU.mult), inc="v")
        p.op("v", lambda e: e.tensor_mul(RB32[0:1, 0:2048], ROWS[0:1, 0:2048], ROWS[0:1, 0:2048]), inc="v")
        p.op("v", lambda e: e.tensor_sub(ROWS[0:1, 2048:4096], ROWS[0:1, 2048:4096],
                                         RB32[0:1, 0:2048]), inc="v")
        p.op("s", lambda e: e.activation(out=RB32[0:1, 0:2048], in_=ROWS[0:1, 2048:4096],
                                         func=AF.Sqrt, bias=epsb[0:1, :]), waits=[("v", p.n("v"))], inc="s")
        p.op("v", lambda e: e.reciprocal(ROWS[0:1, 2048:4096], RB32[0:1, 0:2048]),
             waits=[("s", p.n("s"))], inc="v")
        p.op("v", lambda e: e.tensor_copy(asm[0][0:1, 0:2048], ROWS[0:1, 0:2048]), inc="v")
        p.op("v", lambda e: e.tensor_copy(asm[1][0:1, 0:2048], ROWS[0:1, 2048:4096]), inc="v")
        vr = p.n("v")
        mm4(PA, onesr[0:1, :], asm[0][0:1, 0:2048], True, True, waits=[("v", vr)])
        mm4(PB, onesr[0:1, :], asm[1][0:1, 0:2048], True, True)
        tb = p.n("t")
        p.op("v", lambda e: e.tensor_copy(mubc[:], PA[:, :]), waits=[("t", tb)], inc="v")
        p.op("v", lambda e: e.tensor_copy(rstdbc[:], PB[:, :]), inc="v")
        vb = p.n("v")
        for c in range(CH):
            p.op("v", lambda e, c=c: e.tensor_sub(asm[c % 2][:], xT[c][:], mubc[:]),
                 waits=[("v", vb)], inc="v")
            p.op("v", lambda e, c=c: e.tensor_mul(xnT[c], asm[c % 2][:], rstdbc[:]), inc="v")
            p.dma("sy", hbm["xnb"][128 * c: 128 * (c + 1), :], xnT[c],
                  waits=[("v", p.n("v"))], inc="syq")
        return p.n("v")

    def proj(L, wname, xn_v, post):
        prev_mm = [p.n("t"), p.n("t")]
        prev_ev = [p.n("s"), p.n("s")]
        entry_v = p.n("v")
        for oc in range(CH):
            p.dma("g", ws[oc % 2][:].rearrange("p (a b) -> p a b", a=CH),
                  ins[wname].rearrange("(a p) b -> p a b", p=128)[:, :, 128 * oc: 128 * (oc + 1)],
                  waits=[("t", prev_mm[oc % 2])], inc="gq")
            ld = p.n("gq")
            PP = PA if oc % 2 == 0 else PB
            for c in range(CH):
                mm4(PP, ws[oc % 2][:, 128 * c: 128 * (c + 1)], xnT[c],
                    c == 0, c == CH - 1,
                    waits=[("gq", ld), ("v", max(xn_v, entry_v)), ("s", prev_ev[oc % 2])])
            prev_mm[oc % 2] = p.n("t")
            post(oc, PP, p.n("t"))
            prev_ev[oc % 2] = p.n("s")

    def bucket_idx(srcs, colbase, src_s):
        for oc in range(CH):
            for tt in range(NT):
                sl = slice(128 * tt, 128 * (tt + 1))
                w = [("s", src_s), ("v", p.n("v")), ("g", setup_g)]
                p.op("t", lambda e, oc=oc, sl=sl: e.matmul(
                    PB[:, 1024:1032], srcs[oc][0:64, sl], rtsb[0:64, :], start=True, stop=True),
                    waits=w, inc="t")
                p.op("t", lambda e, oc=oc, sl=sl: e.matmul(
                    PB[:, 1032:1040], srcs[oc][64:128, sl], rtsb[64:128, :], start=True, stop=True), inc="t")
                tl = p.n("t")
                p.op("v", lambda e: e.reduce_max(
                    R3[:, 0:2].rearrange("p (a b) -> p a b", b=1),
                    PB[:, 1024:1040].rearrange("p (a b) -> p a b", b=8), axis=AX.X),
                    waits=[("t", tl)], inc="v")
                p.op("v", lambda e: e.tensor_tensor(
                    out=R3[:, 2:18].rearrange("p (a b) -> p a b", b=8),
                    in0=PB[:, 1024:1040].rearrange("p (a b) -> p a b", b=8),
                    in1=R3[:, 0:2].rearrange("p (a b) -> p a b", b=1).broadcast_to([128, 2, 8]),
                    op=ALU.is_ge), inc="v")
                p.op("v", lambda e: e.tensor_mul(R3[:, 2:18], R3[:, 2:18], iota16[:, 0:16]), inc="v")
                p.op("v", lambda e, oc=oc, tt=tt: e.reduce_sum(
                    IDX[:, colbase + 16 * tt + 2 * oc: colbase + 16 * tt + 2 * oc + 2].rearrange(
                        "p (a b) -> p a b", b=1),
                    R3[:, 2:18].rearrange("p (a b) -> p a b", b=8), axis=AX.X), inc="v")
        return p.n("v")

    def idx_to_rows(colbase, dst_rw, v_ready):
        for tt in range(NT):
            p.op("v", lambda e, tt=tt: e.tensor_copy(
                IDXB[:, 0:16], IDX[:, colbase + 16 * tt: colbase + 16 * tt + 16]),
                waits=[("v", v_ready), ("t", p.n("t"))], inc="v")
            p.op("t", lambda e: e.transpose(PBT16, IDXB[:, 0:16], ident[:]),
                 waits=[("v", p.n("v"))], inc="t")
            p.op("v", lambda e, tt=tt: e.tensor_copy(
                dst_rw[:, 128 * tt: 128 * (tt + 1)], PBT16),
                waits=[("t", p.n("t"))], inc="v")
        return p.n("v")

    def row_dma(dst_pp, kr, rowoff, src_rows_b, waits):
        p.dma("sy", dst_pp.ap().rearrange("(h r) t -> h r t", r=kr)[:, rowoff: rowoff + 1, :],
              src_rows_b.rearrange("h (o t) -> h o t", o=1), waits=waits, inc="syq")

    def phase_B(L, xn_v, sub=None):
        plain_ev = (sub == 'B5a')
        small_wv = (sub == 'B5b')
        kr, vw = KR[L], VW[L]
        pref = "l0_" if L == 0 else "l1_"
        qpp, kpp, vp = hbm[f"qpp{L}"], hbm[f"kpp{L}"], hbm[f"vp{L}"]

        def qpost(oc, PP, tmm):
            if L == 0:
                p.op("v", lambda e, oc=oc, PP=PP: e.tensor_scalar(
                    out=QT[oc], in0=PP[:, :], scalar1=biases["l0_bq"][:, oc: oc + 1],
                    scalar2=None, op0=ALU.add), waits=[("t", tmm)], inc="v")
                p.op("s", lambda e: e.nop(), waits=[("v", p.n("v"))], inc="s")
            else:
                p.op("s", lambda e, oc=oc, PP=PP: e.activation(
                    out=QT[oc], in_=PP[:, :], func=AF.Copy), waits=[("t", tmm)], inc="s")
        proj(L, pref + "wq", xn_v, qpost)
        s_qt = p.n("s")
        if sub == "B1":
            return

        def kpost(oc, PP, tmm):
            if L == 0:
                p.op("v", lambda e, oc=oc, PP=PP: e.tensor_scalar(
                    out=oTs[oc][:], in0=PP[:, :], scalar1=biases["l0_bk"][:, oc: oc + 1],
                    scalar2=None, op0=ALU.add), waits=[("t", tmm)], inc="v")
                p.op("s", lambda e: e.nop(), waits=[("v", p.n("v"))], inc="s")
            else:
                p.op("s", lambda e, oc=oc, PP=PP: e.activation(
                    out=oTs[oc][:], in_=PP[:, :], func=AF.Copy), waits=[("t", tmm)], inc="s")
        proj(L, pref + "wk", xn_v, kpost)
        s_kt = p.n("s")
        if sub == "B2":
            return

        for h in range(16):
            p.dma("sy", kpp[kr * h: kr * h + 64, :],
                  oTs[h // 2][64 * (h % 2): 64 * (h % 2) + 64, :],
                  waits=[("s", s_kt)], inc="syq")
        for h in range(16):
            p.dma("sy", qpp[kr * h: kr * h + 64, :],
                  QT[h // 2][64 * (h % 2): 64 * (h % 2) + 64, :],
                  waits=[("s", s_qt)], inc="syq")

        if L == 1:
            p.op("v", lambda e: e.memset(ROWSB[0:16, 0:2048], 1.0),
                 waits=[("v", p.n("v")), ("syq", p.n("syq"))], inc="v")
            row_dma(kpp, kr, 64, ROWSB[0:16, 0:2048], [("v", p.n("v"))])
        if L == 0:
            vk = bucket_idx(oTs, 0, s_kt)
            vk2 = idx_to_rows(0, ROWS[0:16, 2048:4096], vk)
            p.op("v", lambda e: e.memset(ROWSB[0:16, 0:2048], 1.0), waits=[("v", vk2)], inc="v")
            row_dma(kpp, kr, 64, ROWSB[0:16, 0:2048], [("v", p.n("v"))])
            p.op("v", lambda e: e.tensor_mul(ROWSB[0:16, 2048:4096], ROWS[0:16, 2048:4096],
                                             ROWS[0:16, 2048:4096]), inc="v")
            row_dma(kpp, kr, 65, ROWSB[0:16, 2048:4096], [("v", p.n("v"))])
            p.op("v", lambda e: e.memset(ROWSB[0:16, 0:2048], -CPEN),
                 waits=[("syq", p.n("syq"))], inc="v")
            row_dma(kpp, kr, 66, ROWSB[0:16, 0:2048], [("v", p.n("v"))])
            p.op("v", lambda e: e.tensor_scalar(out=ROWSB[0:16, 2048:4096],
                                                in0=ROWS[0:16, 2048:4096], scalar1=2 * CPEN,
                                                scalar2=None, op0=ALU.mult),
                 waits=[("syq", p.n("syq"))], inc="v")
            row_dma(kpp, kr, 67, ROWSB[0:16, 2048:4096], [("v", p.n("v"))])
            vq = bucket_idx(QT, 256, s_qt)
            vq2 = idx_to_rows(256, ROWS[0:16, 2048:4096], vq)
        else:
            vq2 = p.n("v")
        if sub == "B3":
            return

        q2mm = []
        for c in range(CH):
            w = [("s", s_qt), ("syq", p.n("syq"))]
            if c >= 2:
                w.append(("t", q2mm[c - 2]))
            p.op("v", lambda e, c=c: e.tensor_mul(asm[c % 2][:], QT[c][:], QT[c][:]),
                 waits=w, inc="v")
            mm4(PA[0:16, :], onesW[:, 16 * c: 16 * (c + 1)], asm[c % 2][:],
                c == 0, c == CH - 1, waits=[("v", p.n("v"))])
            q2mm.append(p.n("t"))
        p.op("v", lambda e: e.tensor_copy(ROWS[0:16, 0:2048], PA[0:16, :]),
             waits=[("t", p.n("t"))], inc="v")
        p.op("s", lambda e: e.activation(out=ROWS[0:16, 0:2048], in_=ROWS[0:16, 0:2048],
                                         func=AF.Sqrt), waits=[("v", p.n("v"))], inc="s")
        p.op("v", lambda e: e.tensor_scalar(out=ROWSB[0:16, 0:2048], in0=ROWS[0:16, 0:2048],
                                            scalar1=-16.0, scalar2=None, op0=ALU.mult),
             waits=[("s", p.n("s"))], inc="v")
        row_dma(qpp, kr, 64, ROWSB[0:16, 0:2048], [("v", p.n("v"))])
        if L == 0:
            p.op("v", lambda e: e.memset(ROWSB[0:16, 0:2048], -CPEN),
                 waits=[("syq", p.n("syq")), ("v", vq2)], inc="v")
            row_dma(qpp, kr, 65, ROWSB[0:16, 0:2048], [("v", p.n("v"))])
            p.op("v", lambda e: e.tensor_mul(ROWSB[0:16, 2048:4096], ROWS[0:16, 2048:4096],
                                             ROWS[0:16, 2048:4096]),
                 waits=[("syq", p.n("syq"))], inc="v")
            row_dma(qpp, kr, 66, ROWSB[0:16, 2048:4096], [("v", p.n("v"))])
            p.op("v", lambda e: e.tensor_copy(ROWSB[0:16, 0:2048], ROWS[0:16, 2048:4096]),
                 waits=[("syq", p.n("syq"))], inc="v")
            row_dma(qpp, kr, 67, ROWSB[0:16, 0:2048], [("v", p.n("v"))])
        if sub == "B4":
            return

        wvfull = AR[:, 0:8192]
        ar_guard = [("t", p.n("t")), ("v", p.n("v")), ("s", p.n("s"))]
        if small_wv:
            for c in range(CH):
                p.dma("g", wvfull[:, 1024 * c: 1024 * (c + 1)],
                      ins[pref + "wv"][128 * c: 128 * (c + 1), :], inc="gq")
        else:
            p.dma("g", wvfull.rearrange("p (a b) -> p a b", a=CH),
                  ins[pref + "wv"].rearrange("(a p) b -> p a b", p=128),
                  waits=ar_guard, inc="gq")
        wvld = p.n("gq")
        prev_t = [p.n("t"), p.n("t")]
        prev_s = [p.n("s"), p.n("s")]
        entry_v = p.n("v")
        prev_v = [entry_v, entry_v]
        prev_dma = [0, 0]
        for tt in range(NT):
            PP = PA if tt % 2 == 0 else PB
            for c in range(CH):
                mm4(PP[:, 0:1024], xnT[c][:, 128 * tt: 128 * (tt + 1)],
                    wvfull[:, 1024 * c: 1024 * (c + 1)], c == 0, c == CH - 1,
                    waits=[("gq", wvld), ("v", max(xn_v, prev_v[tt % 2])), ("t", prev_t[tt % 2]),
                           ("s", prev_s[tt % 2])], width=1024)
            tmm = p.n("t")
            buf = asm[tt % 2]
            if L == 0 and plain_ev:
                p.op("s", lambda e, PP=PP, buf=buf: e.activation(
                    out=buf[:, 0:1024], in_=PP[:, 0:1024], func=AF.Copy),
                    waits=[("t", tmm), ("syq", prev_dma[tt % 2])], inc="s")
                p.op("v", lambda e: e.nop(), inc="v")
            elif L == 0:
                p.op("s", lambda e, PP=PP, buf=buf: e.activation(
                    out=buf[:, 0:1040].rearrange("p (a b) -> p a b", b=65)[:, :, 0:64],
                    in_=PP[:, 0:1024].rearrange("p (a b) -> p a b", b=64), func=AF.Copy),
                    waits=[("t", tmm), ("syq", prev_dma[tt % 2])], inc="s")
                p.op("v", lambda e, buf=buf: e.memset(
                    buf[:, 0:1040].rearrange("p (a b) -> p a b", b=65)[:, :, 64:65], 1.0),
                    waits=[("s", p.n("s")), ("syq", prev_dma[tt % 2])], inc="v")
            else:
                p.op("s", lambda e, PP=PP, buf=buf: e.activation(
                    out=buf[:, 0:1024], in_=PP[:, 0:1024], func=AF.Copy),
                    waits=[("t", tmm), ("syq", prev_dma[tt % 2])], inc="s")
                p.op("v", lambda e: e.nop(), inc="v")
            p.dma("sy", vp[128 * tt: 128 * (tt + 1), :], buf[:, 0:vw],
                  waits=[("s", p.n("s")), ("v", p.n("v"))], inc="syq")
            prev_dma[tt % 2] = p.n("syq")
            prev_t[tt % 2] = p.n("t")
            prev_s[tt % 2] = p.n("s")
            prev_v[tt % 2] = p.n("v")
        if sub in ("B5", "B5a", "B5b"):
            return

        agw = [("syq", p.n("syq")), ("gq", p.n("gq")), ("v", p.n("v")), ("s", p.n("s"))]
        p.op("g", lambda e: e.collective_compute(
            "AllGather", ALU.bypass, replica_groups=[[0, 1], [2, 3], [4, 5], [6, 7]],
            ins=[kpp.ap().opt()], outs=[hbm[f"kppf{L}"].ap().opt()]),
            waits=agw, inc=f"cck{L}")
        p.op("g", lambda e: e.collective_compute(
            "AllGather", ALU.bypass, replica_groups=[[0, 1], [2, 3], [4, 5], [6, 7]],
            ins=[vp.ap().opt()], outs=[hbm[f"vpf{L}"].ap().opt()]),
            waits=agw, inc=f"ccv{L}")

    R1f = R1.bitcast(F32)

    def l1_head_post(h):
        SUM0 = ROWS[0:1, 0:2048]
        SUM1 = ROWS[0:1, 2048:4096]
        FMU = R1f[0:1, 4736:6784]
        F0b = R1[0:1, 5120:7168]
        F1b = R1[0:1, 13568:15616]
        H0 = RB32[0:1, 0:1024]
        H1 = RB32[0:1, 1024:2048]
        H2 = R1f[0:1, 512:1536]
        H3 = R1f[0:1, 1536:2560]
        H4 = R3[0:1, 3072:4096]
        H5 = R1f[0:1, 2560:3584]
        H6 = R1f[0:1, 6784:7808]
        stg32 = ROWS[32:33, 0:2048]
        stg64 = ROWS[64:65, 0:2048]
        osh = p.n("odma")
        p.op("v", lambda e: e.tensor_mul(st_sc[0][:, :], o01[0][:, :], o01[0][:, :]),
             waits=[("odma", osh)], inc="v")
        p.op("v", lambda e: e.tensor_mul(st_sc[1][:, :], o01[1][:, :], o01[1][:, :]), inc="v")
        p.op("v", lambda e: e.tensor_mul(st_sc[2][:, :], o01[0][:, :], o01[1][:, :]), inc="v")
        vp_ = p.n("v")
        mm4(PA[0:1, :], ones1[:], o01[0][:, :], True, True,
            waits=[("v", vp_), ("s", p.n("s"))])
        mm4(PA[32:33, :], ones1[:], o01[1][:, :], True, True)
        mm4(PA[64:65, :], ones1[:], st_sc[0][:, :], True, True)
        mm4(PB[0:1, :], ones1[:], st_sc[1][:, :], True, True)
        mm4(PB[32:33, :], ones1[:], st_sc[2][:, :], True, True)
        tst = p.n("t")
        p.op("v", lambda e: e.tensor_copy(stg32, PA[32:33, :]), waits=[("t", tst)], inc="v")
        p.op("v", lambda e: e.tensor_copy(stg64, PA[64:65, :]), inc="v")
        p.op("v", lambda e: e.tensor_copy(ROWS[32:33, 2048:4096], PB[32:33, :]), inc="v")
        for qh in range(2):
            qsl = slice(1024 * qh, 1024 * (qh + 1))
            s0h = SUM0[:, qsl]
            s1h = SUM1[:, qsl]
            p.op("v", lambda e, s0h=s0h: e.reciprocal(H5, s0h), inc="v")
            p.op("v", lambda e, s1h=s1h: e.reciprocal(H6, s1h), inc="v")
            p.op("v", lambda e, qsl=qsl: e.tensor_copy(H0, PA[0:1, qsl]),
                 waits=[("t", tst)], inc="v")
            p.dma("g", H1, stg32[:, qsl], waits=[("v", p.n("v"))], inc="odma")
            p.dma("g", H2, stg64[:, qsl], inc="odma")
            p.op("v", lambda e, qsl=qsl: e.tensor_copy(H3, PB[0:1, qsl]), inc="v")
            od = p.n("odma")
            p.op("v", lambda e: e.tensor_mul(H0, H5, H0), waits=[("odma", od)], inc="v")
            p.op("v", lambda e: e.tensor_mul(H1, H6, H1), inc="v")
            p.op("v", lambda e: e.tensor_scalar(out=H1, in0=H1, scalar1=lamv[0:1, 0:1],
                                                scalar2=None, op0=ALU.mult), inc="v")
            p.op("v", lambda e: e.tensor_sub(H0, H0, H1), inc="v")
            p.op("v", lambda e: e.tensor_scalar(out=H0, in0=H0, scalar1=1.0 / 128,
                                                scalar2=None, op0=ALU.mult), inc="v")
            p.op("v", lambda e: e.tensor_mul(H1, H5, H5), inc="v")
            p.op("v", lambda e: e.tensor_mul(H1, H1, H2), inc="v")
            p.op("v", lambda e: e.tensor_mul(H2, H6, H6), inc="v")
            p.op("v", lambda e: e.tensor_mul(H2, H2, H3), inc="v")
            p.op("v", lambda e: e.tensor_scalar(out=H2, in0=H2, scalar1=lamv[0:1, 0:1],
                                                scalar2=None, op0=ALU.mult), inc="v")
            p.op("v", lambda e: e.tensor_scalar(out=H2, in0=H2, scalar1=lamv[0:1, 0:1],
                                                scalar2=None, op0=ALU.mult), inc="v")
            p.dma("g", H3, ROWS[32:33, 2048 + 1024 * qh: 2048 + 1024 * (qh + 1)],
                  waits=[("v", p.n("v"))], inc="odma")
            p.op("v", lambda e: e.tensor_mul(H4, H5, H6),
                 waits=[("odma", p.n("odma"))], inc="v")
            p.op("v", lambda e: e.tensor_mul(H4, H4, H3), inc="v")
            p.op("v", lambda e: e.tensor_scalar(out=H4, in0=H4, scalar1=lamv[0:1, 0:1],
                                                scalar2=None, op0=ALU.mult), inc="v")
            p.op("v", lambda e: e.tensor_scalar(out=H4, in0=H4, scalar1=2.0,
                                                scalar2=None, op0=ALU.mult), inc="v")
            p.op("v", lambda e: e.tensor_add(H1, H1, H2), inc="v")
            p.op("v", lambda e: e.tensor_sub(H1, H1, H4), inc="v")
            p.op("v", lambda e: e.tensor_scalar(out=H1, in0=H1, scalar1=1.0 / 128,
                                                scalar2=None, op0=ALU.mult), inc="v")
            p.op("v", lambda e: e.tensor_mul(H2, H0, H0), inc="v")
            p.op("v", lambda e: e.tensor_sub(H1, H1, H2), inc="v")
            p.op("s", lambda e: e.activation(out=H2, in_=H1, func=AF.Sqrt, bias=epsb[0:1, :]),
                 waits=[("v", p.n("v"))], inc="s")
            p.op("v", lambda e: e.reciprocal(H1, H2), waits=[("s", p.n("s"))], inc="v")
            p.op("v", lambda e, s0h=s0h: e.tensor_mul(s0h, H5, H1), inc="v")
            p.op("v", lambda e, s1h=s1h: e.tensor_mul(s1h, H6, H1), inc="v")
            p.op("v", lambda e, s1h=s1h: e.tensor_scalar(out=s1h, in0=s1h,
                                                         scalar1=lamv[0:1, 0:1],
                                                         scalar2=None, op0=ALU.mult), inc="v")
            p.op("v", lambda e: e.tensor_mul(H0, H0, H1), inc="v")
            p.op("v", lambda e, qsl=qsl: e.tensor_copy(FMU[:, qsl], H0),
                 waits=[("syq", p.n("syq"))], inc="v")
        p.dma("sy", hbm["m8"][h: h + 1, :], FMU, waits=[("v", p.n("v"))], inc="syq")
        p.op("v", lambda e: e.tensor_copy(F0b, SUM0), inc="v")
        vF = p.n("v")
        mm4(PA, onesr[0:1, :], F0b, True, True, waits=[("v", vF)])
        p.op("v", lambda e: e.tensor_copy(asm[0][:, :], PA[:, :]), waits=[("t", p.n("t"))], inc="v")
        p.op("v", lambda e: e.tensor_copy(F1b, SUM1), inc="v")
        vF1 = p.n("v")
        mm4(PA, onesr[0:1, :], F1b, True, True, waits=[("v", vF1)])
        p.op("v", lambda e: e.tensor_copy(asm[1][:, :], PA[:, :]), waits=[("t", p.n("t"))], inc="v")
        p.op("v", lambda e: e.tensor_mul(st_sc[0][:, :], o01[0][:, :], asm[0][:, :]), inc="v")
        p.op("v", lambda e: e.tensor_mul(st_sc[1][:, :], o01[1][:, :], asm[1][:, :]), inc="v")
        p.op("v", lambda e, h=h: e.tensor_sub(oTs[h][:, :], st_sc[0][:, :], st_sc[1][:, :]), inc="v")

    def phase_C(L, c0=False, upto_c1=False):
        kr, vw = KR[L], VW[L]
        nh = 16 if L == 0 else 8
        njl = 1 if L == 0 else 2
        kppf, vpf, qpp = hbm[f"kppf{L}"], hbm[f"vpf{L}"], hbm[f"qpp{L}"]
        scores = [PA[:, 0:1024], PA[:, 1024:2048]]
        misc = PB[:, 1024:2048]
        opsum = PB[0:65, 0:1024] if L == 0 else PB[0:128, 0:1024]

        kst_used = [0, 0]
        qst_used = [0, 0]
        vst_used = [0, 0]
        kn = [p.n("kl0"), p.n("kl1")]
        qn = [p.n("ql0"), p.n("ql1")]
        vn = [p.n("vl0"), p.n("vl1")]
        p.counts.setdefault("kl0", 0); p.counts.setdefault("kl1", 0)
        p.counts.setdefault("ql0", 0); p.counts.setdefault("ql1", 0)
        p.counts.setdefault("vl0", 0); p.counts.setdefault("vl1", 0)
        sc_free = [0, 0]
        exp_free = [0, 0]
        exp_done = [0, 0]
        op_free = p.n("v")
        misc_free = p.n("v")
        pa_free = p.n("s")
        state = {"qtog": 0}

        def load_kst(h, b):
            for chunk in range(2):
                p.dma("sy", kst[b][0:kr, 2048 * chunk: 2048 * (chunk + 1)],
                      kppf[chunk * 16 * kr + kr * h: chunk * 16 * kr + kr * h + kr, :],
                      waits=[("t", kst_used[b]), (f"cck{L}", 1)], inc=f"kl{b}")
            kn[b] += 32

        def load_qst(hh, qh, b):
            p.dma("sy", qst[b][0:kr, :], qpp[kr * hh: kr * hh + kr, 1024 * qh: 1024 * (qh + 1)],
                  waits=[("t", qst_used[b])], inc=f"ql{b}")
            qn[b] += 16

        def load_vst(h, kq, b):
            csl = slice(0, 1040) if L == 0 else slice(128 * h, 128 * (h + 1))
            p.dma("sy", vst(L, b)[:, :, :],
                  vpf.ap().rearrange("(a p) b -> p a b", p=128)[:, 8 * kq: 8 * kq + 8, csl],
                  waits=[("t", vst_used[b]), (f"ccv{L}", 1)], inc=f"vl{b}")
            vn[b] += 16

        if L == 0:
            load_kst(0, 0)
        for h in range(nh):
            if L == 0:
                kb = h % 2
                if h + 1 < nh:
                    load_kst(h + 1, (h + 1) % 2)
            else:
                load_kst(2 * h, 0)
                load_kst(2 * h + 1, 1)
            for qh in range(2):
                for j in range(njl):
                    hh = h if L == 0 else 2 * h + j
                    if L == 1:
                        kb = j
                    qb = state["qtog"]
                    state["qtog"] ^= 1
                    load_qst(hh, qh, qb)
                    myq = qn[qb]
                    for kq in range(4):
                        vb = kq % 2
                        load_vst(h, kq, vb)
                        myv = vn[vb]
                        for ki in range(8):
                            k = 8 * kq + ki
                            sp = k % 2
                            w = [("s", max(sc_free[sp], pa_free)),
                                 (f"kl{kb}", kn[kb]), (f"ql{qb}", myq)]
                            for i in range(2):
                                sl = slice(512 * i, 512 * (i + 1))
                                p.op("t", lambda e, sp=sp, sl=sl, k=k, kb=kb, qb=qb: e.matmul(
                                    scores[sp][:, sl],
                                    kst[kb][0:kr, 128 * k: 128 * (k + 1)],
                                    qst[qb][0:kr, sl], start=True, stop=True),
                                    waits=(w if i == 0 else ()), inc="t")
                            kst_used[kb] = qst_used[qb] = tqk = p.n("t")
                            p.op("s", lambda e, sp=sp: e.activation(
                                out=expb[sp], in_=scores[sp][:, :], func=AF.Exp, scale=0.125),
                                waits=[("t", max(tqk, exp_free[sp]))], inc="s")
                            exp_done[sp] = sc_free[sp] = p.n("s")
                            w2 = [("s", exp_done[sp]), (f"vl{vb}", myv)]
                            if k == 0:
                                w2.append(("v", op_free))
                                w2.append(("s", pa_free))
                            dv = 65 if L == 0 else 128
                            voff = 65 * h if L == 0 else 0
                            for i in range(2):
                                sl = slice(512 * i, 512 * (i + 1))
                                p.op("t", lambda e, vb=vb, ki=ki, sl=sl, k=k, sp=sp,
                                     dv=dv, voff=voff: e.matmul(
                                         opsum[:, sl],
                                         vst(L, vb)[:, ki: ki + 1, voff: voff + dv].rearrange(
                                             "p a b -> p (a b)"),
                                         expb[sp][:, sl], start=(k == 0), stop=(k == 31)),
                                     waits=(w2 if i == 0 else ()), inc="t")
                            vst_used[vb] = p.n("t")
                            if L == 1:
                                w3 = [("v", misc_free)] if k == 0 else ()
                                for i in range(2):
                                    sl = slice(512 * i, 512 * (i + 1))
                                    p.op("t", lambda e, j=j, sl=sl, k=k, sp=sp: e.matmul(
                                        misc[64 * j: 64 * j + 1, sl], ones1[:],
                                        expb[sp][:, sl], start=(k == 0), stop=(k == 31)),
                                        waits=(w3 if i == 0 else ()), inc="t")
                            exp_free[sp] = p.n("t")
                    tav = p.n("t")
                    if L == 0:
                        rsl = ROWS[64:65, 1024 * qh: 1024 * (qh + 1)]
                        rbl = ROWSB[64:65, 1024 * qh: 1024 * (qh + 1)]
                        p.op("v", lambda e, rsl=rsl: e.reciprocal(rsl, opsum[64:65, :]),
                             waits=[("t", tav)], inc="v")
                        p.op("v", lambda e, rsl=rsl, rbl=rbl: e.tensor_copy(rbl, rsl), inc="v")
                        vr = p.n("v")
                        for i in range(2):
                            sl = slice(512 * i, 512 * (i + 1))
                            p.op("t", lambda e, rbl=rbl, sl=sl: e.matmul(
                                misc[:, sl], onesr[64:65, :], rbl[:, sl], start=True, stop=True),
                                waits=([("v", max(vr, misc_free))] if i == 0 else ()), inc="t")
                        p.op("v", lambda e, qh=qh: e.tensor_copy(rbcs[qh][:, :], misc[:, :]),
                             waits=[("t", p.n("t"))], inc="v")
                        misc_free = p.n("v")
                        p.op("v", lambda e, qh=qh: e.tensor_mul(
                            l0tsl[0:64, 0:1024], opsum[0:64, :], rbcs[qh][0:64, :]), inc="v")
                        op_free = p.n("v")
                        if h % 2 == 0:
                            p.op("v", lambda e, h=h, qh=qh: e.tensor_scalar(
                                out=oTs[h // 2][0:64, 1024 * qh: 1024 * (qh + 1)],
                                in0=l0tsl[0:64, 0:1024],
                                scalar1=biases["l0_bv"][0:64, h // 2: h // 2 + 1],
                                scalar2=None, op0=ALU.add), inc="v")
                        else:
                            p.dma("g", oTs[h // 2][64:128, 1024 * qh: 1024 * (qh + 1)],
                                  l0tsl[0:64, 0:1024], waits=[("v", p.n("v"))], inc="odma")
                            p.op("v", lambda e, h=h, qh=qh: e.tensor_scalar(
                                out=oTs[h // 2][64:128, 1024 * qh: 1024 * (qh + 1)],
                                in0=oTs[h // 2][64:128, 1024 * qh: 1024 * (qh + 1)],
                                scalar1=biases["l0_bv"][64:128, h // 2: h // 2 + 1],
                                scalar2=None, op0=ALU.add),
                                waits=[("odma", p.n("odma"))], inc="v")
                    else:
                        p.op("v", lambda e, j=j, qh=qh: e.tensor_copy(
                            o01[j][:, 1024 * qh: 1024 * (qh + 1)], opsum[:, :]),
                            waits=[("t", tav)], inc="v")
                        op_free = p.n("v")
                        if j == 0:
                            p.op("v", lambda e, qh=qh: e.tensor_copy(
                                ROWS[0:1, 1024 * qh: 1024 * (qh + 1)], misc[0:1, :]), inc="v")
                        else:
                            p.op("v", lambda e, qh=qh: e.tensor_copy(
                                ROWS[64:65, 1024 * qh: 1024 * (qh + 1)], misc[64:65, :]), inc="v")
                            p.dma("g", ROWS[0:1, 2048 + 1024 * qh: 2048 + 1024 * (qh + 1)],
                                  ROWS[64:65, 1024 * qh: 1024 * (qh + 1)],
                                  waits=[("v", p.n("v"))], inc="odma")
                        misc_free = p.n("v")
            if upto_c1 and L == 1:
                l1_head_post(h)
                d_rows1 = nc.dram_tensor("d_rows1", [128, 4096], F32, kind="ExternalOutput").ap()
                d_f0bc = nc.dram_tensor("d_f0bc", [128, T], BF16, kind="ExternalOutput").ap()
                d_f1bc = nc.dram_tensor("d_f1bc", [128, T], BF16, kind="ExternalOutput").ap()
                d_oc = nc.dram_tensor("d_oc", [128, T], BF16, kind="ExternalOutput").ap()
                d_r1 = nc.dram_tensor("d_r1", [128, 16896], BF16, kind="ExternalOutput").ap()
                allw = [("t", p.n("t")), ("s", p.n("s")), ("v", p.n("v")),
                        ("syq", p.n("syq")), ("gq", p.n("gq")), ("odma", p.n("odma"))]
                p.dma("sy", d_rows1, ROWS[:], waits=allw, inc="ost")
                p.dma("sy", d_f0bc, asm[0][:], waits=allw, inc="ost")
                p.dma("sy", d_f1bc, asm[1][:], waits=allw, inc="ost")
                p.dma("sy", d_oc, oTs[0][:], waits=allw, inc="ost")
                p.dma("sy", d_r1, R1[:], waits=allw, inc="ost")
                return
            if c0:
                d_o0 = nc.dram_tensor("d_o0", [128, T], BF16, kind="ExternalOutput").ap()
                d_o1 = nc.dram_tensor("d_o1", [128, T], BF16, kind="ExternalOutput").ap()
                d_rows = nc.dram_tensor("d_rows", [128, 4096], F32, kind="ExternalOutput").ap()
                allw = [("t", p.n("t")), ("s", p.n("s")), ("v", p.n("v")),
                        ("syq", p.n("syq")), ("gq", p.n("gq")), ("odma", p.n("odma"))]
                p.dma("sy", d_o0, o01[0], waits=allw, inc="ost")
                p.dma("sy", d_o1, o01[1], waits=allw, inc="ost")
                p.dma("sy", d_rows, ROWS[:], waits=allw, inc="ost")
                return
            if L == 1:
                l1_head_post(h)
                op_free = p.n("v")
                misc_free = p.n("v")
                pa_free = 0
                sc_free = [0, 0]
                vpost = p.n("v")
                sc_free = [0, 0]
                self_gate = vpost
                p.op("s", lambda e: e.nop(), waits=[("v", vpost)], inc="s")
                sc_free = [p.n("s"), p.n("s")]

    def phase_D(L):
        pref = "l0_" if L == 0 else "l1_"
        oc_v = p.n("v")
        vprep = p.n("v")
        if L == 1:
            prevg = p.n("t")
            dprep_s = p.n("s")
            for hh in range(8):
                p.dma("g", ws[hh % 2][:].rearrange("p (a b) -> p a b", a=CH),
                      ins["l1_wo"][128 * hh: 128 * (hh + 1), :].rearrange("p (a b) -> p a b", a=CH),
                      waits=[("t", prevg)], inc="gq")
                ld = p.n("gq")
                for i in range(2):
                    sl = slice(512 * i, 512 * (i + 1))
                    p.op("t", lambda e, hh=hh, sl=sl: e.matmul(
                        PA[0:8, sl], lngW[:, 8 * hh: 8 * hh + 8],
                        ws[hh % 2][:, sl], start=(hh == 0), stop=(hh == 7)),
                        waits=[("gq", ld), ("v", oc_v), ("s", dprep_s)], inc="t")
                for i in range(2):
                    sl = slice(512 * i, 512 * (i + 1))
                    p.op("t", lambda e, hh=hh, sl=sl: e.matmul(
                        PB[0:1, sl], lnbb[:], ws[hh % 2][:, sl],
                        start=(hh == 0), stop=(hh == 7)), inc="t")
                prevg = p.n("t")
            p.op("v", lambda e: e.tensor_copy(ROWSB[0:8, 0:1024], PA[0:8, 0:1024]),
                 waits=[("t", p.n("t"))], inc="v")
            p.op("v", lambda e: e.tensor_copy(ROWS[0:1, 0:1024], PB[0:1, 0:1024]), inc="v")
            p.op("v", lambda e: e.tensor_copy(ROWSB[0:1, 1024:2048], ROWS[0:1, 0:1024]), inc="v")
            vc1 = p.n("v")
            for dc in range(CH):
                p.op("t", lambda e, dc=dc: e.transpose(
                    PBT128, ROWSB[0:1, 1024 + 128 * dc: 1024 + 128 * (dc + 1)], ident[0:1, 0:1]),
                    waits=[("v", vc1), ("v", p.n("v"))], inc="t")
                p.op("v", lambda e, dc=dc: e.tensor_copy(c1sb[:, dc: dc + 1], PBT128),
                     waits=[("t", p.n("t"))], inc="v")
            p.dma("sy", ROWS[0:8, 0:2048], hbm["m8"][:, :],
                  waits=[("syq", p.n("syq")), ("v", p.n("v")), ("s", p.n("s")),
                         ("t", p.n("t"))], inc="syq")
            p.op("v", lambda e: e.tensor_scalar(out=ROWSB[0:8, 2048:4096], in0=ROWS[0:8, 0:2048],
                                                scalar1=-1.0, scalar2=None, op0=ALU.mult),
                 waits=[("syq", p.n("syq"))], inc="v")
            vprep = p.n("v")
            if os.environ.get("DUMPD") == "1":
                d_rb = nc.dram_tensor("d_rb", [128, 4096], BF16, kind="ExternalOutput").ap()
                d_rw = nc.dram_tensor("d_rw", [128, 4096], F32, kind="ExternalOutput").ap()
                p.dma("sy", d_rb, ROWSB[:], waits=[("v", vprep), ("t", p.n("t")), ("s", p.n("s"))], inc="dbgst")
                p.dma("sy", d_rw, ROWS[:], waits=[("v", vprep), ("t", p.n("t")), ("s", p.n("s"))], inc="dbgst")
                p.op("sy", lambda e: e.nop(), waits=[("dbgst", p.n("dbgst"))], inc="sy2")

        prev_mm = [p.n("t"), p.n("t")]
        prev_ev = [p.n("s"), p.n("s")]
        for dc in range(CH):
            p.dma("g", ws[dc % 2][:].rearrange("p (a b) -> p a b", a=CH),
                  ins[pref + "wo"].rearrange("(a p) b -> p a b", p=128)[:, :, 128 * dc: 128 * (dc + 1)],
                  waits=[("t", prev_mm[dc % 2])], inc="gq")
            ld = p.n("gq")
            if L == 1:
                p.op("v", lambda e, dc=dc: e.tensor_scalar(
                    out=ws[dc % 2][:], in0=ws[dc % 2][:], scalar1=biases["l1_lng"][:],
                    scalar2=None, op0=ALU.mult), waits=[("gq", ld), ("v", vprep)], inc="v")
                wsr = [("v", p.n("v")), ("s", prev_ev[dc % 2])]
            else:
                wsr = [("gq", ld), ("s", prev_ev[dc % 2])]
            PP = PA if dc % 2 == 0 else PB
            for oc in range(CH):
                last = (oc == CH - 1) and (L == 0 or os.environ.get("DRANK8") == "0")
                mm4(PP, ws[dc % 2][:, 128 * oc: 128 * (oc + 1)], oTs[oc][:, :],
                    oc == 0, last, waits=wsr + [("v", oc_v)])
            if L == 1 and os.environ.get("DRANK8") != "0":
                mm4(PP, ROWSB[0:8, 128 * dc: 128 * (dc + 1)],
                    ROWSB[0:8, 2048:4096], False, True, waits=[("v", vprep)])
            prev_mm[dc % 2] = p.n("t")
            bb = biases["l0_bo"][:, dc: dc + 1] if L == 0 else c1sb[:, dc: dc + 1]
            p.op("v", lambda e, dc=dc, PP=PP, bb=bb: e.tensor_scalar(
                out=sT[dc], in0=PP[:, :], scalar1=bb, scalar2=None, op0=ALU.add),
                waits=[("t", prev_mm[dc % 2])], inc="v")
            p.op("s", lambda e: e.nop(), waits=[("v", p.n("v"))], inc="s")
            prev_ev[dc % 2] = p.n("s")
        return p.n("s"), p.n("t")

    def phase_E0(sD, tD, first_out):
        for c in range(CH):
            p.dma("g", R3[:, 0:1024], ins["l0_fw"][128 * c: 128 * (c + 1), :],
                  waits=[("v", p.n("v")), ("s", sD)], inc="gq")
            p.op("v", lambda e, c=c: e.reduce_max(
                wscale[:, c: c + 1].rearrange("p (a b) -> p a b", b=1),
                R3[:, 0:1024].rearrange("p (a b) -> p a b", b=1024),
                axis=AX.X, apply_absolute_value=True),
                waits=[("gq", p.n("gq"))], inc="v")
            p.op("v", lambda e, c=c: e.tensor_scalar(
                out=wscale[:, c: c + 1], in0=wscale[:, c: c + 1], scalar1=1e-8,
                scalar2=None, op0=ALU.max), inc="v")
            p.op("v", lambda e, c=c: e.reciprocal(wrecip[:, c: c + 1], wscale[:, c: c + 1]), inc="v")
            p.op("v", lambda e, c=c: e.tensor_scalar(
                out=wrecip[:, c: c + 1], in0=wrecip[:, c: c + 1], scalar1=127.0,
                scalar2=None, op0=ALU.mult), inc="v")
            p.op("v", lambda e, c=c: e.tensor_scalar(
                out=wscale[:, c: c + 1], in0=wscale[:, c: c + 1], scalar1=1.0 / 127.0,
                scalar2=None, op0=ALU.mult), inc="v")
        vq = p.n("v")
        for c in range(CH):
            p.op("t", lambda e, c=c: e.transpose(PBF1, wrecip[:, c: c + 1], identf[:]),
                 waits=[("v", vq), ("v", p.n("v")), ("t", tD)], inc="t")
            p.op("v", lambda e, c=c: e.tensor_copy(
                ROWS[0:1, 128 * c: 128 * (c + 1)], PBF1),
                waits=[("t", p.n("t"))], inc="v")
        vrow = p.n("v")
        for i in range(2):
            sl = slice(512 * i, 512 * (i + 1))
            p.op("t", lambda e, sl=sl: e.matmul(
                PB[:, sl], onesrF[0:1, :], ROWS[0:1, 0:1024][:, sl], start=True, stop=True),
                waits=([("v", vrow)] if i == 0 else ()), inc="t")
        p.op("v", lambda e: e.tensor_copy(R3[:, 1024:2048], PB[:, 0:1024]),
             waits=[("t", p.n("t"))], inc="v")
        vrw = p.n("v")
        for c in range(CH):
            p.dma("g", R3[:, 0:1024], ins["l0_fwt"][128 * c: 128 * (c + 1), :],
                  waits=[("v", p.n("v")), ("v", vrw)], inc="gq")
            p.op("v", lambda e: e.tensor_mul(R3[:, 2048:3072], R3[:, 0:1024], R3[:, 1024:2048]),
                 waits=[("gq", p.n("gq"))], inc="v")
            p.op("v", lambda e, c=c: e.tensor_scalar(
                out=AR[:, 1024 * c: 1024 * (c + 1)], in0=R3[:, 2048:3072],
                scalar1=MAGIC, scalar2=MAGIC, op0=ALU.add, op1=ALU.subtract), inc="v")
        vwq = p.n("v")

        tmaj = [AR[:, 8192:9216], AR[:, 9216:10240]]
        prev_red = [0, 0]
        for tt in range(NT):
            b = tt % 2
            p.op("sy", lambda e, b=b, tt=tt: e.dma_start_transpose(
                tmaj[b], hbm["xnb"][:, 128 * tt: 128 * (tt + 1)]),
                waits=[("v", prev_red[b]), ("odma", p.n("odma")), ("syq", setup_syq)],
                inc=f"tm{b}", inck=16)
            p.op("v", lambda e, b=b, tt=tt: e.reduce_max(
                IDX[:, tt: tt + 1].rearrange("p (a b) -> p a b", b=1),
                tmaj[b].rearrange("p (a b) -> p a b", b=1024),
                axis=AX.X, apply_absolute_value=True),
                waits=[(f"tm{b}", 16 * (tt // 2 + 1))], inc="v")
            prev_red[b] = p.n("v")
        p.op("v", lambda e: e.tensor_scalar(out=IDX[:, 0:16], in0=IDX[:, 0:16],
                                            scalar1=1e-8, scalar2=None, op0=ALU.max), inc="v")
        vam = p.n("v")
        for tt in range(NT):
            p.op("t", lambda e, tt=tt: e.transpose(PBF1, IDX[:, tt: tt + 1], identf[:]),
                 waits=[("v", vam), ("v", p.n("v"))], inc="t")
            p.op("v", lambda e, tt=tt: e.tensor_copy(ROWS[0:1, 128 * tt: 128 * (tt + 1)], PBF1),
                 waits=[("t", p.n("t"))], inc="v")
        p.op("v", lambda e: e.tensor_scalar(out=ROWSB[0:1, 0:2048], in0=ROWS[0:1, 0:2048],
                                            scalar1=1.0 / 127.0, scalar2=None, op0=ALU.mult), inc="v")
        p.op("v", lambda e: e.reciprocal(ROWS[0:1, 2048:4096], ROWS[0:1, 0:2048]), inc="v")
        p.op("v", lambda e: e.tensor_scalar(out=ROWS[0:1, 2048:4096], in0=ROWS[0:1, 2048:4096],
                                            scalar1=127.0, scalar2=None, op0=ALU.mult), inc="v")
        vxr = p.n("v")
        mm4(PA, onesrF[0:1, :], ROWS[0:1, 2048:4096], True, True, waits=[("v", vxr)])
        p.op("v", lambda e: e.tensor_copy(R3[:, 0:2048], PA[:, :]), waits=[("t", p.n("t"))], inc="v")
        mm4(PB, onesr[0:1, :], ROWSB[0:1, 0:2048], True, True, waits=[("v", vxr)])
        p.op("v", lambda e: e.tensor_copy(asm[0][:, :], PB[:, :]), waits=[("t", p.n("t"))], inc="v")
        vbc = p.n("v")
        for c in range(CH):
            p.dma("sy", oTs[c][:], hbm["xnb"][128 * c: 128 * (c + 1), :],
                  waits=[("t", tD)], inc="syq")
        ldx = p.n("syq")
        for c in range(CH):
            p.op("v", lambda e, c=c: e.tensor_mul(R3[:, 2048:4096], oTs[c][:], R3[:, 0:2048]),
                 waits=[("syq", ldx), ("v", vbc)], inc="v")
            p.op("v", lambda e, c=c: e.tensor_scalar(
                out=oTs[c][:], in0=R3[:, 2048:4096], scalar1=MAGIC, scalar2=MAGIC,
                op0=ALU.add, op1=ALU.subtract), inc="v")
        vxq = p.n("v")

        prev_mm = [0, 0]
        prev_ev = [0, 0]
        for oc in range(CH):
            PP = PA if oc % 2 == 0 else PB
            for c in range(CH):
                mm4(PP, AR[:, 1024 * c + 128 * oc: 1024 * c + 128 * (oc + 1)],
                    oTs[c][:, :], c == 0, c == CH - 1,
                    waits=[("v", max(vxq, vwq)), ("t", prev_mm[oc % 2]),
                           ("v", prev_ev[oc % 2])])
            prev_mm[oc % 2] = p.n("t")
            p.op("v", lambda e, oc=oc, PP=PP: e.scalar_tensor_tensor(
                out=asm[1][:, :], in0=PP[:, :], scalar=wscale[:, oc: oc + 1],
                in1=asm[0][:, :], op0=ALU.mult, op1=ALU.mult),
                waits=[("t", prev_mm[oc % 2]), ("s", p.n("s"))], inc="v")
            prev_ev[oc % 2] = p.n("v")
            p.op("s", lambda e, oc=oc: e.activation(
                out=AR[:, 8192:10240], in_=asm[1][:, :], func=AF.Silu,
                bias=biases["l0_fb"][:, oc: oc + 1]), waits=[("v", p.n("v"))], inc="s")
            sg = p.n("s")
            if first_out:
                fout = R3[:, 0:2048] if oc % 2 == 0 else ROWS[:, 0:2048]
                p.op("v", lambda e, oc=oc, fout=fout: e.tensor_mul(fout, sT[oc], AR[:, 8192:10240]),
                     waits=[("s", sg), ("ost", max(0, p.n("ost") - 16))], inc="v")
                p.op("v", lambda e, oc=oc, fout=fout: e.tensor_add(fout, fout, xT[oc][:]),
                     inc="v")
                p.dma("sy", outT[128 * oc: 128 * (oc + 1), :], fout,
                      waits=[("v", p.n("v"))], inc="ost")
                p.op("sy", lambda e: e.nop(), waits=[("ost", p.n("ost"))], inc="sy2")
            else:
                p.op("v", lambda e, oc=oc: e.tensor_mul(R3b[:, 4096:6144], sT[oc], AR[:, 8192:10240]),
                     waits=[("s", sg)], inc="v")
                p.op("v", lambda e, oc=oc: e.tensor_add(xT[oc][:], xT[oc][:], R3b[:, 4096:6144]),
                     inc="v")

    def phase_E1(sD, tD, dump_dacc=False):
        if dump_dacc:
            d_st3 = nc.dram_tensor("d_st3", [D, T], BF16, kind="ExternalOutput").ap()
            for c in range(CH):
                p.dma("sy", d_st3[128 * c: 128 * (c + 1), :], sT[c],
                      waits=[("s", sD), ("t", tD)], inc="dbgst")
            p.op("sy", lambda e: e.nop(), waits=[("dbgst", p.n("dbgst"))], inc="sy2")
        wd2 = [AR[:, 0:3072], AR[:, 3072:6144]]
        ws2 = [AR[:, 6144:7168], AR[:, 7168:8192]]
        gsil = AR[:, 8192:8704]
        for c in range(CH):
            p.dma("sy", oTs[c][:], hbm["xnb"][128 * c: 128 * (c + 1), :],
                  waits=[("t", tD)], inc="syq")
        ldx = p.n("syq")
        e1_v = p.n("v")
        ev = {}
        for grp in range(4):
            for ig in range(6):
                hc = 6 * grp + ig
                p.dma("g", ws[hc % 2][:].rearrange("p (a b) -> p a b", a=CH),
                      ins["l1_wgt"].rearrange("(a p) b -> p a b", p=128)[:, :, 128 * hc: 128 * (hc + 1)],
                      waits=[("t", ev.get(("mm", hc - 2), tD))], inc="gq")
                p.dma("g", ws2[hc % 2].rearrange("p (a b) -> p a b", a=CH),
                      ins["l1_wut"].rearrange("(a p) b -> p a b", p=128)[:, :, 128 * hc: 128 * (hc + 1)],
                      inc="gq")
                ld = p.n("gq")
                w0 = [("gq", ld), ("syq", ldx), ("s", sD),
                      ("v", max(e1_v, ev.get(("hv", hc - 1), 0))),
                      ("s", ev.get(("hs", hc - 1), 0))]
                for c in range(CH):
                    for qq in range(4):
                        sl = slice(512 * qq, 512 * (qq + 1))
                        p.op("t", lambda e, hc=hc, c=c, sl=sl: e.matmul(
                            PA[:, sl], ws[hc % 2][:, 128 * c: 128 * (c + 1)],
                            oTs[c][:, sl], start=(c == 0), stop=(c == CH - 1)),
                            waits=(w0 if (c == 0 and qq == 0) else ()), inc="t")
                    for qq in range(4):
                        sl = slice(512 * qq, 512 * (qq + 1))
                        p.op("t", lambda e, hc=hc, c=c, sl=sl: e.matmul(
                            PB[:, sl], ws2[hc % 2][:, 128 * c: 128 * (c + 1)],
                            oTs[c][:, sl], start=(c == 0), stop=(c == CH - 1)), inc="t")
                ev[("mm", hc)] = p.n("t")
                for qq in range(4):
                    sl = slice(512 * qq, 512 * (qq + 1))
                    p.op("s", lambda e, sl=sl: e.activation(
                        out=gsil, in_=PA[:, sl], func=AF.Silu, scale=1.0 / 127.0),
                        waits=[("t", ev[("mm", hc)]), ("v", p.n("v"))], inc="s")
                    p.op("v", lambda e, ig=ig, sl=sl: e.scalar_tensor_tensor(
                        out=hsl[ig][:, sl], in0=PB[:, sl], scalar=1.0 / 127.0,
                        in1=gsil, op0=ALU.mult, op1=ALU.mult),
                        waits=[("s", p.n("s"))], inc="v")
                ev[("hv", hc)] = p.n("v")
                ev[("hs", hc)] = p.n("s")
            hv_grp = p.n("v")
            prev_d = [ev[("mm", 6 * grp + 5)], ev[("mm", 6 * grp + 5)]]
            for dc in range(CH):
                p.dma("g", wd2[dc % 2].rearrange("p (a b) -> p a b", a=24),
                      ins["l1_wdt"].rearrange("(a p) b -> p a b", p=128)[:, :, 128 * dc: 128 * (dc + 1)],
                      waits=[("t", prev_d[dc % 2])], inc="gq")
                ldd = p.n("gq")
                PP = PA if dc % 2 == 0 else PB
                for ig in range(6):
                    hc = 6 * grp + ig
                    mm4(PP, wd2[dc % 2][:, 128 * hc: 128 * (hc + 1)],
                        hsl[ig][:, :], ig == 0, ig == 5,
                        waits=[("gq", ldd), ("v", hv_grp), ("v", ev.get(("dacc", dc), 0))])
                prev_d[dc % 2] = p.n("t")
                if grp == 0:
                    p.op("v", lambda e, dc=dc, PP=PP: e.tensor_copy(dacc[dc], PP[:, :]),
                         waits=[("t", prev_d[dc % 2])], inc="v")
                else:
                    p.op("v", lambda e, dc=dc, PP=PP: e.tensor_add(dacc[dc], dacc[dc], PP[:, :]),
                         waits=[("t", prev_d[dc % 2])], inc="v")
                ev[("dacc", dc)] = p.n("v")
            ev[("hv", 6 * grp + 5)] = p.n("v")
        vdacc = p.n("v")
        if dump_dacc:
            d_da = nc.dram_tensor("d_da", [D, T], BF16, kind="ExternalOutput").ap()
            d_st2 = nc.dram_tensor("d_st2", [D, T], BF16, kind="ExternalOutput").ap()
            allw = [("t", p.n("t")), ("s", p.n("s")), ("v", p.n("v")),
                    ("syq", p.n("syq")), ("gq", p.n("gq"))]
            d_ots2 = nc.dram_tensor("d_ots2", [D, T], BF16, kind="ExternalOutput").ap()
            for c in range(CH):
                p.dma("sy", d_da[128 * c: 128 * (c + 1), :], dacc[c], waits=allw, inc="dbgst")
                p.dma("sy", d_st2[128 * c: 128 * (c + 1), :], sT[c], waits=allw, inc="dbgst")
                p.dma("sy", d_ots2[128 * c: 128 * (c + 1), :], oTs[c][:], waits=allw, inc="dbgst")
            p.op("sy", lambda e: e.nop(), waits=[("dbgst", p.n("dbgst"))], inc="sy2")
        ost0 = p.n("ost")
        if dump_dacc:
            d_g1 = nc.dram_tensor("d_g1", [D, T], BF16, kind="ExternalOutput").ap()
        for dc in range(CH):
            fout = R3[:, 2048:4096] if dc % 2 == 0 else ROWS[:, 0:2048]
            p.op("s", lambda e, dc=dc: e.activation(
                out=st_sc[0][:, :], in_=dacc[dc], func=AF.Silu, scale=1.0 / 127.0),
                waits=[("v", vdacc), ("v", p.n("v"))], inc="s")
            sg = p.n("s")
            p.op("v", lambda e, dc=dc, fout=fout: e.tensor_mul(fout, sT[dc], st_sc[0][:, :]),
                 waits=[("s", sg), ("ost", max(ost0, p.n("ost") - 16))], inc="v")
            p.op("v", lambda e, dc=dc, fout=fout: e.tensor_add(fout, fout, xT[dc][:]), inc="v")
            if dump_dacc:
                p.dma("sy", d_g1[128 * dc: 128 * (dc + 1), :], st_sc[0][:, :],
                      waits=[("v", p.n("v"))], inc="ost")
                p.op("sy", lambda e: e.nop(), waits=[("ost", p.n("ost"))], inc="sy2")
            p.dma("sy", outT[128 * dc: 128 * (dc + 1), :], fout,
                  waits=[("v", p.n("v"))], inc="ost")

    done = False
    for li, L in enumerate(layers):
        first = (li == 0)
        p.counts.setdefault("sy2", 0)
        p.counts.setdefault("odma", 0)
        if os.environ.get("DUMPX1") == "1" and li == 1:
            d_x1 = nc.dram_tensor("d_x1", [D, T], BF16, kind="ExternalOutput").ap()
            bw = [("v", p.n("v")), ("s", p.n("s")), ("t", p.n("t"))]
            for c in range(CH):
                p.dma("sy", d_x1[128 * c: 128 * (c + 1), :], xT[c][:], waits=bw, inc="dbgst")
            p.op("sy", lambda e: e.nop(), waits=[("dbgst", p.n("dbgst"))], inc="sy2")
        xn_v = phase_A(L, first)
        if upto == "A":
            done = True
            break
        phase_B(L, xn_v, sub=(upto if upto and upto.startswith("B") else None))
        if dbg:
            kr = KR[L]
            d_xn = nc.dram_tensor("d_xn", [D, T], BF16, kind="ExternalOutput").ap()
            d_idx = nc.dram_tensor("d_idx", [128, 512], F32, kind="ExternalOutput").ap()
            d_qpp = nc.dram_tensor("d_qpp", [16 * kr, T], BF16, kind="ExternalOutput").ap()
            d_kpp = nc.dram_tensor("d_kpp", [16 * kr, T], BF16, kind="ExternalOutput").ap()
            d_vp = nc.dram_tensor("d_vp", [T, VW[L]], BF16, kind="ExternalOutput").ap()
            allw = [("t", p.n("t")), ("s", p.n("s")), ("v", p.n("v")),
                    ("syq", p.n("syq")), ("gq", p.n("gq"))]
            p.dma("sy", d_xn, hbm["xnb"][:, :], waits=allw, inc="ost")
            p.dma("sy", d_idx, IDX[:], waits=allw, inc="ost")
            p.dma("sy", d_qpp, hbm[f"qpp{L}"][:, :], waits=allw, inc="ost")
            p.dma("sy", d_kpp, hbm[f"kpp{L}"][:, :], waits=allw, inc="ost")
            p.dma("sy", d_vp, hbm[f"vp{L}"][:, :], waits=allw, inc="ost")
        if upto and upto.startswith("B"):
            done = True
            break
        phase_C(L, c0=(upto == "C0"), upto_c1=(upto == "C1"))
        if upto == "C1":
            done = True
            break
        if upto == "C0":
            done = True
            break
        if dbg and upto == "C":
            d_ots = nc.dram_tensor("d_ots", [D, T], BF16, kind="ExternalOutput").ap()
            allw = [("t", p.n("t")), ("s", p.n("s")), ("v", p.n("v")),
                    ("syq", p.n("syq")), ("gq", p.n("gq")), ("odma", p.n("odma"))]
            for c in range(CH):
                p.dma("sy", d_ots[128 * c: 128 * (c + 1), :], oTs[c][:], waits=allw, inc="ost")
            if L == 1:
                d_m8 = nc.dram_tensor("d_m8", [8, T], F32, kind="ExternalOutput").ap()
                p.dma("sy", d_m8, hbm["m8"][:, :], waits=allw, inc="ost")
        if upto == "C":
            done = True
            break
        sD, tD = phase_D(L)
        if dbg and upto == "D":
            d_st = nc.dram_tensor("d_st", [D, T], BF16, kind="ExternalOutput").ap()
            allw = [("t", p.n("t")), ("s", p.n("s")), ("v", p.n("v")),
                    ("syq", p.n("syq")), ("gq", p.n("gq")), ("odma", p.n("odma"))]
            for c in range(CH):
                p.dma("sy", d_st[128 * c: 128 * (c + 1), :], sT[c], waits=allw, inc="ost")
        if upto == "D":
            done = True
            break
        if L == 0:
            phase_E0(sD, tD, first_out=(L == layers[-1]))
        else:
            import os as _os2
            phase_E1(sD, tD, dump_dacc=((dbg and upto == "E") or _os2.environ.get("DUMPF") == "1"))
        if upto == "E":
            done = True
            break
    if done:
        for c in range(CH):
            p.dma("g", outT[128 * c: 128 * (c + 1), :], xT[c][:],
                  waits=[("t", p.n("t")), ("s", p.n("s")), ("v", p.n("v")),
                         ("syq", p.n("syq")), ("gq", p.n("gq")),
                         ("odma", p.n("odma"))] + (
                      [(f"cck{layers[0]}", 1), (f"ccv{layers[0]}", 1)]
                      if p.n(f"cck{layers[0]}") else []),
                  inc="ost")
    p.op("sy", lambda e: e.nop(), waits=[("ost", p.n("ost"))], inc="sy2")
    p.op("g", lambda e: e.nop(), waits=[("ost", p.n("ost"))], inc="g")

    sems = {name: nc.alloc_semaphore(name) for name in p.counts}
    with nc.Block() as block:
        block.tensor(lambda e: p.emit("t", e, sems))
        block.scalar(lambda e: p.emit("s", e, sems))
        block.vector(lambda e: p.emit("v", e, sems))
        block.sync(lambda e: p.emit("sy", e, sems))
        block.gpsimd(lambda e: p.emit("g", e, sems))
    return nc


def _prep_inputs(inputs):
    ac = np.ascontiguousarray
    f32 = np.float32

    def colmaj(v):
        return ac(np.asarray(v, f32).reshape(CH, 128).T)

    base0 = base1 = None
    base0 = {
        "l0_wq": ac(np.asarray(inputs["L0_attn_Wq"], f32)),
        "l0_wk": ac(np.asarray(inputs["L0_attn_Wk"], f32)),
        "l0_wv": ac(np.asarray(inputs["L0_attn_Wv"], f32)),
        "l0_wo": ac(np.asarray(inputs["L0_attn_Wo"], f32)),
        "l0_fw": ac(np.asarray(inputs["L0_ffn_weight"], f32)),
        "l0_fwt": ac(np.asarray(inputs["L0_ffn_weight"], f32).T),
        "l0_bq": colmaj(inputs["L0_attn_bq"]),
        "l0_bk": colmaj(inputs["L0_attn_bk"]),
        "l0_bv": colmaj(inputs["L0_attn_bv"]),
        "l0_bo": colmaj(inputs["L0_attn_bo"]),
        "l0_fb": colmaj(inputs["L0_ffn_bias"]),
        "l0_rt": ac(np.asarray(inputs["L0_attn_R"], f32).T),
    }
    base1 = {
        "l1_wq": ac(np.asarray(inputs["L1_attn_Wq"], f32)),
        "l1_wk": ac(np.asarray(inputs["L1_attn_Wk"], f32)),
        "l1_wv": ac(np.asarray(inputs["L1_attn_Wv"], f32)),
        "l1_wo": ac(np.asarray(inputs["L1_attn_Wo"], f32)),
        "l1_lq1": ac(np.asarray(inputs["L1_attn_lq1"], f32).reshape(1, 64)),
        "l1_lk1": ac(np.asarray(inputs["L1_attn_lk1"], f32).reshape(1, 64)),
        "l1_lq2": ac(np.asarray(inputs["L1_attn_lq2"], f32).reshape(1, 64)),
        "l1_lk2": ac(np.asarray(inputs["L1_attn_lk2"], f32).reshape(1, 64)),
        "l1_lng": ac(np.asarray(inputs["L1_attn_ln_g"], f32).reshape(128, 1)),
        "l1_lnb": ac(np.asarray(inputs["L1_attn_ln_b"], f32).reshape(128, 1)),
        "l1_wgt": ac(np.asarray(inputs["L1_ffn_w_gate"], np.int32).T),
        "l1_wut": ac(np.asarray(inputs["L1_ffn_w_up"], np.int32).T),
        "l1_wdt": ac(np.asarray(inputs["L1_ffn_w_down"], np.int32).T),
    }
    x = np.asarray(inputs["L0_attn_x"], f32)
    xsh = []
    for c in range(8):
        b, half = c // 2, c % 2
        xsh.append(ac(x[b, 2048 * half: 2048 * (half + 1), :].T))
    return base0, base1, xsh


def kernel(**inputs):
    base0, base1, xsh = _prep_inputs(inputs)
    if FUSED:
        nc = build_program([0, 1])
        in_maps = [dict(base0, **base1, xoT=xsh[c]) for c in range(8)]
        res = run_bass_kernel_spmd(nc, in_maps, core_ids=list(range(8)))
        outs = [res.results[c]["outT"] for c in range(8)]
    else:
        nc0 = build_program([0])
        in_maps = [dict(base0, xoT=xsh[c]) for c in range(8)]
        r0 = run_bass_kernel_spmd(nc0, in_maps, core_ids=list(range(8)))
        x1 = [np.ascontiguousarray(np.asarray(r0.results[c]["outT"], np.float32))
              for c in range(8)]
        nc1 = build_program([1])
        in_maps = [dict(base1, xoT=x1[c]) for c in range(8)]
        r1 = run_bass_kernel_spmd(nc1, in_maps, core_ids=list(range(8)))
        outs = [r1.results[c]["outT"] for c in range(8)]
    out = np.empty((4, 4096, 1024), np.float32)
    for c in range(8):
        b, half = c // 2, c % 2
        out[b, 2048 * half: 2048 * (half + 1), :] = np.asarray(outs[c], np.float32).T
    return out


# version 07 (final)
# speedup vs baseline: 1.0050x; 1.0004x over previous
import math
import os
import numpy as np
import ml_dtypes
import concourse.bass as bass
import concourse.mybir as mybir
from concourse.bass_utils import run_bass_kernel_spmd

F32 = mybir.dt.float32
BF16 = mybir.dt.bfloat16
I32 = mybir.dt.int32
AF = mybir.ActivationFunctionType
ALU = mybir.AluOpType
AX = mybir.AxisListType

D = 1024
T = 2048
CH = 8
NT = 16
MAGIC = float(1.5 * 2 ** 23)
CPEN = 400.0
LAM_INIT = 0.8 - 0.6 * math.exp(-0.3 * 12)
KR = {0: 68, 1: 65}
VW = {0: 1040, 1: 1024}

FUSED = True


class Prog:

    def __init__(self):
        self.items = {k: [] for k in ("t", "s", "v", "sy", "g")}
        self.counts = {}

    def n(self, sem):
        return self.counts.get(sem, 0)

    def op(self, eng, fn, waits=(), inc=None, inck=1):
        w = [(s, v) for (s, v) in waits if v > 0]
        self.items[eng].append((fn, w, inc, inck))
        if inc is not None:
            self.counts[inc] = self.counts.get(inc, 0) + inck
        for s, _ in w:
            self.counts.setdefault(s, 0)

    def dma(self, eng, out, in_, waits=(), inc=None, **kw):
        assert inc is not None
        self.op(eng, lambda e: e.dma_start(out=out, in_=in_, **kw), waits, inc, 16)

    def emit(self, eng_name, e, sems):
        self_edge = eng_name in ("v",) and eng_name in sems
        waited = {}
        own = 0
        for fn, waits, inc, inck in self.items[eng_name]:
            if self_edge and own > 0 and waited.get(eng_name, 0) < own:
                e.wait_ge(sems[eng_name], own)
                waited[eng_name] = own
            for snm, val in waits:
                if waited.get(snm, 0) >= val:
                    continue
                e.wait_ge(sems[snm], val)
                waited[snm] = val
            instr = fn(e)
            if inc is not None:
                instr.then_inc(sems[inc], inck)
                if inc == eng_name:
                    own += inck


def build_program(layers, upto=None, dbg=False):
    nc = bass.Bass("TRN2", target_bir_lowering=False, debug=False)
    p = Prog()

    xoT = nc.dram_tensor("xoT", [D, T], F32, kind="ExternalInput").ap()
    outT = nc.dram_tensor("outT", [D, T], F32, kind="ExternalOutput").ap()
    ins = {}

    def ext(name, shape, dtype=F32):
        ins[name] = nc.dram_tensor(name, shape, dtype, kind="ExternalInput").ap()

    if 0 in layers:
        for w in ("l0_wq", "l0_wk", "l0_wv", "l0_wo", "l0_fw", "l0_fwt"):
            ext(w, [D, D])
        for b in ("l0_bq", "l0_bk", "l0_bv", "l0_bo", "l0_fb"):
            ext(b, [128, CH])
        ext("l0_rt", [64, 8])
    if 1 in layers:
        for w in ("l1_wq", "l1_wk", "l1_wv", "l1_wo"):
            ext(w, [D, D])
        for b in ("l1_lq1", "l1_lk1", "l1_lq2", "l1_lk2"):
            ext(b, [1, 64])
        ext("l1_lng", [128, 1])
        ext("l1_lnb", [128, 1])
        ext("l1_wgt", [D, 3072], I32)
        ext("l1_wut", [D, 3072], I32)
        ext("l1_wdt", [3072, D], I32)

    hbm = {"xnb": nc.dram_tensor("xnb", [D, T], BF16, kind="Internal")}
    for L in layers:
        kr, vw = KR[L], VW[L]
        hbm[f"qpp{L}"] = nc.dram_tensor(f"qpp{L}", [16 * kr, T], BF16, kind="Internal")
        hbm[f"kpp{L}"] = nc.dram_tensor(f"kpp{L}", [16 * kr, T], BF16, kind="Internal")
        hbm[f"kppf{L}"] = nc.dram_tensor(f"kppf{L}", [32 * kr, T], BF16, kind="Internal")
        hbm[f"vp{L}"] = nc.dram_tensor(f"vp{L}", [T, vw], BF16, kind="Internal")
        hbm[f"vpf{L}"] = nc.dram_tensor(f"vpf{L}", [2 * T, vw], BF16, kind="Internal")
    if 1 in layers:
        hbm["m8"] = nc.dram_tensor("m8", [8, T], F32, kind="Internal")

    A = nc.alloc_sbuf_tensor
    xT = [A(f"xT{c}", [128, T], BF16) for c in range(CH)]
    oTs = [A(f"oTs{c}", [128, T], BF16) for c in range(CH)]
    R1 = A("R1", [128, 16896], BF16)
    R2 = A("R2", [128, 16384], BF16)
    R3 = A("R3", [128, 4096], F32)
    R3b = R3.bitcast(BF16)
    asm = [A(f"asm{i}", [128, T], BF16) for i in range(2)]
    ws = [A(f"ws{i}", [128, 1024], BF16) for i in range(2)]
    AR = A("AR", [128, 10240], BF16)
    ROWS = A("ROWS", [128, 4096], F32)
    ROWSB = A("ROWSB", [128, 4096], BF16)
    RB32 = ROWSB.bitcast(F32)
    IDX = A("IDX", [128, 512], F32)
    IDXB = A("IDXB", [128, 16], BF16)
    ident = A("ident", [128, 128], BF16)
    idrow = A("idrow", [128, 128], F32)
    pidx = A("pidx", [128, 1], F32)
    identf = A("identf", [128, 128], F32)
    ones1 = A("ones1", [128, 1], BF16)
    onesr = A("onesr", [128, 128], BF16)
    onesrF = A("onesrF", [128, 128], F32)
    onesW = A("onesW", [128, 128], BF16)
    iota16 = A("iota16", [128, 16], F32)
    epsb = A("epsb", [128, 1], F32)
    wscale = A("wscale", [128, CH], F32)
    wrecip = A("wrecip", [128, CH], F32)
    biases = {}
    lrows = {}
    if 0 in layers:
        for b in ("l0_bq", "l0_bk", "l0_bv", "l0_bo", "l0_fb"):
            biases[b] = A("sb_" + b, [128, CH], F32)
        rtsb = A("rtsb", [128, 8], BF16)
    if 1 in layers:
        for b in ("l1_lq1", "l1_lk1", "l1_lq2", "l1_lk2"):
            lrows[b] = A("sb_" + b, [1, 64], F32)
        biases["l1_lng"] = A("sb_l1_lng", [128, 1], F32)
        biases["l1_lnb"] = A("sb_l1_lnb", [128, 1], F32)
        lamv = A("lam", [1, 1], F32)
        lngb = A("lngb", [128, 1], BF16)
        lnbb = A("lnbb", [128, 1], BF16)
        lngW = A("lngW", [128, 64], BF16)
        c1sb = A("c1sb", [128, CH], F32)

    QT = [R1[:, 2048 * c: 2048 * (c + 1)] for c in range(CH)]
    dacc = QT

    def vst(L, buf):
        w = VW[L] if L == 0 else 128
        return R1[:, 8448 * buf: 8448 * buf + 8 * w].rearrange("p (a b) -> p a b", a=8)

    xnT = [R2[:, 2048 * c: 2048 * (c + 1)] for c in range(CH)]
    sT = xnT
    kst = [R2[0:128, 0:4096], R2[0:128, 4096:8192]]
    qst = [R2[0:128, 8192:9216], R2[0:128, 9216:10240]]
    expb = [R2[:, 10240:11264], R2[:, 11264:12288]]
    o01 = [R2[:, 12288:14336], R2[:, 14336:16384]]

    mubc = R3b[:, 0:2048]
    rstdbc = R3b[:, 2048:4096]
    st_sc = [R3b[:, 0:2048], R3b[:, 2048:4096], R3b[:, 4096:6144]]
    rbcs = [R3b[:, 0:1024], R3b[:, 1024:2048]]
    l0tsl = R3b[:, 4096:6144]
    hsl = [R3b[:, 2048 * i: 2048 * (i + 1)] for i in range(4)] + \
          [asm[0][:, :], asm[1][:, :]]

    PA = nc.alloc_psum_tensor("PA", [128, 2048], F32)
    PB = nc.alloc_psum_tensor("PB", [128, 2048], F32)
    PBT16 = PB[0:16, 1152:1216].bitcast(BF16)
    PBT128 = PB[0:128, 1216:1217].bitcast(BF16)[:, 0:1]
    PBF1 = PB[0:1, 1280:1408]

    def mm4(PPcols, lhsT, rhs, start, stop, waits=(), width=2048):
        n = width // 512
        for i in range(n):
            sl = slice(512 * i, 512 * (i + 1))
            p.op("t", lambda e, sl=sl: e.matmul(PPcols[:, sl], lhsT, rhs[:, sl],
                                                start=start, stop=stop),
                 waits=(waits if i == 0 else ()), inc="t")

    p.op("g", lambda e: e.iota(idrow[:], pattern=[[1, 128]], base=0,
                               channel_multiplier=0, allow_small_or_imprecise_dtypes=True), inc="g")
    p.op("g", lambda e: e.iota(pidx[:], pattern=[[0, 1]], base=0,
                               channel_multiplier=1, allow_small_or_imprecise_dtypes=True), inc="g")
    p.op("g", lambda e: e.iota(iota16[:], pattern=[[0, 2], [1, 8]], base=0,
                               channel_multiplier=0, allow_small_or_imprecise_dtypes=True), inc="g")
    p.op("v", lambda e: e.tensor_scalar(out=identf[:], in0=idrow[:], scalar1=pidx[:],
                                        scalar2=None, op0=ALU.is_equal),
         waits=[("g", 2)], inc="v")
    p.op("v", lambda e: e.tensor_copy(ident[:], identf[:]), inc="v")
    p.op("v", lambda e: e.memset(ones1[:], 1.0), inc="v")
    p.op("v", lambda e: e.memset(onesr[:], 1.0), inc="v")
    p.op("v", lambda e: e.memset(onesrF[:], 1.0), inc="v")
    p.op("v", lambda e: e.memset(onesW[:], 0.0), inc="v")
    p.op("v", lambda e: e.memset(epsb[:], 1e-5), inc="v")
    for c in range(CH):
        p.op("v", lambda e, c=c: e.memset(onesW[0:64, 16 * c + 2 * c: 16 * c + 2 * c + 1], 1.0), inc="v")
        p.op("v", lambda e, c=c: e.memset(onesW[64:128, 16 * c + 2 * c + 1: 16 * c + 2 * c + 2], 1.0), inc="v")
    for b, sb in biases.items():
        p.dma("sy", sb[:], ins[b], inc="syq")
    for b, sb in lrows.items():
        p.dma("sy", sb[:], ins[b], inc="syq")
    if 0 in layers:
        p.dma("g", rtsb[0:64, :], ins["l0_rt"], inc="gq")
        p.dma("g", rtsb[64:128, :], ins["l0_rt"], inc="gq")
    for c in range(CH):
        p.dma("g", xT[c][:], xoT[128 * c: 128 * (c + 1), :], inc="gq")
    if 1 in layers:
        p.op("v", lambda e: e.tensor_copy(lngb[:], biases["l1_lng"][:]),
             waits=[("syq", p.n("syq"))], inc="v")
        p.op("v", lambda e: e.tensor_copy(lnbb[:], biases["l1_lnb"][:]), inc="v")
        p.op("v", lambda e: e.memset(lngW[:], 0.0), inc="v")
        for h in range(8):
            p.op("v", lambda e, h=h: e.tensor_copy(lngW[:, 8 * h + h: 8 * h + h + 1], lngb[:]), inc="v")
        p.op("v", lambda e: e.tensor_mul(ROWS[0:1, 0:64], lrows["l1_lq1"][:], lrows["l1_lk1"][:]), inc="v")
        p.op("v", lambda e: e.tensor_mul(ROWS[0:1, 64:128], lrows["l1_lq2"][:], lrows["l1_lk2"][:]), inc="v")
        p.op("v", lambda e: e.reduce_sum(ROWS[0:1, 128:129], ROWS[0:1, 0:64], axis=AX.X), inc="v")
        p.op("v", lambda e: e.reduce_sum(ROWS[0:1, 129:130], ROWS[0:1, 64:128], axis=AX.X), inc="v")
        p.op("s", lambda e: e.activation(out=ROWS[0:1, 130:131], in_=ROWS[0:1, 128:129], func=AF.Exp),
             waits=[("v", p.n("v"))], inc="s")
        p.op("s", lambda e: e.activation(out=ROWS[0:1, 131:132], in_=ROWS[0:1, 129:130], func=AF.Exp), inc="s")
        p.op("v", lambda e: e.tensor_sub(ROWS[0:1, 132:133], ROWS[0:1, 130:131], ROWS[0:1, 131:132]),
             waits=[("s", p.n("s"))], inc="v")
        p.op("v", lambda e: e.tensor_scalar(out=lamv[:], in0=ROWS[0:1, 132:133], scalar1=LAM_INIT,
                                            scalar2=None, op0=ALU.add), inc="v")
    setup_gq = p.n("gq")
    setup_syq = p.n("syq")
    setup_v = p.n("v")
    setup_g = p.n("g")

    def phase_A(L, first):
        if first:
            xw = [("gq", setup_gq), ("syq", setup_syq), ("v", setup_v), ("g", setup_g)]
        else:
            xw = [("v", p.n("v")), ("s", p.n("s")), ("t", p.n("t")),
                  ("syq", p.n("syq")), ("gq", p.n("gq")), ("odma", p.n("odma"))]
        amm = []
        for c in range(CH):
            buf = asm[c % 2]
            w = list(xw)
            if c >= 2:
                w.append(("t", amm[c - 2]))
            p.op("v", lambda e, b=buf, c=c: e.tensor_mul(b[:], xT[c][:], xT[c][:]),
                 waits=w, inc="v")
            vsq = p.n("v")
            mm4(PA[0:1, :], ones1[:], xT[c][:], c == 0, c == CH - 1, waits=xw)
            mm4(PB[0:1, :], ones1[:], buf[:], c == 0, c == CH - 1, waits=[("v", vsq)])
            amm.append(p.n("t"))
        ts = p.n("t")
        p.op("v", lambda e: e.tensor_scalar(out=ROWS[0:1, 0:2048], in0=PA[0:1, :],
                                            scalar1=1.0 / D, scalar2=None, op0=ALU.mult),
             waits=[("t", ts)], inc="v")
        p.op("v", lambda e: e.tensor_scalar(out=ROWS[0:1, 2048:4096], in0=PB[0:1, :],
                                            scalar1=1.0 / D, scalar2=None, op0=ALU.mult), inc="v")
        p.op("v", lambda e: e.tensor_mul(RB32[0:1, 0:2048], ROWS[0:1, 0:2048], ROWS[0:1, 0:2048]), inc="v")
        p.op("v", lambda e: e.tensor_sub(ROWS[0:1, 2048:4096], ROWS[0:1, 2048:4096],
                                         RB32[0:1, 0:2048]), inc="v")
        p.op("s", lambda e: e.activation(out=RB32[0:1, 0:2048], in_=ROWS[0:1, 2048:4096],
                                         func=AF.Sqrt, bias=epsb[0:1, :]), waits=[("v", p.n("v"))], inc="s")
        p.op("v", lambda e: e.reciprocal(ROWS[0:1, 2048:4096], RB32[0:1, 0:2048]),
             waits=[("s", p.n("s"))], inc="v")
        p.op("v", lambda e: e.tensor_copy(asm[0][0:1, 0:2048], ROWS[0:1, 0:2048]), inc="v")
        p.op("v", lambda e: e.tensor_copy(asm[1][0:1, 0:2048], ROWS[0:1, 2048:4096]), inc="v")
        vr = p.n("v")
        mm4(PA, onesr[0:1, :], asm[0][0:1, 0:2048], True, True, waits=[("v", vr)])
        mm4(PB, onesr[0:1, :], asm[1][0:1, 0:2048], True, True)
        tb = p.n("t")
        p.op("v", lambda e: e.tensor_copy(mubc[:], PA[:, :]), waits=[("t", tb)], inc="v")
        p.op("v", lambda e: e.tensor_copy(rstdbc[:], PB[:, :]), inc="v")
        vb = p.n("v")
        for c in range(CH):
            p.op("v", lambda e, c=c: e.tensor_sub(asm[c % 2][:], xT[c][:], mubc[:]),
                 waits=[("v", vb)], inc="v")
            p.op("v", lambda e, c=c: e.tensor_mul(xnT[c], asm[c % 2][:], rstdbc[:]), inc="v")
            p.dma("sy", hbm["xnb"][128 * c: 128 * (c + 1), :], xnT[c],
                  waits=[("v", p.n("v"))], inc="syq")
        return p.n("v")

    def proj(L, wname, xn_v, post):
        prev_mm = [p.n("t"), p.n("t")]
        prev_ev = [p.n("s"), p.n("s")]
        entry_v = p.n("v")
        for oc in range(CH):
            p.dma("g", ws[oc % 2][:].rearrange("p (a b) -> p a b", a=CH),
                  ins[wname].rearrange("(a p) b -> p a b", p=128)[:, :, 128 * oc: 128 * (oc + 1)],
                  waits=[("t", prev_mm[oc % 2])], inc="gq")
            ld = p.n("gq")
            PP = PA if oc % 2 == 0 else PB
            for c in range(CH):
                mm4(PP, ws[oc % 2][:, 128 * c: 128 * (c + 1)], xnT[c],
                    c == 0, c == CH - 1,
                    waits=[("gq", ld), ("v", max(xn_v, entry_v)), ("s", prev_ev[oc % 2])])
            prev_mm[oc % 2] = p.n("t")
            post(oc, PP, p.n("t"))
            prev_ev[oc % 2] = p.n("s")

    def bucket_idx(srcs, colbase, src_s):
        for oc in range(CH):
            for tt in range(NT):
                sl = slice(128 * tt, 128 * (tt + 1))
                w = [("s", src_s), ("v", p.n("v")), ("g", setup_g)]
                p.op("t", lambda e, oc=oc, sl=sl: e.matmul(
                    PB[:, 1024:1032], srcs[oc][0:64, sl], rtsb[0:64, :], start=True, stop=True),
                    waits=w, inc="t")
                p.op("t", lambda e, oc=oc, sl=sl: e.matmul(
                    PB[:, 1032:1040], srcs[oc][64:128, sl], rtsb[64:128, :], start=True, stop=True), inc="t")
                tl = p.n("t")
                p.op("v", lambda e: e.reduce_max(
                    R3[:, 0:2].rearrange("p (a b) -> p a b", b=1),
                    PB[:, 1024:1040].rearrange("p (a b) -> p a b", b=8), axis=AX.X),
                    waits=[("t", tl)], inc="v")
                p.op("v", lambda e: e.tensor_tensor(
                    out=R3[:, 2:18].rearrange("p (a b) -> p a b", b=8),
                    in0=PB[:, 1024:1040].rearrange("p (a b) -> p a b", b=8),
                    in1=R3[:, 0:2].rearrange("p (a b) -> p a b", b=1).broadcast_to([128, 2, 8]),
                    op=ALU.is_ge), inc="v")
                p.op("v", lambda e: e.tensor_mul(R3[:, 2:18], R3[:, 2:18], iota16[:, 0:16]), inc="v")
                p.op("v", lambda e, oc=oc, tt=tt: e.reduce_sum(
                    IDX[:, colbase + 16 * tt + 2 * oc: colbase + 16 * tt + 2 * oc + 2].rearrange(
                        "p (a b) -> p a b", b=1),
                    R3[:, 2:18].rearrange("p (a b) -> p a b", b=8), axis=AX.X), inc="v")
        return p.n("v")

    def idx_to_rows(colbase, dst_rw, v_ready):
        for tt in range(NT):
            p.op("v", lambda e, tt=tt: e.tensor_copy(
                IDXB[:, 0:16], IDX[:, colbase + 16 * tt: colbase + 16 * tt + 16]),
                waits=[("v", v_ready), ("t", p.n("t"))], inc="v")
            p.op("t", lambda e: e.transpose(PBT16, IDXB[:, 0:16], ident[:]),
                 waits=[("v", p.n("v"))], inc="t")
            p.op("v", lambda e, tt=tt: e.tensor_copy(
                dst_rw[:, 128 * tt: 128 * (tt + 1)], PBT16),
                waits=[("t", p.n("t"))], inc="v")
        return p.n("v")

    def row_dma(dst_pp, kr, rowoff, src_rows_b, waits):
        p.dma("sy", dst_pp.ap().rearrange("(h r) t -> h r t", r=kr)[:, rowoff: rowoff + 1, :],
              src_rows_b.rearrange("h (o t) -> h o t", o=1), waits=waits, inc="syq")

    def phase_B(L, xn_v, sub=None):
        plain_ev = (sub == 'B5a')
        small_wv = (sub == 'B5b')
        kr, vw = KR[L], VW[L]
        pref = "l0_" if L == 0 else "l1_"
        qpp, kpp, vp = hbm[f"qpp{L}"], hbm[f"kpp{L}"], hbm[f"vp{L}"]

        def qpost(oc, PP, tmm):
            if L == 0:
                p.op("v", lambda e, oc=oc, PP=PP: e.tensor_scalar(
                    out=QT[oc], in0=PP[:, :], scalar1=biases["l0_bq"][:, oc: oc + 1],
                    scalar2=None, op0=ALU.add), waits=[("t", tmm)], inc="v")
                p.op("s", lambda e: e.nop(), waits=[("v", p.n("v"))], inc="s")
            else:
                p.op("s", lambda e, oc=oc, PP=PP: e.activation(
                    out=QT[oc], in_=PP[:, :], func=AF.Copy), waits=[("t", tmm)], inc="s")
        proj(L, pref + "wq", xn_v, qpost)
        s_qt = p.n("s")
        if sub == "B1":
            return

        def kpost(oc, PP, tmm):
            if L == 0:
                p.op("v", lambda e, oc=oc, PP=PP: e.tensor_scalar(
                    out=oTs[oc][:], in0=PP[:, :], scalar1=biases["l0_bk"][:, oc: oc + 1],
                    scalar2=None, op0=ALU.add), waits=[("t", tmm)], inc="v")
                p.op("s", lambda e: e.nop(), waits=[("v", p.n("v"))], inc="s")
            else:
                p.op("s", lambda e, oc=oc, PP=PP: e.activation(
                    out=oTs[oc][:], in_=PP[:, :], func=AF.Copy), waits=[("t", tmm)], inc="s")
        proj(L, pref + "wk", xn_v, kpost)
        s_kt = p.n("s")
        if sub == "B2":
            return

        for h in range(16):
            p.dma("sy", kpp[kr * h: kr * h + 64, :],
                  oTs[h // 2][64 * (h % 2): 64 * (h % 2) + 64, :],
                  waits=[("s", s_kt)], inc="syq")
        for h in range(16):
            p.dma("sy", qpp[kr * h: kr * h + 64, :],
                  QT[h // 2][64 * (h % 2): 64 * (h % 2) + 64, :],
                  waits=[("s", s_qt)], inc="syq")

        if L == 1:
            p.op("v", lambda e: e.memset(ROWSB[0:16, 0:2048], 1.0),
                 waits=[("v", p.n("v")), ("syq", p.n("syq"))], inc="v")
            row_dma(kpp, kr, 64, ROWSB[0:16, 0:2048], [("v", p.n("v"))])
        if L == 0:
            vk = bucket_idx(oTs, 0, s_kt)
            vk2 = idx_to_rows(0, ROWS[0:16, 2048:4096], vk)
            p.op("v", lambda e: e.memset(ROWSB[0:16, 0:2048], 1.0), waits=[("v", vk2)], inc="v")
            row_dma(kpp, kr, 64, ROWSB[0:16, 0:2048], [("v", p.n("v"))])
            p.op("v", lambda e: e.tensor_mul(ROWSB[0:16, 2048:4096], ROWS[0:16, 2048:4096],
                                             ROWS[0:16, 2048:4096]), inc="v")
            row_dma(kpp, kr, 65, ROWSB[0:16, 2048:4096], [("v", p.n("v"))])
            p.op("v", lambda e: e.memset(ROWSB[0:16, 0:2048], -CPEN),
                 waits=[("syq", p.n("syq"))], inc="v")
            row_dma(kpp, kr, 66, ROWSB[0:16, 0:2048], [("v", p.n("v"))])
            p.op("v", lambda e: e.tensor_scalar(out=ROWSB[0:16, 2048:4096],
                                                in0=ROWS[0:16, 2048:4096], scalar1=2 * CPEN,
                                                scalar2=None, op0=ALU.mult),
                 waits=[("syq", p.n("syq"))], inc="v")
            row_dma(kpp, kr, 67, ROWSB[0:16, 2048:4096], [("v", p.n("v"))])
            vq = bucket_idx(QT, 256, s_qt)
            vq2 = idx_to_rows(256, ROWS[0:16, 2048:4096], vq)
        else:
            vq2 = p.n("v")
        if sub == "B3":
            return

        q2mm = []
        for c in range(CH):
            w = [("s", s_qt), ("syq", p.n("syq"))]
            if c >= 2:
                w.append(("t", q2mm[c - 2]))
            p.op("v", lambda e, c=c: e.tensor_mul(asm[c % 2][:], QT[c][:], QT[c][:]),
                 waits=w, inc="v")
            mm4(PA[0:16, :], onesW[:, 16 * c: 16 * (c + 1)], asm[c % 2][:],
                c == 0, c == CH - 1, waits=[("v", p.n("v"))])
            q2mm.append(p.n("t"))
        p.op("v", lambda e: e.tensor_copy(ROWS[0:16, 0:2048], PA[0:16, :]),
             waits=[("t", p.n("t"))], inc="v")
        p.op("s", lambda e: e.activation(out=ROWS[0:16, 0:2048], in_=ROWS[0:16, 0:2048],
                                         func=AF.Sqrt), waits=[("v", p.n("v"))], inc="s")
        p.op("v", lambda e: e.tensor_scalar(out=ROWSB[0:16, 0:2048], in0=ROWS[0:16, 0:2048],
                                            scalar1=-16.0, scalar2=None, op0=ALU.mult),
             waits=[("s", p.n("s"))], inc="v")
        row_dma(qpp, kr, 64, ROWSB[0:16, 0:2048], [("v", p.n("v"))])
        if L == 0:
            p.op("v", lambda e: e.memset(ROWSB[0:16, 0:2048], -CPEN),
                 waits=[("syq", p.n("syq")), ("v", vq2)], inc="v")
            row_dma(qpp, kr, 65, ROWSB[0:16, 0:2048], [("v", p.n("v"))])
            p.op("v", lambda e: e.tensor_mul(ROWSB[0:16, 2048:4096], ROWS[0:16, 2048:4096],
                                             ROWS[0:16, 2048:4096]),
                 waits=[("syq", p.n("syq"))], inc="v")
            row_dma(qpp, kr, 66, ROWSB[0:16, 2048:4096], [("v", p.n("v"))])
            p.op("v", lambda e: e.tensor_copy(ROWSB[0:16, 0:2048], ROWS[0:16, 2048:4096]),
                 waits=[("syq", p.n("syq"))], inc="v")
            row_dma(qpp, kr, 67, ROWSB[0:16, 0:2048], [("v", p.n("v"))])
        if sub == "B4":
            return

        wvfull = AR[:, 0:8192]
        ar_guard = [("t", p.n("t")), ("v", p.n("v")), ("s", p.n("s"))]
        if small_wv:
            for c in range(CH):
                p.dma("g", wvfull[:, 1024 * c: 1024 * (c + 1)],
                      ins[pref + "wv"][128 * c: 128 * (c + 1), :], inc="gq")
        else:
            p.dma("g", wvfull.rearrange("p (a b) -> p a b", a=CH),
                  ins[pref + "wv"].rearrange("(a p) b -> p a b", p=128),
                  waits=ar_guard, inc="gq")
        wvld = p.n("gq")
        prev_t = [p.n("t"), p.n("t")]
        prev_s = [p.n("s"), p.n("s")]
        entry_v = p.n("v")
        prev_v = [entry_v, entry_v]
        prev_dma = [0, 0]
        for tt in range(NT):
            PP = PA if tt % 2 == 0 else PB
            for c in range(CH):
                mm4(PP[:, 0:1024], xnT[c][:, 128 * tt: 128 * (tt + 1)],
                    wvfull[:, 1024 * c: 1024 * (c + 1)], c == 0, c == CH - 1,
                    waits=[("gq", wvld), ("v", max(xn_v, prev_v[tt % 2])), ("t", prev_t[tt % 2]),
                           ("s", prev_s[tt % 2])], width=1024)
            tmm = p.n("t")
            buf = asm[tt % 2]
            if L == 0 and plain_ev:
                p.op("s", lambda e, PP=PP, buf=buf: e.activation(
                    out=buf[:, 0:1024], in_=PP[:, 0:1024], func=AF.Copy),
                    waits=[("t", tmm), ("syq", prev_dma[tt % 2])], inc="s")
                p.op("v", lambda e: e.nop(), inc="v")
            elif L == 0:
                p.op("s", lambda e, PP=PP, buf=buf: e.activation(
                    out=buf[:, 0:1040].rearrange("p (a b) -> p a b", b=65)[:, :, 0:64],
                    in_=PP[:, 0:1024].rearrange("p (a b) -> p a b", b=64), func=AF.Copy),
                    waits=[("t", tmm), ("syq", prev_dma[tt % 2])], inc="s")
                p.op("v", lambda e, buf=buf: e.memset(
                    buf[:, 0:1040].rearrange("p (a b) -> p a b", b=65)[:, :, 64:65], 1.0),
                    waits=[("s", p.n("s")), ("syq", prev_dma[tt % 2])], inc="v")
            else:
                p.op("s", lambda e, PP=PP, buf=buf: e.activation(
                    out=buf[:, 0:1024], in_=PP[:, 0:1024], func=AF.Copy),
                    waits=[("t", tmm), ("syq", prev_dma[tt % 2])], inc="s")
                p.op("v", lambda e: e.nop(), inc="v")
            p.dma("sy", vp[128 * tt: 128 * (tt + 1), :], buf[:, 0:vw],
                  waits=[("s", p.n("s")), ("v", p.n("v"))], inc="syq")
            prev_dma[tt % 2] = p.n("syq")
            prev_t[tt % 2] = p.n("t")
            prev_s[tt % 2] = p.n("s")
            prev_v[tt % 2] = p.n("v")
        if sub in ("B5", "B5a", "B5b"):
            return

        agw = [("syq", p.n("syq")), ("gq", p.n("gq")), ("v", p.n("v")), ("s", p.n("s"))]
        p.op("g", lambda e: e.collective_compute(
            "AllGather", ALU.bypass, replica_groups=[[0, 1], [2, 3], [4, 5], [6, 7]],
            ins=[kpp.ap().opt()], outs=[hbm[f"kppf{L}"].ap().opt()]),
            waits=agw, inc=f"cck{L}")
        p.op("g", lambda e: e.collective_compute(
            "AllGather", ALU.bypass, replica_groups=[[0, 1], [2, 3], [4, 5], [6, 7]],
            ins=[vp.ap().opt()], outs=[hbm[f"vpf{L}"].ap().opt()]),
            waits=agw, inc=f"ccv{L}")

    R1f = R1.bitcast(F32)

    def l1_head_post(h):
        SUM0 = ROWS[0:1, 0:2048]
        SUM1 = ROWS[0:1, 2048:4096]
        FMU = R1f[0:1, 4736:6784]
        F0b = R1[0:1, 5120:7168]
        F1b = R1[0:1, 13568:15616]
        H0 = RB32[0:1, 0:1024]
        H1 = RB32[0:1, 1024:2048]
        H2 = R1f[0:1, 512:1536]
        H3 = R1f[0:1, 1536:2560]
        H4 = R3[0:1, 3072:4096]
        H5 = R1f[0:1, 2560:3584]
        H6 = R1f[0:1, 6784:7808]
        stg32 = ROWS[32:33, 0:2048]
        stg64 = ROWS[64:65, 0:2048]
        osh = p.n("odma")
        p.op("v", lambda e: e.tensor_mul(st_sc[0][:, :], o01[0][:, :], o01[0][:, :]),
             waits=[("odma", osh)], inc="v")
        p.op("v", lambda e: e.tensor_mul(st_sc[1][:, :], o01[1][:, :], o01[1][:, :]), inc="v")
        p.op("v", lambda e: e.tensor_mul(st_sc[2][:, :], o01[0][:, :], o01[1][:, :]), inc="v")
        vp_ = p.n("v")
        mm4(PA[0:1, :], ones1[:], o01[0][:, :], True, True,
            waits=[("v", vp_), ("s", p.n("s"))])
        mm4(PA[32:33, :], ones1[:], o01[1][:, :], True, True)
        mm4(PA[64:65, :], ones1[:], st_sc[0][:, :], True, True)
        mm4(PB[0:1, :], ones1[:], st_sc[1][:, :], True, True)
        mm4(PB[32:33, :], ones1[:], st_sc[2][:, :], True, True)
        tst = p.n("t")
        p.op("v", lambda e: e.tensor_copy(stg32, PA[32:33, :]), waits=[("t", tst)], inc="v")
        p.op("v", lambda e: e.tensor_copy(stg64, PA[64:65, :]), inc="v")
        p.op("v", lambda e: e.tensor_copy(ROWS[32:33, 2048:4096], PB[32:33, :]), inc="v")
        for qh in range(2):
            qsl = slice(1024 * qh, 1024 * (qh + 1))
            s0h = SUM0[:, qsl]
            s1h = SUM1[:, qsl]
            p.op("v", lambda e, s0h=s0h: e.reciprocal(H5, s0h), inc="v")
            p.op("v", lambda e, s1h=s1h: e.reciprocal(H6, s1h), inc="v")
            p.op("v", lambda e, qsl=qsl: e.tensor_copy(H0, PA[0:1, qsl]),
                 waits=[("t", tst)], inc="v")
            p.dma("g", H1, stg32[:, qsl], waits=[("v", p.n("v"))], inc="odma")
            p.dma("g", H2, stg64[:, qsl], inc="odma")
            p.op("v", lambda e, qsl=qsl: e.tensor_copy(H3, PB[0:1, qsl]), inc="v")
            od = p.n("odma")
            p.op("v", lambda e: e.tensor_mul(H0, H5, H0), waits=[("odma", od)], inc="v")
            p.op("v", lambda e: e.tensor_mul(H1, H6, H1), inc="v")
            p.op("v", lambda e: e.tensor_scalar(out=H1, in0=H1, scalar1=lamv[0:1, 0:1],
                                                scalar2=None, op0=ALU.mult), inc="v")
            p.op("v", lambda e: e.tensor_sub(H0, H0, H1), inc="v")
            p.op("v", lambda e: e.tensor_scalar(out=H0, in0=H0, scalar1=1.0 / 128,
                                                scalar2=None, op0=ALU.mult), inc="v")
            p.op("v", lambda e: e.tensor_mul(H1, H5, H5), inc="v")
            p.op("v", lambda e: e.tensor_mul(H1, H1, H2), inc="v")
            p.op("v", lambda e: e.tensor_mul(H2, H6, H6), inc="v")
            p.op("v", lambda e: e.tensor_mul(H2, H2, H3), inc="v")
            p.op("v", lambda e: e.tensor_scalar(out=H2, in0=H2, scalar1=lamv[0:1, 0:1],
                                                scalar2=None, op0=ALU.mult), inc="v")
            p.op("v", lambda e: e.tensor_scalar(out=H2, in0=H2, scalar1=lamv[0:1, 0:1],
                                                scalar2=None, op0=ALU.mult), inc="v")
            p.dma("g", H3, ROWS[32:33, 2048 + 1024 * qh: 2048 + 1024 * (qh + 1)],
                  waits=[("v", p.n("v"))], inc="odma")
            p.op("v", lambda e: e.tensor_mul(H4, H5, H6),
                 waits=[("odma", p.n("odma"))], inc="v")
            p.op("v", lambda e: e.tensor_mul(H4, H4, H3), inc="v")
            p.op("v", lambda e: e.tensor_scalar(out=H4, in0=H4, scalar1=lamv[0:1, 0:1],
                                                scalar2=None, op0=ALU.mult), inc="v")
            p.op("v", lambda e: e.tensor_scalar(out=H4, in0=H4, scalar1=2.0,
                                                scalar2=None, op0=ALU.mult), inc="v")
            p.op("v", lambda e: e.tensor_add(H1, H1, H2), inc="v")
            p.op("v", lambda e: e.tensor_sub(H1, H1, H4), inc="v")
            p.op("v", lambda e: e.tensor_scalar(out=H1, in0=H1, scalar1=1.0 / 128,
                                                scalar2=None, op0=ALU.mult), inc="v")
            p.op("v", lambda e: e.tensor_mul(H2, H0, H0), inc="v")
            p.op("v", lambda e: e.tensor_sub(H1, H1, H2), inc="v")
            p.op("s", lambda e: e.activation(out=H2, in_=H1, func=AF.Sqrt, bias=epsb[0:1, :]),
                 waits=[("v", p.n("v"))], inc="s")
            p.op("v", lambda e: e.reciprocal(H1, H2), waits=[("s", p.n("s"))], inc="v")
            p.op("v", lambda e, s0h=s0h: e.tensor_mul(s0h, H5, H1), inc="v")
            p.op("v", lambda e, s1h=s1h: e.tensor_mul(s1h, H6, H1), inc="v")
            p.op("v", lambda e, s1h=s1h: e.tensor_scalar(out=s1h, in0=s1h,
                                                         scalar1=lamv[0:1, 0:1],
                                                         scalar2=None, op0=ALU.mult), inc="v")
            p.op("v", lambda e: e.tensor_mul(H0, H0, H1), inc="v")
            p.op("v", lambda e, qsl=qsl: e.tensor_copy(FMU[:, qsl], H0),
                 waits=[("syq", p.n("syq"))], inc="v")
        p.dma("sy", hbm["m8"][h: h + 1, :], FMU, waits=[("v", p.n("v"))], inc="syq")
        p.op("v", lambda e: e.tensor_copy(F0b, SUM0), inc="v")
        vF = p.n("v")
        mm4(PA, onesr[0:1, :], F0b, True, True, waits=[("v", vF)])
        p.op("v", lambda e: e.tensor_copy(asm[0][:, :], PA[:, :]), waits=[("t", p.n("t"))], inc="v")
        p.op("v", lambda e: e.tensor_copy(F1b, SUM1), inc="v")
        vF1 = p.n("v")
        mm4(PA, onesr[0:1, :], F1b, True, True, waits=[("v", vF1)])
        p.op("v", lambda e: e.tensor_copy(asm[1][:, :], PA[:, :]), waits=[("t", p.n("t"))], inc="v")
        p.op("v", lambda e: e.tensor_mul(st_sc[0][:, :], o01[0][:, :], asm[0][:, :]), inc="v")
        p.op("v", lambda e: e.tensor_mul(st_sc[1][:, :], o01[1][:, :], asm[1][:, :]), inc="v")
        p.op("v", lambda e, h=h: e.tensor_sub(oTs[h][:, :], st_sc[0][:, :], st_sc[1][:, :]), inc="v")

    def phase_C(L, c0=False, upto_c1=False):
        kr, vw = KR[L], VW[L]
        nh = 16 if L == 0 else 8
        njl = 1 if L == 0 else 2
        kppf, vpf, qpp = hbm[f"kppf{L}"], hbm[f"vpf{L}"], hbm[f"qpp{L}"]
        scores = [PA[:, 0:1024], PA[:, 1024:2048]]
        misc = PB[:, 1024:2048]
        opsum = PB[0:65, 0:1024] if L == 0 else PB[0:128, 0:1024]

        kst_used = [0, 0]
        qst_used = [0, 0]
        vst_used = [0, 0]
        kn = [p.n("kl0"), p.n("kl1")]
        qn = [p.n("ql0"), p.n("ql1")]
        vn = [p.n("vl0"), p.n("vl1")]
        p.counts.setdefault("kl0", 0); p.counts.setdefault("kl1", 0)
        p.counts.setdefault("ql0", 0); p.counts.setdefault("ql1", 0)
        p.counts.setdefault("vl0", 0); p.counts.setdefault("vl1", 0)
        sc_free = [0, 0]
        exp_free = [0, 0]
        exp_done = [0, 0]
        op_free = p.n("v")
        misc_free = p.n("v")
        pa_free = p.n("s")
        state = {"qtog": 0}

        def load_kst(h, b):
            for chunk in range(2):
                p.dma("sy", kst[b][0:kr, 2048 * chunk: 2048 * (chunk + 1)],
                      kppf[chunk * 16 * kr + kr * h: chunk * 16 * kr + kr * h + kr, :],
                      waits=[("t", kst_used[b]), (f"cck{L}", 1)], inc=f"kl{b}")
            kn[b] += 32

        def load_qst(hh, qh, b):
            p.dma("sy", qst[b][0:kr, :], qpp[kr * hh: kr * hh + kr, 1024 * qh: 1024 * (qh + 1)],
                  waits=[("t", qst_used[b])], inc=f"ql{b}")
            qn[b] += 16

        def load_vst(h, kq, b):
            csl = slice(0, 1040) if L == 0 else slice(128 * h, 128 * (h + 1))
            p.dma("sy", vst(L, b)[:, :, :],
                  vpf.ap().rearrange("(a p) b -> p a b", p=128)[:, 8 * kq: 8 * kq + 8, csl],
                  waits=[("t", vst_used[b]), (f"ccv{L}", 1)], inc=f"vl{b}")
            vn[b] += 16

        if L == 0:
            load_kst(0, 0)
        for h in range(nh):
            if L == 0:
                kb = h % 2
                if h + 1 < nh:
                    load_kst(h + 1, (h + 1) % 2)
            else:
                load_kst(2 * h, 0)
                load_kst(2 * h + 1, 1)
            for qh in range(2):
                for j in range(njl):
                    hh = h if L == 0 else 2 * h + j
                    if L == 1:
                        kb = j
                    qb = state["qtog"]
                    state["qtog"] ^= 1
                    load_qst(hh, qh, qb)
                    myq = qn[qb]
                    for kq in range(4):
                        vb = kq % 2
                        load_vst(h, kq, vb)
                        myv = vn[vb]
                        for ki in range(8):
                            k = 8 * kq + ki
                            sp = k % 2
                            w = [("s", max(sc_free[sp], pa_free)),
                                 (f"kl{kb}", kn[kb]), (f"ql{qb}", myq)]
                            for i in range(2):
                                sl = slice(512 * i, 512 * (i + 1))
                                p.op("t", lambda e, sp=sp, sl=sl, k=k, kb=kb, qb=qb: e.matmul(
                                    scores[sp][:, sl],
                                    kst[kb][0:kr, 128 * k: 128 * (k + 1)],
                                    qst[qb][0:kr, sl], start=True, stop=True),
                                    waits=(w if i == 0 else ()), inc="t")
                            kst_used[kb] = qst_used[qb] = tqk = p.n("t")
                            p.op("s", lambda e, sp=sp: e.activation(
                                out=expb[sp], in_=scores[sp][:, :], func=AF.Exp, scale=0.125),
                                waits=[("t", max(tqk, exp_free[sp]))], inc="s")
                            exp_done[sp] = sc_free[sp] = p.n("s")
                            w2 = [("s", exp_done[sp]), (f"vl{vb}", myv)]
                            if k == 0:
                                w2.append(("v", op_free))
                                w2.append(("s", pa_free))
                            dv = 65 if L == 0 else 128
                            voff = 65 * h if L == 0 else 0
                            for i in range(2):
                                sl = slice(512 * i, 512 * (i + 1))
                                p.op("t", lambda e, vb=vb, ki=ki, sl=sl, k=k, sp=sp,
                                     dv=dv, voff=voff: e.matmul(
                                         opsum[:, sl],
                                         vst(L, vb)[:, ki: ki + 1, voff: voff + dv].rearrange(
                                             "p a b -> p (a b)"),
                                         expb[sp][:, sl], start=(k == 0), stop=(k == 31)),
                                     waits=(w2 if i == 0 else ()), inc="t")
                            vst_used[vb] = p.n("t")
                            if L == 1:
                                w3 = [("v", misc_free)] if k == 0 else ()
                                for i in range(2):
                                    sl = slice(512 * i, 512 * (i + 1))
                                    p.op("t", lambda e, j=j, sl=sl, k=k, sp=sp: e.matmul(
                                        misc[64 * j: 64 * j + 1, sl], ones1[:],
                                        expb[sp][:, sl], start=(k == 0), stop=(k == 31)),
                                        waits=(w3 if i == 0 else ()), inc="t")
                            exp_free[sp] = p.n("t")
                    tav = p.n("t")
                    if L == 0:
                        rsl = ROWS[64:65, 1024 * qh: 1024 * (qh + 1)]
                        rbl = ROWSB[64:65, 1024 * qh: 1024 * (qh + 1)]
                        p.op("v", lambda e, rsl=rsl: e.reciprocal(rsl, opsum[64:65, :]),
                             waits=[("t", tav)], inc="v")
                        p.op("v", lambda e, rsl=rsl, rbl=rbl: e.tensor_copy(rbl, rsl), inc="v")
                        vr = p.n("v")
                        for i in range(2):
                            sl = slice(512 * i, 512 * (i + 1))
                            p.op("t", lambda e, rbl=rbl, sl=sl: e.matmul(
                                misc[:, sl], onesr[64:65, :], rbl[:, sl], start=True, stop=True),
                                waits=([("v", max(vr, misc_free))] if i == 0 else ()), inc="t")
                        p.op("v", lambda e, qh=qh: e.tensor_copy(rbcs[qh][:, :], misc[:, :]),
                             waits=[("t", p.n("t"))], inc="v")
                        misc_free = p.n("v")
                        p.op("v", lambda e, qh=qh: e.tensor_mul(
                            l0tsl[0:64, 0:1024], opsum[0:64, :], rbcs[qh][0:64, :]), inc="v")
                        op_free = p.n("v")
                        if h % 2 == 0:
                            p.op("v", lambda e, h=h, qh=qh: e.tensor_scalar(
                                out=oTs[h // 2][0:64, 1024 * qh: 1024 * (qh + 1)],
                                in0=l0tsl[0:64, 0:1024],
                                scalar1=biases["l0_bv"][0:64, h // 2: h // 2 + 1],
                                scalar2=None, op0=ALU.add), inc="v")
                        else:
                            p.dma("g", oTs[h // 2][64:128, 1024 * qh: 1024 * (qh + 1)],
                                  l0tsl[0:64, 0:1024], waits=[("v", p.n("v"))], inc="odma")
                            p.op("v", lambda e, h=h, qh=qh: e.tensor_scalar(
                                out=oTs[h // 2][64:128, 1024 * qh: 1024 * (qh + 1)],
                                in0=oTs[h // 2][64:128, 1024 * qh: 1024 * (qh + 1)],
                                scalar1=biases["l0_bv"][64:128, h // 2: h // 2 + 1],
                                scalar2=None, op0=ALU.add),
                                waits=[("odma", p.n("odma"))], inc="v")
                    else:
                        p.op("v", lambda e, j=j, qh=qh: e.tensor_copy(
                            o01[j][:, 1024 * qh: 1024 * (qh + 1)], opsum[:, :]),
                            waits=[("t", tav)], inc="v")
                        op_free = p.n("v")
                        if j == 0:
                            p.op("v", lambda e, qh=qh: e.tensor_copy(
                                ROWS[0:1, 1024 * qh: 1024 * (qh + 1)], misc[0:1, :]), inc="v")
                        else:
                            p.op("v", lambda e, qh=qh: e.tensor_copy(
                                ROWS[64:65, 1024 * qh: 1024 * (qh + 1)], misc[64:65, :]), inc="v")
                            p.dma("g", ROWS[0:1, 2048 + 1024 * qh: 2048 + 1024 * (qh + 1)],
                                  ROWS[64:65, 1024 * qh: 1024 * (qh + 1)],
                                  waits=[("v", p.n("v"))], inc="odma")
                        misc_free = p.n("v")
            if upto_c1 and L == 1:
                l1_head_post(h)
                d_rows1 = nc.dram_tensor("d_rows1", [128, 4096], F32, kind="ExternalOutput").ap()
                d_f0bc = nc.dram_tensor("d_f0bc", [128, T], BF16, kind="ExternalOutput").ap()
                d_f1bc = nc.dram_tensor("d_f1bc", [128, T], BF16, kind="ExternalOutput").ap()
                d_oc = nc.dram_tensor("d_oc", [128, T], BF16, kind="ExternalOutput").ap()
                d_r1 = nc.dram_tensor("d_r1", [128, 16896], BF16, kind="ExternalOutput").ap()
                allw = [("t", p.n("t")), ("s", p.n("s")), ("v", p.n("v")),
                        ("syq", p.n("syq")), ("gq", p.n("gq")), ("odma", p.n("odma"))]
                p.dma("sy", d_rows1, ROWS[:], waits=allw, inc="ost")
                p.dma("sy", d_f0bc, asm[0][:], waits=allw, inc="ost")
                p.dma("sy", d_f1bc, asm[1][:], waits=allw, inc="ost")
                p.dma("sy", d_oc, oTs[0][:], waits=allw, inc="ost")
                p.dma("sy", d_r1, R1[:], waits=allw, inc="ost")
                return
            if c0:
                d_o0 = nc.dram_tensor("d_o0", [128, T], BF16, kind="ExternalOutput").ap()
                d_o1 = nc.dram_tensor("d_o1", [128, T], BF16, kind="ExternalOutput").ap()
                d_rows = nc.dram_tensor("d_rows", [128, 4096], F32, kind="ExternalOutput").ap()
                allw = [("t", p.n("t")), ("s", p.n("s")), ("v", p.n("v")),
                        ("syq", p.n("syq")), ("gq", p.n("gq")), ("odma", p.n("odma"))]
                p.dma("sy", d_o0, o01[0], waits=allw, inc="ost")
                p.dma("sy", d_o1, o01[1], waits=allw, inc="ost")
                p.dma("sy", d_rows, ROWS[:], waits=allw, inc="ost")
                return
            if L == 1:
                l1_head_post(h)
                op_free = p.n("v")
                misc_free = p.n("v")
                pa_free = 0
                sc_free = [0, 0]
                vpost = p.n("v")
                sc_free = [0, 0]
                self_gate = vpost
                p.op("s", lambda e: e.nop(), waits=[("v", vpost)], inc="s")
                sc_free = [p.n("s"), p.n("s")]

    def phase_D(L):
        pref = "l0_" if L == 0 else "l1_"
        oc_v = p.n("v")
        vprep = p.n("v")
        if L == 1:
            prevg = p.n("t")
            dprep_s = p.n("s")
            for hh in range(8):
                p.dma("g", ws[hh % 2][:].rearrange("p (a b) -> p a b", a=CH),
                      ins["l1_wo"][128 * hh: 128 * (hh + 1), :].rearrange("p (a b) -> p a b", a=CH),
                      waits=[("t", prevg)], inc="gq")
                ld = p.n("gq")
                for i in range(2):
                    sl = slice(512 * i, 512 * (i + 1))
                    p.op("t", lambda e, hh=hh, sl=sl: e.matmul(
                        PA[0:8, sl], lngW[:, 8 * hh: 8 * hh + 8],
                        ws[hh % 2][:, sl], start=(hh == 0), stop=(hh == 7)),
                        waits=[("gq", ld), ("v", oc_v), ("s", dprep_s)], inc="t")
                for i in range(2):
                    sl = slice(512 * i, 512 * (i + 1))
                    p.op("t", lambda e, hh=hh, sl=sl: e.matmul(
                        PB[0:1, sl], lnbb[:], ws[hh % 2][:, sl],
                        start=(hh == 0), stop=(hh == 7)), inc="t")
                prevg = p.n("t")
            p.op("v", lambda e: e.tensor_copy(ROWSB[0:8, 0:1024], PA[0:8, 0:1024]),
                 waits=[("t", p.n("t"))], inc="v")
            p.op("v", lambda e: e.tensor_copy(ROWS[0:1, 0:1024], PB[0:1, 0:1024]), inc="v")
            p.op("v", lambda e: e.tensor_copy(ROWSB[0:1, 1024:2048], ROWS[0:1, 0:1024]), inc="v")
            vc1 = p.n("v")
            for dc in range(CH):
                p.op("t", lambda e, dc=dc: e.transpose(
                    PBT128, ROWSB[0:1, 1024 + 128 * dc: 1024 + 128 * (dc + 1)], ident[0:1, 0:1]),
                    waits=[("v", vc1), ("v", p.n("v"))], inc="t")
                p.op("v", lambda e, dc=dc: e.tensor_copy(c1sb[:, dc: dc + 1], PBT128),
                     waits=[("t", p.n("t"))], inc="v")
            p.dma("sy", ROWS[0:8, 0:2048], hbm["m8"][:, :],
                  waits=[("syq", p.n("syq")), ("v", p.n("v")), ("s", p.n("s")),
                         ("t", p.n("t"))], inc="syq")
            p.op("v", lambda e: e.tensor_scalar(out=ROWSB[0:8, 2048:4096], in0=ROWS[0:8, 0:2048],
                                                scalar1=-1.0, scalar2=None, op0=ALU.mult),
                 waits=[("syq", p.n("syq"))], inc="v")
            vprep = p.n("v")
            if os.environ.get("DUMPD") == "1":
                d_rb = nc.dram_tensor("d_rb", [128, 4096], BF16, kind="ExternalOutput").ap()
                d_rw = nc.dram_tensor("d_rw", [128, 4096], F32, kind="ExternalOutput").ap()
                p.dma("sy", d_rb, ROWSB[:], waits=[("v", vprep), ("t", p.n("t")), ("s", p.n("s"))], inc="dbgst")
                p.dma("sy", d_rw, ROWS[:], waits=[("v", vprep), ("t", p.n("t")), ("s", p.n("s"))], inc="dbgst")
                p.op("sy", lambda e: e.nop(), waits=[("dbgst", p.n("dbgst"))], inc="sy2")

        prev_mm = [p.n("t"), p.n("t")]
        prev_ev = [p.n("s"), p.n("s")]
        for dc in range(CH):
            p.dma("g", ws[dc % 2][:].rearrange("p (a b) -> p a b", a=CH),
                  ins[pref + "wo"].rearrange("(a p) b -> p a b", p=128)[:, :, 128 * dc: 128 * (dc + 1)],
                  waits=[("t", prev_mm[dc % 2])], inc="gq")
            ld = p.n("gq")
            if L == 1:
                p.op("v", lambda e, dc=dc: e.tensor_scalar(
                    out=ws[dc % 2][:], in0=ws[dc % 2][:], scalar1=biases["l1_lng"][:],
                    scalar2=None, op0=ALU.mult), waits=[("gq", ld), ("v", vprep)], inc="v")
                wsr = [("v", p.n("v")), ("s", prev_ev[dc % 2])]
            else:
                wsr = [("gq", ld), ("s", prev_ev[dc % 2])]
            PP = PA if dc % 2 == 0 else PB
            for oc in range(CH):
                last = (oc == CH - 1) and (L == 0 or os.environ.get("DRANK8") == "0")
                mm4(PP, ws[dc % 2][:, 128 * oc: 128 * (oc + 1)], oTs[oc][:, :],
                    oc == 0, last, waits=wsr + [("v", oc_v)])
            if L == 1 and os.environ.get("DRANK8") != "0":
                mm4(PP, ROWSB[0:8, 128 * dc: 128 * (dc + 1)],
                    ROWSB[0:8, 2048:4096], False, True, waits=[("v", vprep)])
            prev_mm[dc % 2] = p.n("t")
            bb = biases["l0_bo"][:, dc: dc + 1] if L == 0 else c1sb[:, dc: dc + 1]
            p.op("v", lambda e, dc=dc, PP=PP, bb=bb: e.tensor_scalar(
                out=sT[dc], in0=PP[:, :], scalar1=bb, scalar2=None, op0=ALU.add),
                waits=[("t", prev_mm[dc % 2])], inc="v")
            p.op("s", lambda e: e.nop(), waits=[("v", p.n("v"))], inc="s")
            prev_ev[dc % 2] = p.n("s")
        return p.n("s"), p.n("t")

    def phase_E0(sD, tD, first_out):
        for c in range(CH):
            p.dma("g", R3[:, 0:1024], ins["l0_fw"][128 * c: 128 * (c + 1), :],
                  waits=[("v", p.n("v")), ("s", sD)], inc="gq")
            p.op("v", lambda e, c=c: e.reduce_max(
                wscale[:, c: c + 1].rearrange("p (a b) -> p a b", b=1),
                R3[:, 0:1024].rearrange("p (a b) -> p a b", b=1024),
                axis=AX.X, apply_absolute_value=True),
                waits=[("gq", p.n("gq"))], inc="v")
            p.op("v", lambda e, c=c: e.tensor_scalar(
                out=wscale[:, c: c + 1], in0=wscale[:, c: c + 1], scalar1=1e-8,
                scalar2=None, op0=ALU.max), inc="v")
            p.op("v", lambda e, c=c: e.reciprocal(wrecip[:, c: c + 1], wscale[:, c: c + 1]), inc="v")
            p.op("v", lambda e, c=c: e.tensor_scalar(
                out=wrecip[:, c: c + 1], in0=wrecip[:, c: c + 1], scalar1=127.0,
                scalar2=None, op0=ALU.mult), inc="v")
            p.op("v", lambda e, c=c: e.tensor_scalar(
                out=wscale[:, c: c + 1], in0=wscale[:, c: c + 1], scalar1=1.0 / 127.0,
                scalar2=None, op0=ALU.mult), inc="v")
        vq = p.n("v")
        for c in range(CH):
            p.op("t", lambda e, c=c: e.transpose(PBF1, wrecip[:, c: c + 1], identf[:]),
                 waits=[("v", vq), ("v", p.n("v")), ("t", tD)], inc="t")
            p.op("v", lambda e, c=c: e.tensor_copy(
                ROWS[0:1, 128 * c: 128 * (c + 1)], PBF1),
                waits=[("t", p.n("t"))], inc="v")
        vrow = p.n("v")
        for i in range(2):
            sl = slice(512 * i, 512 * (i + 1))
            p.op("t", lambda e, sl=sl: e.matmul(
                PB[:, sl], onesrF[0:1, :], ROWS[0:1, 0:1024][:, sl], start=True, stop=True),
                waits=([("v", vrow)] if i == 0 else ()), inc="t")
        p.op("v", lambda e: e.tensor_copy(R3[:, 1024:2048], PB[:, 0:1024]),
             waits=[("t", p.n("t"))], inc="v")
        vrw = p.n("v")
        for c in range(CH):
            p.dma("g", R3[:, 0:1024], ins["l0_fwt"][128 * c: 128 * (c + 1), :],
                  waits=[("v", p.n("v")), ("v", vrw)], inc="gq")
            p.op("v", lambda e: e.tensor_mul(R3[:, 2048:3072], R3[:, 0:1024], R3[:, 1024:2048]),
                 waits=[("gq", p.n("gq"))], inc="v")
            p.op("v", lambda e, c=c: e.tensor_scalar(
                out=AR[:, 1024 * c: 1024 * (c + 1)], in0=R3[:, 2048:3072],
                scalar1=MAGIC, scalar2=MAGIC, op0=ALU.add, op1=ALU.subtract), inc="v")
        vwq = p.n("v")

        tmaj = [AR[:, 8192:9216], AR[:, 9216:10240]]
        prev_red = [0, 0]
        for tt in range(NT):
            b = tt % 2
            p.op("sy", lambda e, b=b, tt=tt: e.dma_start_transpose(
                tmaj[b], hbm["xnb"][:, 128 * tt: 128 * (tt + 1)]),
                waits=[("v", prev_red[b]), ("odma", p.n("odma")), ("syq", setup_syq)],
                inc=f"tm{b}", inck=16)
            p.op("v", lambda e, b=b, tt=tt: e.reduce_max(
                IDX[:, tt: tt + 1].rearrange("p (a b) -> p a b", b=1),
                tmaj[b].rearrange("p (a b) -> p a b", b=1024),
                axis=AX.X, apply_absolute_value=True),
                waits=[(f"tm{b}", 16 * (tt // 2 + 1))], inc="v")
            prev_red[b] = p.n("v")
        p.op("v", lambda e: e.tensor_scalar(out=IDX[:, 0:16], in0=IDX[:, 0:16],
                                            scalar1=1e-8, scalar2=None, op0=ALU.max), inc="v")
        vam = p.n("v")
        for tt in range(NT):
            p.op("t", lambda e, tt=tt: e.transpose(PBF1, IDX[:, tt: tt + 1], identf[:]),
                 waits=[("v", vam), ("v", p.n("v"))], inc="t")
            p.op("v", lambda e, tt=tt: e.tensor_copy(ROWS[0:1, 128 * tt: 128 * (tt + 1)], PBF1),
                 waits=[("t", p.n("t"))], inc="v")
        p.op("v", lambda e: e.tensor_scalar(out=ROWSB[0:1, 0:2048], in0=ROWS[0:1, 0:2048],
                                            scalar1=1.0 / 127.0, scalar2=None, op0=ALU.mult), inc="v")
        p.op("v", lambda e: e.reciprocal(ROWS[0:1, 2048:4096], ROWS[0:1, 0:2048]), inc="v")
        p.op("v", lambda e: e.tensor_scalar(out=ROWS[0:1, 2048:4096], in0=ROWS[0:1, 2048:4096],
                                            scalar1=127.0, scalar2=None, op0=ALU.mult), inc="v")
        vxr = p.n("v")
        mm4(PA, onesrF[0:1, :], ROWS[0:1, 2048:4096], True, True, waits=[("v", vxr)])
        p.op("v", lambda e: e.tensor_copy(R3[:, 0:2048], PA[:, :]), waits=[("t", p.n("t"))], inc="v")
        mm4(PB, onesr[0:1, :], ROWSB[0:1, 0:2048], True, True, waits=[("v", vxr)])
        p.op("v", lambda e: e.tensor_copy(asm[0][:, :], PB[:, :]), waits=[("t", p.n("t"))], inc="v")
        vbc = p.n("v")
        for c in range(CH):
            p.dma("sy", oTs[c][:], hbm["xnb"][128 * c: 128 * (c + 1), :],
                  waits=[("t", tD)], inc="syq")
        ldx = p.n("syq")
        for c in range(CH):
            p.op("v", lambda e, c=c: e.tensor_mul(R3[:, 2048:4096], oTs[c][:], R3[:, 0:2048]),
                 waits=[("syq", ldx), ("v", vbc)], inc="v")
            p.op("v", lambda e, c=c: e.tensor_scalar(
                out=oTs[c][:], in0=R3[:, 2048:4096], scalar1=MAGIC, scalar2=MAGIC,
                op0=ALU.add, op1=ALU.subtract), inc="v")
        vxq = p.n("v")

        prev_mm = [0, 0]
        prev_ev = [0, 0]
        for oc in range(CH):
            PP = PA if oc % 2 == 0 else PB
            for c in range(CH):
                mm4(PP, AR[:, 1024 * c + 128 * oc: 1024 * c + 128 * (oc + 1)],
                    oTs[c][:, :], c == 0, c == CH - 1,
                    waits=[("v", max(vxq, vwq)), ("t", prev_mm[oc % 2]),
                           ("v", prev_ev[oc % 2])])
            prev_mm[oc % 2] = p.n("t")
            p.op("v", lambda e, oc=oc, PP=PP: e.scalar_tensor_tensor(
                out=asm[1][:, :], in0=PP[:, :], scalar=wscale[:, oc: oc + 1],
                in1=asm[0][:, :], op0=ALU.mult, op1=ALU.mult),
                waits=[("t", prev_mm[oc % 2]), ("s", p.n("s"))], inc="v")
            prev_ev[oc % 2] = p.n("v")
            p.op("s", lambda e, oc=oc: e.activation(
                out=AR[:, 8192:10240], in_=asm[1][:, :], func=AF.Silu,
                bias=biases["l0_fb"][:, oc: oc + 1]), waits=[("v", p.n("v"))], inc="s")
            sg = p.n("s")
            if first_out:
                fout = R3[:, 0:2048] if oc % 2 == 0 else ROWS[:, 0:2048]
                p.op("v", lambda e, oc=oc, fout=fout: e.tensor_mul(fout, sT[oc], AR[:, 8192:10240]),
                     waits=[("s", sg), ("ost", max(0, p.n("ost") - 16))], inc="v")
                p.op("v", lambda e, oc=oc, fout=fout: e.tensor_add(fout, fout, xT[oc][:]),
                     inc="v")
                p.dma("sy", outT[128 * oc: 128 * (oc + 1), :], fout,
                      waits=[("v", p.n("v"))], inc="ost")
                p.op("sy", lambda e: e.nop(), waits=[("ost", p.n("ost"))], inc="sy2")
            else:
                p.op("v", lambda e, oc=oc: e.tensor_mul(R3b[:, 4096:6144], sT[oc], AR[:, 8192:10240]),
                     waits=[("s", sg)], inc="v")
                p.op("v", lambda e, oc=oc: e.tensor_add(xT[oc][:], xT[oc][:], R3b[:, 4096:6144]),
                     inc="v")

    def phase_E1(sD, tD, dump_dacc=False):
        if dump_dacc:
            d_st3 = nc.dram_tensor("d_st3", [D, T], BF16, kind="ExternalOutput").ap()
            for c in range(CH):
                p.dma("sy", d_st3[128 * c: 128 * (c + 1), :], sT[c],
                      waits=[("s", sD), ("t", tD)], inc="dbgst")
            p.op("sy", lambda e: e.nop(), waits=[("dbgst", p.n("dbgst"))], inc="sy2")
        wd2 = [AR[:, 0:3072], AR[:, 3072:6144]]
        ws2 = [AR[:, 6144:7168], AR[:, 7168:8192]]
        gsil = AR[:, 8192:8704]
        for c in range(CH):
            p.dma("sy", oTs[c][:], hbm["xnb"][128 * c: 128 * (c + 1), :],
                  waits=[("t", tD)], inc="syq")
        ldx = p.n("syq")
        e1_v = p.n("v")
        ev = {}
        for grp in range(4):
            for ig in range(6):
                hc = 6 * grp + ig
                p.dma("g", ws[hc % 2][:].rearrange("p (a b) -> p a b", a=CH),
                      ins["l1_wgt"].rearrange("(a p) b -> p a b", p=128)[:, :, 128 * hc: 128 * (hc + 1)],
                      waits=[("t", ev.get(("mm", hc - 2), tD))], inc="gq")
                p.dma("g", ws2[hc % 2].rearrange("p (a b) -> p a b", a=CH),
                      ins["l1_wut"].rearrange("(a p) b -> p a b", p=128)[:, :, 128 * hc: 128 * (hc + 1)],
                      inc="gq")
                ld = p.n("gq")
                w0 = [("gq", ld), ("syq", ldx), ("s", sD),
                      ("v", max(e1_v, ev.get(("hv", hc - 1), 0))),
                      ("s", ev.get(("hs", hc - 1), 0))]
                for c in range(CH):
                    for qq in range(4):
                        sl = slice(512 * qq, 512 * (qq + 1))
                        p.op("t", lambda e, hc=hc, c=c, sl=sl: e.matmul(
                            PA[:, sl], ws[hc % 2][:, 128 * c: 128 * (c + 1)],
                            oTs[c][:, sl], start=(c == 0), stop=(c == CH - 1)),
                            waits=(w0 if (c == 0 and qq == 0) else ()), inc="t")
                    for qq in range(4):
                        sl = slice(512 * qq, 512 * (qq + 1))
                        p.op("t", lambda e, hc=hc, c=c, sl=sl: e.matmul(
                            PB[:, sl], ws2[hc % 2][:, 128 * c: 128 * (c + 1)],
                            oTs[c][:, sl], start=(c == 0), stop=(c == CH - 1)), inc="t")
                ev[("mm", hc)] = p.n("t")
                for qq in range(4):
                    sl = slice(512 * qq, 512 * (qq + 1))
                    p.op("s", lambda e, sl=sl: e.activation(
                        out=gsil, in_=PA[:, sl], func=AF.Silu, scale=1.0 / 127.0),
                        waits=[("t", ev[("mm", hc)]), ("v", p.n("v"))], inc="s")
                    p.op("v", lambda e, ig=ig, sl=sl: e.scalar_tensor_tensor(
                        out=hsl[ig][:, sl], in0=PB[:, sl], scalar=1.0 / 127.0,
                        in1=gsil, op0=ALU.mult, op1=ALU.mult),
                        waits=[("s", p.n("s"))], inc="v")
                ev[("hv", hc)] = p.n("v")
                ev[("hs", hc)] = p.n("s")
            hv_grp = p.n("v")
            prev_d = [ev[("mm", 6 * grp + 5)], ev[("mm", 6 * grp + 5)]]
            for dc in range(CH):
                p.dma("g", wd2[dc % 2].rearrange("p (a b) -> p a b", a=24),
                      ins["l1_wdt"].rearrange("(a p) b -> p a b", p=128)[:, :, 128 * dc: 128 * (dc + 1)],
                      waits=[("t", prev_d[dc % 2])], inc="gq")
                ldd = p.n("gq")
                PP = PA if dc % 2 == 0 else PB
                for ig in range(6):
                    hc = 6 * grp + ig
                    mm4(PP, wd2[dc % 2][:, 128 * hc: 128 * (hc + 1)],
                        hsl[ig][:, :], ig == 0, ig == 5,
                        waits=[("gq", ldd), ("v", hv_grp), ("v", ev.get(("dacc", dc), 0))])
                prev_d[dc % 2] = p.n("t")
                if grp == 0:
                    p.op("v", lambda e, dc=dc, PP=PP: e.tensor_copy(dacc[dc], PP[:, :]),
                         waits=[("t", prev_d[dc % 2])], inc="v")
                else:
                    p.op("v", lambda e, dc=dc, PP=PP: e.tensor_add(dacc[dc], dacc[dc], PP[:, :]),
                         waits=[("t", prev_d[dc % 2])], inc="v")
                ev[("dacc", dc)] = p.n("v")
            ev[("hv", 6 * grp + 5)] = p.n("v")
        vdacc = p.n("v")
        if dump_dacc:
            d_da = nc.dram_tensor("d_da", [D, T], BF16, kind="ExternalOutput").ap()
            d_st2 = nc.dram_tensor("d_st2", [D, T], BF16, kind="ExternalOutput").ap()
            allw = [("t", p.n("t")), ("s", p.n("s")), ("v", p.n("v")),
                    ("syq", p.n("syq")), ("gq", p.n("gq"))]
            d_ots2 = nc.dram_tensor("d_ots2", [D, T], BF16, kind="ExternalOutput").ap()
            for c in range(CH):
                p.dma("sy", d_da[128 * c: 128 * (c + 1), :], dacc[c], waits=allw, inc="dbgst")
                p.dma("sy", d_st2[128 * c: 128 * (c + 1), :], sT[c], waits=allw, inc="dbgst")
                p.dma("sy", d_ots2[128 * c: 128 * (c + 1), :], oTs[c][:], waits=allw, inc="dbgst")
            p.op("sy", lambda e: e.nop(), waits=[("dbgst", p.n("dbgst"))], inc="sy2")
        ost0 = p.n("ost")
        if dump_dacc:
            d_g1 = nc.dram_tensor("d_g1", [D, T], BF16, kind="ExternalOutput").ap()
        for dc in range(CH):
            fout = R3[:, 2048:4096] if dc % 2 == 0 else ROWS[:, 0:2048]
            p.op("s", lambda e, dc=dc: e.activation(
                out=st_sc[0][:, :], in_=dacc[dc], func=AF.Silu, scale=1.0 / 127.0),
                waits=[("v", vdacc), ("v", p.n("v"))], inc="s")
            sg = p.n("s")
            p.op("v", lambda e, dc=dc, fout=fout: e.tensor_mul(fout, sT[dc], st_sc[0][:, :]),
                 waits=[("s", sg), ("ost", max(ost0, p.n("ost") - 16))], inc="v")
            p.op("v", lambda e, dc=dc, fout=fout: e.tensor_add(fout, fout, xT[dc][:]), inc="v")
            if dump_dacc:
                p.dma("sy", d_g1[128 * dc: 128 * (dc + 1), :], st_sc[0][:, :],
                      waits=[("v", p.n("v"))], inc="ost")
                p.op("sy", lambda e: e.nop(), waits=[("ost", p.n("ost"))], inc="sy2")
            p.dma("sy", outT[128 * dc: 128 * (dc + 1), :], fout,
                  waits=[("v", p.n("v"))], inc="ost")

    done = False
    for li, L in enumerate(layers):
        first = (li == 0)
        p.counts.setdefault("sy2", 0)
        p.counts.setdefault("odma", 0)
        if os.environ.get("DUMPX1") == "1" and li == 1:
            d_x1 = nc.dram_tensor("d_x1", [D, T], BF16, kind="ExternalOutput").ap()
            bw = [("v", p.n("v")), ("s", p.n("s")), ("t", p.n("t"))]
            for c in range(CH):
                p.dma("sy", d_x1[128 * c: 128 * (c + 1), :], xT[c][:], waits=bw, inc="dbgst")
            p.op("sy", lambda e: e.nop(), waits=[("dbgst", p.n("dbgst"))], inc="sy2")
        xn_v = phase_A(L, first)
        if upto == "A":
            done = True
            break
        phase_B(L, xn_v, sub=(upto if upto and upto.startswith("B") else None))
        if dbg:
            kr = KR[L]
            d_xn = nc.dram_tensor("d_xn", [D, T], BF16, kind="ExternalOutput").ap()
            d_idx = nc.dram_tensor("d_idx", [128, 512], F32, kind="ExternalOutput").ap()
            d_qpp = nc.dram_tensor("d_qpp", [16 * kr, T], BF16, kind="ExternalOutput").ap()
            d_kpp = nc.dram_tensor("d_kpp", [16 * kr, T], BF16, kind="ExternalOutput").ap()
            d_vp = nc.dram_tensor("d_vp", [T, VW[L]], BF16, kind="ExternalOutput").ap()
            allw = [("t", p.n("t")), ("s", p.n("s")), ("v", p.n("v")),
                    ("syq", p.n("syq")), ("gq", p.n("gq"))]
            p.dma("sy", d_xn, hbm["xnb"][:, :], waits=allw, inc="ost")
            p.dma("sy", d_idx, IDX[:], waits=allw, inc="ost")
            p.dma("sy", d_qpp, hbm[f"qpp{L}"][:, :], waits=allw, inc="ost")
            p.dma("sy", d_kpp, hbm[f"kpp{L}"][:, :], waits=allw, inc="ost")
            p.dma("sy", d_vp, hbm[f"vp{L}"][:, :], waits=allw, inc="ost")
        if upto and upto.startswith("B"):
            done = True
            break
        phase_C(L, c0=(upto == "C0"), upto_c1=(upto == "C1"))
        if upto == "C1":
            done = True
            break
        if upto == "C0":
            done = True
            break
        if dbg and upto == "C":
            d_ots = nc.dram_tensor("d_ots", [D, T], BF16, kind="ExternalOutput").ap()
            allw = [("t", p.n("t")), ("s", p.n("s")), ("v", p.n("v")),
                    ("syq", p.n("syq")), ("gq", p.n("gq")), ("odma", p.n("odma"))]
            for c in range(CH):
                p.dma("sy", d_ots[128 * c: 128 * (c + 1), :], oTs[c][:], waits=allw, inc="ost")
            if L == 1:
                d_m8 = nc.dram_tensor("d_m8", [8, T], F32, kind="ExternalOutput").ap()
                p.dma("sy", d_m8, hbm["m8"][:, :], waits=allw, inc="ost")
        if upto == "C":
            done = True
            break
        sD, tD = phase_D(L)
        if dbg and upto == "D":
            d_st = nc.dram_tensor("d_st", [D, T], BF16, kind="ExternalOutput").ap()
            allw = [("t", p.n("t")), ("s", p.n("s")), ("v", p.n("v")),
                    ("syq", p.n("syq")), ("gq", p.n("gq")), ("odma", p.n("odma"))]
            for c in range(CH):
                p.dma("sy", d_st[128 * c: 128 * (c + 1), :], sT[c], waits=allw, inc="ost")
        if upto == "D":
            done = True
            break
        if L == 0:
            phase_E0(sD, tD, first_out=(L == layers[-1]))
        else:
            import os as _os2
            phase_E1(sD, tD, dump_dacc=((dbg and upto == "E") or _os2.environ.get("DUMPF") == "1"))
        if upto == "E":
            done = True
            break
    if done:
        for c in range(CH):
            p.dma("g", outT[128 * c: 128 * (c + 1), :], xT[c][:],
                  waits=[("t", p.n("t")), ("s", p.n("s")), ("v", p.n("v")),
                         ("syq", p.n("syq")), ("gq", p.n("gq")),
                         ("odma", p.n("odma"))] + (
                      [(f"cck{layers[0]}", 1), (f"ccv{layers[0]}", 1)]
                      if p.n(f"cck{layers[0]}") else []),
                  inc="ost")
    p.op("sy", lambda e: e.nop(), waits=[("ost", p.n("ost"))], inc="sy2")
    p.op("g", lambda e: e.nop(), waits=[("ost", p.n("ost"))], inc="g")

    sems = {name: nc.alloc_semaphore(name) for name in p.counts}
    with nc.Block() as block:
        block.tensor(lambda e: p.emit("t", e, sems))
        block.scalar(lambda e: p.emit("s", e, sems))
        block.vector(lambda e: p.emit("v", e, sems))
        block.sync(lambda e: p.emit("sy", e, sems))
        block.gpsimd(lambda e: p.emit("g", e, sems))
    return nc


def _prep_inputs(inputs):
    ac = np.ascontiguousarray
    f32 = np.float32

    def colmaj(v):
        return ac(np.asarray(v, f32).reshape(CH, 128).T)

    base0 = base1 = None
    base0 = {
        "l0_wq": ac(np.asarray(inputs["L0_attn_Wq"], f32)),
        "l0_wk": ac(np.asarray(inputs["L0_attn_Wk"], f32)),
        "l0_wv": ac(np.asarray(inputs["L0_attn_Wv"], f32)),
        "l0_wo": ac(np.asarray(inputs["L0_attn_Wo"], f32)),
        "l0_fw": ac(np.asarray(inputs["L0_ffn_weight"], f32)),
        "l0_fwt": ac(np.asarray(inputs["L0_ffn_weight"], f32).T),
        "l0_bq": colmaj(inputs["L0_attn_bq"]),
        "l0_bk": colmaj(inputs["L0_attn_bk"]),
        "l0_bv": colmaj(inputs["L0_attn_bv"]),
        "l0_bo": colmaj(inputs["L0_attn_bo"]),
        "l0_fb": colmaj(inputs["L0_ffn_bias"]),
        "l0_rt": ac(np.asarray(inputs["L0_attn_R"], f32).T),
    }
    base1 = {
        "l1_wq": ac(np.asarray(inputs["L1_attn_Wq"], f32)),
        "l1_wk": ac(np.asarray(inputs["L1_attn_Wk"], f32)),
        "l1_wv": ac(np.asarray(inputs["L1_attn_Wv"], f32)),
        "l1_wo": ac(np.asarray(inputs["L1_attn_Wo"], f32)),
        "l1_lq1": ac(np.asarray(inputs["L1_attn_lq1"], f32).reshape(1, 64)),
        "l1_lk1": ac(np.asarray(inputs["L1_attn_lk1"], f32).reshape(1, 64)),
        "l1_lq2": ac(np.asarray(inputs["L1_attn_lq2"], f32).reshape(1, 64)),
        "l1_lk2": ac(np.asarray(inputs["L1_attn_lk2"], f32).reshape(1, 64)),
        "l1_lng": ac(np.asarray(inputs["L1_attn_ln_g"], f32).reshape(128, 1)),
        "l1_lnb": ac(np.asarray(inputs["L1_attn_ln_b"], f32).reshape(128, 1)),
        "l1_wgt": ac(np.asarray(inputs["L1_ffn_w_gate"], np.int32).T),
        "l1_wut": ac(np.asarray(inputs["L1_ffn_w_up"], np.int32).T),
        "l1_wdt": ac(np.asarray(inputs["L1_ffn_w_down"], np.int32).T),
    }
    x = np.asarray(inputs["L0_attn_x"], f32)
    xsh = []
    for c in range(8):
        b, half = c // 2, c % 2
        xsh.append(ac(x[b, 2048 * half: 2048 * (half + 1), :].T))
    return base0, base1, xsh


def kernel(**inputs):
    base0, base1, xsh = _prep_inputs(inputs)
    if FUSED:
        nc = build_program([0, 1])
        in_maps = [dict(base0, **base1, xoT=xsh[c]) for c in range(8)]
        res = run_bass_kernel_spmd(nc, in_maps, core_ids=list(range(8)))
        outs = [res.results[c]["outT"] for c in range(8)]
    else:
        nc0 = build_program([0])
        in_maps = [dict(base0, xoT=xsh[c]) for c in range(8)]
        r0 = run_bass_kernel_spmd(nc0, in_maps, core_ids=list(range(8)))
        x1 = [np.ascontiguousarray(np.asarray(r0.results[c]["outT"], np.float32))
              for c in range(8)]
        nc1 = build_program([1])
        in_maps = [dict(base1, xoT=x1[c]) for c in range(8)]
        r1 = run_bass_kernel_spmd(nc1, in_maps, core_ids=list(range(8)))
        outs = [r1.results[c]["outT"] for c in range(8)]
    out = np.empty((4, 4096, 1024), np.float32)
    for c in range(8):
        b, half = c // 2, c % 2
        out[b, 2048 * half: 2048 * (half + 1), :] = np.asarray(outs[c], np.float32).T
    return out
